# version 13
# baseline (speedup 1.0000x reference)
import numpy as np
import ml_dtypes
from contextlib import ExitStack

import concourse.bass as bass
import concourse.mybir as mybir
from concourse.bass_utils import run_bass_kernel_spmd

F32 = mybir.dt.float32
BF16 = mybir.dt.bfloat16
I32 = mybir.dt.int32
AF = mybir.ActivationFunctionType
ALU = mybir.AluOpType
AX = mybir.AxisListType

D = 1024
NC = 8
DFF = 4096
NFF = 32
NTOK = 2048
NSEQ = 8192
HEADS = 16
HDIM = 64
LN_EPS = 1e-5
RMS_EPS = 1e-8
PHI_EPS = 1e-6
DATA_NORM = float(np.float32(64.0 ** -0.25))
NITER_BISECT = 50
ROUTER_K = 1024.0
KOFF = 16384


class Prog:
    SECTIONS = {
        "tensor": ("pe", 1),
        "vector": ("ve", 1),
        "scalar": ("sc", 1),
        "gpsimd": ("gp", 1),
    }

    def __init__(self, nc):
        self.nc = nc
        self.es = ExitStack()
        self.items = {s: [] for s in
                      ("tensor", "vector", "scalar", "gpsimd", "sync")}
        self.counts = {}
        self.n_cc = 0
        self.final_deps = []

    def _add(self, section, sem, inc, fn, deps):
        n = self.counts.get(sem, 0) + inc
        self.counts[sem] = n
        self.items[section].append((fn, [d for d in deps if d is not None],
                                    sem, inc))
        return (sem, n)

    def t(self, fn, deps=()):
        return self._add("tensor", "pe", 1, fn, deps)

    def v(self, fn, deps=()):
        return self._add("vector", "ve", 1, fn, deps)

    def s(self, fn, deps=()):
        return self._add("scalar", "sc", 1, fn, deps)

    def g(self, fn, deps=()):
        return self._add("gpsimd", "gp", 1, fn, deps)

    def sd(self, fn, deps=(), ch="ld"):
        return self._add("sync", f"syd_{ch}", 16, fn, deps)

    def ad(self, fn, deps=(), ch="ld"):
        return self._add("scalar", f"scd_{ch}", 16, fn, deps)

    def gd(self, fn, deps=(), ch="ld"):
        return self._add("gpsimd", f"gpd_{ch}", 16, fn, deps)

    def total(self, queue, ch):
        name = f"{queue}_{ch}"
        return (name, self.counts.get(name, 0))

    def col(self, fn, deps=()):
        self.n_cc += 1
        sem = f"cc{self.n_cc}"
        return self._add("gpsimd", sem, 1, fn, deps)

    def finish(self, deps):
        self.final_deps = list(deps)

    def build(self):
        nc = self.nc
        sems = {}
        for name in self.counts:
            sems[name] = self.es.enter_context(nc.semaphore(name))
        block = self.es.enter_context(nc.Block())
        hw = {}

        def coalesce(deps):
            best = {}
            for dsem, dval in deps:
                if best.get(dsem, 0) < dval:
                    best[dsem] = dval
            return best.items()

        def emit(section, eng):
            for fn, deps, sem, inc in self.items[section]:
                for dsem, dval in coalesce(deps):
                    key = (section, dsem)
                    if hw.get(key, 0) < dval:
                        hw[key] = dval
                        eng.wait_ge(sems[dsem], dval)
                ins = fn(eng)
                ins.then_inc(sems[sem], inc)
            if section == "sync":
                for dsem, dval in coalesce(self.final_deps):
                    key = (section, dsem)
                    if hw.get(key, 0) < dval:
                        hw[key] = dval
                        eng.wait_ge(sems[dsem], dval)

        if self.items["tensor"]:
            block.tensor(lambda t: emit("tensor", t))
        if self.items["vector"]:
            block.vector(lambda v: emit("vector", v))
        if self.items["scalar"]:
            block.scalar(lambda s: emit("scalar", s))
        if self.items["gpsimd"]:
            block.gpsimd(lambda g: emit("gpsimd", g))
        block.sync(lambda s: emit("sync", s))
        return self


def sb(nc, es, name, shape, dt=F32):
    return es.enter_context(nc.sbuf_tensor(name, shape, dt))


def ps(nc, es, name, shape, dt=F32):
    return es.enter_context(nc.psum_tensor(name, shape, dt))


def fmview(ap, n):
    return ap.rearrange("(a p) n -> p a n", p=128)


def emit_bisect(P, s_rwf, s_ones2, s_oner, p_cnt, p_bc, st,
                dep_rwf, dep_ones2, k=ROUTER_K, niter=NITER_BISECT):
    t_lo, t_hi, t_mid = st["t_lo"], st["t_hi"], st["t_mid"]
    fs, fsn, rowcnt = st["fs"], st["fsn"], st["rowcnt"]
    mask64 = st["mask64"]

    c1 = P.v(lambda v: v.memset(t_hi[:], 1e4), [dep_rwf])
    c2 = P.v(lambda v: v.memset(t_lo[:], -1e4), [c1])
    last = P.v(lambda v: v.memset(t_mid[:], 0.0), [c2])
    prev_f1 = None
    for i in range(niter):
        slot = i % 2
        mk = P.v(lambda v: v.tensor_scalar(
            mask64[:], s_rwf[:], t_mid[:], 0.0, op0=ALU.is_ge, op1=ALU.add,
            accum_out=rowcnt[:]), [last, prev_f1])
        cm = P.t(lambda t, _s=slot: t.matmul(
            p_cnt[:, _s:_s + 1], lhsT=s_ones2[:], rhs=rowcnt[:],
            start=True, stop=True), [mk, dep_ones2, prev_f1])
        f1 = P.v(lambda v, _s=slot: v.tensor_scalar(
            fs[:], p_cnt[:, _s:_s + 1], k, None, op0=ALU.is_ge), [cm])
        f2 = P.v(lambda v: v.tensor_scalar(fsn[:], fs[:], -1, 1,
                                           op0=ALU.mult, op1=ALU.add), [f1])
        s1 = P.v(lambda v: v.copy_predicated(t_lo[:], fs[:], t_mid[:]), [f2])
        s2 = P.v(lambda v: v.copy_predicated(t_hi[:], fsn[:], t_mid[:]), [s1])
        s3 = P.v(lambda v: v.tensor_tensor(t_mid[:], t_lo[:], t_hi[:],
                                           ALU.add), [s2])
        last = P.v(lambda v: v.tensor_scalar(t_mid[:], t_mid[:], 0.5, None,
                                             op0=ALU.mult), [s3])
        prev_f1 = f1
    return last


def emit_ln(P, n, i, w, s_y, s_out, s_m, s_r, s_tmp, s_trow, s_onec, s_oner,
            p_s1, p_s2, p_mbc, p_rbc, ytoks, war, s_eps=None,
            extra_out_dep=None, s_onecb=None):
    s1tok = s2tok = None
    sqprev = war.get("sq")
    tmpb = s_tmp[:].bitcast(BF16) if s_onecb is not None else None
    for j in range(NC):
        off = j * n + i * 512
        s1tok = P.t(lambda t, _o=off, _w=w, _j=j: t.matmul(
            p_s1[:, 0:_w], lhsT=s_onec[:], rhs=s_y[:, _o:_o + _w],
            start=(_j == 0), stop=(_j == NC - 1)),
            [ytoks[j], war.get("s1")])
        sq = P.s(lambda s, _o=off, _w=w: s.activation(
            tmpb[:, 0:_w], s_y[:, _o:_o + _w], AF.Square),
            [ytoks[j], sqprev])
        s2tok = P.t(lambda t, _w=w, _j=j: t.matmul(
            p_s2[:, 0:_w], lhsT=s_onecb[:], rhs=tmpb[:, 0:_w],
            start=(_j == 0), stop=(_j == NC - 1)),
            [sq, war.get("s2")])
        sqprev = s2tok
    war["sq"] = sqprev
    o = 0
    f1 = P.v(lambda v, _w=w, _o=o: v.tensor_scalar(
        s_m[:, _o:_o + _w], p_s1[:, 0:_w], 1.0 / D, None, op0=ALU.mult),
        [s1tok])
    f2 = P.v(lambda v, _w=w, _o=o: v.tensor_scalar(
        s_r[:, _o:_o + _w], p_s2[:, 0:_w], 1.0 / D, None, op0=ALU.mult),
        [s2tok])
    war["s1"] = f1
    war["s2"] = f2
    f3 = P.v(lambda v, _w=w, _o=o: v.tensor_tensor(
        s_trow[:, 0:_w], s_m[:, _o:_o + _w], s_m[:, _o:_o + _w],
        ALU.mult), [f1, f2, war.get("trow")])
    f4 = P.v(lambda v, _w=w, _o=o: v.tensor_tensor(
        s_r[:, _o:_o + _w], s_r[:, _o:_o + _w], s_trow[:, 0:_w],
        ALU.subtract), [f3])
    war["trow"] = f4
    f5 = P.s(lambda s, _w=w, _o=o: s.activation(
        s_r[:, _o:_o + _w], s_r[:, _o:_o + _w], AF.Ln,
        bias=s_eps[:]), [f4, war.get("eps_dep")])
    war["eps_dep"] = None
    f6 = P.s(lambda s, _w=w, _o=o: s.activation(
        s_r[:, _o:_o + _w], s_r[:, _o:_o + _w], AF.Exp, scale=-0.5), [f5])
    b1 = P.t(lambda t, _w=w, _o=o: t.matmul(
        p_mbc[:, 0:_w], lhsT=s_oner[:], rhs=s_m[:, _o:_o + _w],
        start=True, stop=True), [f1, war.get("bc")])
    b2 = P.t(lambda t, _w=w, _o=o: t.matmul(
        p_rbc[:, 0:_w], lhsT=s_oner[:], rhs=s_r[:, _o:_o + _w],
        start=True, stop=True), [f6])
    outtoks = []
    for j in range(NC):
        off = j * n + i * 512
        n1 = P.v(lambda v, _o=off, _w=w: v.tensor_tensor(
            s_out[:, _o:_o + _w], s_y[:, _o:_o + _w], p_mbc[:, 0:_w],
            ALU.subtract), [b1, b2, ytoks[j], extra_out_dep])
        n2 = P.v(lambda v, _o=off, _w=w: v.tensor_tensor(
            s_out[:, _o:_o + _w], s_out[:, _o:_o + _w], p_rbc[:, 0:_w],
            ALU.mult), [n1])
        outtoks.append(n2)
    war["bc"] = outtoks[-1]
    return outtoks


def build_v2(n=NTOK, dump=False):
    nc = bass.Bass("TRN2", target_bir_lowering=False, debug=False)
    nsub = n // 512
    ntile = n // 128
    H = 1024

    XT = nc.dram_tensor("xt", [D, n], F32, kind="ExternalInput").ap()
    XRT = nc.dram_tensor("xrevt", [D, n], F32, kind="ExternalInput").ap()
    RWC = nc.dram_tensor("rwc", [128, NC], F32, kind="ExternalInput").ap()
    RB = nc.dram_tensor("rb", [1, 1], F32, kind="ExternalInput").ap()
    IDB = nc.dram_tensor("identb", [128, 128], BF16, kind="ExternalInput").ap()
    WUPS = nc.dram_tensor("wups", [4, 128, 1024], F32,
                          kind="ExternalInput").ap()
    WDNS = nc.dram_tensor("wdns", [128, 4096], F32,
                          kind="ExternalInput").ap()
    WQKVC = nc.dram_tensor("wqkvc", [6, 128, NC, 512], F32,
                           kind="ExternalInput").ap()
    BQ = nc.dram_tensor("bq", [128, 8], F32, kind="ExternalInput").ap()
    BK = nc.dram_tensor("bk", [128, 8], F32, kind="ExternalInput").ap()
    BV = nc.dram_tensor("bvrow", [1, D], F32, kind="ExternalInput").ap()
    PJ = nc.dram_tensor("projt", [64, HEADS * 256], F32,
                        kind="ExternalInput").ap()
    WOC = nc.dram_tensor("woc", [2, 128, NC, 512], F32,
                         kind="ExternalInput").ap()
    BO = nc.dram_tensor("bo", [128, NC], F32, kind="ExternalInput").ap()
    WINC = nc.dram_tensor("winc", [128, NC, 64], F32,
                          kind="ExternalInput").ap()
    BIN = nc.dram_tensor("bin", [64, 1], F32, kind="ExternalInput").ap()
    W1 = nc.dram_tensor("w1", [64, 16], F32, kind="ExternalInput").ap()
    B1 = nc.dram_tensor("b1", [64, 16], F32, kind="ExternalInput").ap()
    W2 = nc.dram_tensor("w2", [64, 16], F32, kind="ExternalInput").ap()
    B2_ = nc.dram_tensor("b2", [64, 1], F32, kind="ExternalInput").ap()
    WOUT = nc.dram_tensor("wout", [64, D], F32, kind="ExternalInput").ap()
    BOUT = nc.dram_tensor("bout", [128, NC], F32, kind="ExternalInput").ap()
    YT = nc.dram_tensor("yt", [D, n], F32, kind="ExternalOutput").ap()

    cc1_in = nc.dram_tensor("cc1_in", [1, n], F32, kind="Internal").ap()
    cc1_out = nc.dram_tensor("cc1_out", [4, n], F32, kind="Internal").ap()
    ST_in = nc.dram_tensor("st_in", [1, 2], F32, kind="Internal").ap()
    ST_out = nc.dram_tensor("st_out", [8, 2], F32, kind="Internal").ap()
    WUP_in = nc.dram_tensor("wup_in", [1, 4 * 128 * 1024], BF16,
                            kind="Internal").ap()
    WUP_out = nc.dram_tensor("wup_out", [8, 4 * 128 * 1024], BF16,
                             kind="Internal").ap()
    WDN_in = nc.dram_tensor("wdn_in", [1, 4 * 128 * 1024], BF16,
                            kind="Internal").ap()
    WDN_out = nc.dram_tensor("wdn_out", [8, 4 * 128 * 1024], BF16,
                             kind="Internal").ap()
    kv_in = nc.dram_tensor("kv_in", [128, 2080], F32, kind="Internal").ap()
    kv_out = nc.dram_tensor("kv_out", [128, 2080], F32, kind="Internal").ap()
    if dump:
        XD1 = nc.dram_tensor("xd1", [128, NC * n], F32,
                             kind="ExternalOutput").ap()
        XDH = nc.dram_tensor("xdh", [128, NC * n], BF16,
                             kind="ExternalOutput").ap()
        XD2 = nc.dram_tensor("xd2", [128, NC * n], F32,
                             kind="ExternalOutput").ap()
        XDW = nc.dram_tensor("xdw", [8, 4 * 128 * 1024], BF16,
                             kind="ExternalOutput").ap()
        XDW2 = nc.dram_tensor("xdw2", [8, 4 * 128 * 1024], BF16,
                              kind="ExternalOutput").ap()
        XD3 = nc.dram_tensor("xd3", [128, NC * n], F32,
                             kind="ExternalOutput").ap()
        XDKV = nc.dram_tensor("xdkv", [128, 2080], F32,
                              kind="ExternalOutput").ap()
        XDHH = nc.dram_tensor("xdhh", [128, NFF * 1024], BF16,
                              kind="ExternalOutput").ap()
        XDSB = nc.dram_tensor("xdsb", [128, 1024], F32,
                              kind="ExternalOutput").ap()
        XDAW = nc.dram_tensor("xdaw", [128, 1024], BF16,
                              kind="ExternalOutput").ap()
        XDH3 = nc.dram_tensor("xdh3", [128, NC * n], BF16,
                              kind="ExternalOutput").ap()

    P = Prog(nc)
    es = P.es
    SA = sb(nc, es, "SA", [128, NC * n])
    SB = sb(nc, es, "SB", [128, NC * n])
    SBb = SB[:].bitcast(BF16)
    SC = sb(nc, es, "SC", [128, 8192])
    SCb = SC[:].bitcast(BF16)
    SD = sb(nc, es, "SD", [128, 2080])
    SE = sb(nc, es, "SE", [128, 2048])
    SEb = SE[:].bitcast(BF16)
    AW = sb(nc, es, "AW", [128, 2048])
    AWb = AW[:].bitcast(BF16)
    PJT = sb(nc, es, "PJT", [128, HEADS * 256], BF16)
    s_pj = PJT
    s_tmp = sb(nc, es, "s_tmp", [128, 512])
    s_tmpb = s_tmp[:].bitcast(BF16)
    s_p56 = sb(nc, es, "s_p56", [128, 1024])
    s_rwc = sb(nc, es, "s_rwc", [128, NC])
    s_rb = sb(nc, es, "s_rb", [1, 1])
    s_rwf = sb(nc, es, "s_rwf", [128, 4 * n // 128])
    s_idb = sb(nc, es, "s_idb", [128, 128], BF16)
    s_onec = sb(nc, es, "s_onec", [128, 1])
    s_onecb = sb(nc, es, "s_onecb", [128, 1], BF16)
    s_ones2 = sb(nc, es, "s_ones2", [128, 128])
    s_oner = s_ones2[0:1, :]
    s_onerb = sb(nc, es, "s_onerb", [1, 128], BF16)
    s_eps = sb(nc, es, "s_eps", [1, 1])
    s_eps8 = sb(nc, es, "s_eps8", [1, 1])
    s_epsphi = sb(nc, es, "s_epsphi", [1, 1])
    s_m = sb(nc, es, "s_m", [1, 512])
    s_r = sb(nc, es, "s_r", [1, 512])
    s_trow = sb(nc, es, "s_trow", [1, 512])
    s_vv = SD[0:1, 0:512]
    s_c2 = SD[0:1, 512:1024]
    s_hr2 = SD[0:1, 1024:2048]
    s_mrow = s_m[0:1, 0:512]
    s_acc = sb(nc, es, "s_acc", [128, 8])
    s_red = sb(nc, es, "s_red", [128, 2])
    s_th = sb(nc, es, "s_th", [1, 2])
    s_thc = sb(nc, es, "s_thc", [128, 2])
    s_aps = sb(nc, es, "s_aps", [1, 16])
    st = {kk: sb(nc, es, kk, [128, 1]) for kk in
          ("t_lo", "t_hi", "t_mid", "rowcnt")}
    st["fs"] = sb(nc, es, "fs", [128, 1], I32)
    st["fsn"] = sb(nc, es, "fsn", [128, 1], I32)
    st["mask64"] = sb(nc, es, "mask64", [128, 4 * n // 128], BF16)
    s_bq = sb(nc, es, "s_bq", [128, 8])
    s_bk = sb(nc, es, "s_bk", [128, 8])
    s_bvb = SD[0:1, 0:512].bitcast(BF16)
    s_denb = s_trow[:].bitcast(BF16)[0:1, 0:512]
    s_phik2 = [sb(nc, es, f"s_phik{i}", [128, 256], BF16) for i in range(2)]
    s_nmx4 = sb(nc, es, "s_nmx4", [128, 4])
    s_nmx2 = [sb(nc, es, f"s_nmx{i}", [128, 1]) for i in range(2)]
    s_phiqT = [SEb[:, 0:512], SEb[:, 512:1024]]
    s_kvs = sb(nc, es, "s_kvs", [128, 132], BF16)
    s_bo = sb(nc, es, "s_bo", [128, NC])
    s_bin = sb(nc, es, "s_bin", [64, 1])
    s_w1 = sb(nc, es, "s_w1", [64, 16])
    s_b1 = sb(nc, es, "s_b1", [64, 16])
    s_w2 = sb(nc, es, "s_w2", [64, 16])
    s_b2 = sb(nc, es, "s_b2", [64, 1])
    s_bout = sb(nc, es, "s_bout", [128, NC])
    s_bv = s_p56[0:1, 0:D]
    s_exp2 = [s_p56[:, 512:768], s_p56[:, 768:1024]]
    s_den = s_p56[0:1, 0:512]
    s_dbc = s_p56[:, 0:512]
    s_sbc = SE[:, 0:1024]
    s_s = SC[0:64, 0:512]
    s_hi = SD[0:64, 0:512]
    s_g = SC[0:64, 512:1024]
    s_winb = SCb[:, 8192:8704]
    s_woutb = SCb[0:64, 8704:9728]
    s_gb = SCb[0:64, 9728:10240]
    P01 = ps(nc, es, "P01", [128, 1024])
    P23 = ps(nc, es, "P23", [128, 1024])
    B = [P01[:, 0:512], P01[:, 512:1024], P23[:, 0:512], P23[:, 512:1024]]
    B += [ps(nc, es, f"B{i}", [128, 512]) for i in range(4, 7)]
    B7 = ps(nc, es, "B7", [128, 1024], BF16)
    PP = [P01, P23]

    war = {}

    P.sd(lambda s: s.dma_start(
        out=SA[:].rearrange("p (a m) -> p a m", a=NC), in_=fmview(XT, n)),
        ch="x")
    P.sd(lambda s: s.dma_start(
        out=SB[:].rearrange("p (a m) -> p a m", a=NC), in_=fmview(XRT, n)),
        ch="x")
    lxall = P.total("syd", "x")
    for ap_, tile in ((RWC, s_rwc), (RB, s_rb), (IDB, s_idb), (BQ, s_bq),
                      (BK, s_bk), (BO, s_bo), (BIN, s_bin), (W1, s_w1),
                      (B1, s_b1), (W2, s_w2), (B2_, s_b2), (BOUT, s_bout)):
        P.sd(lambda s, _a=ap_, _t=tile: s.dma_start(out=_t[:], in_=_a))
    lall = P.total("syd", "ld")
    m2 = P.v(lambda v: v.memset(s_onec[:], 1.0))
    m2b = P.v(lambda v: v.memset(s_onecb[:], 1.0))
    m3 = P.v(lambda v: v.memset(s_ones2[:], 1.0))
    m1 = m3
    m3b = P.v(lambda v: v.memset(s_onerb[:], 1.0))
    m4 = P.v(lambda v: v.memset(s_eps[:], LN_EPS))
    m4b = P.v(lambda v: v.memset(s_eps8[:], RMS_EPS))
    m4c = P.v(lambda v: v.memset(s_epsphi[:], PHI_EPS))

    s_rwo = SD[0:1, 0:n]
    rwevs = []
    for i in range(nsub):
        pb = B[i % 2]
        mmtok = None
        for j in range(NC):
            mmtok = P.t(lambda t, _i=i, _j=j, _pb=pb: t.matmul(
                _pb[0:1, 0:512], lhsT=s_rwc[:, _j:_j + 1],
                rhs=SA[:, _j * n + _i * 512:_j * n + _i * 512 + 512],
                start=(_j == 0), stop=(_j == NC - 1)),
                [lxall, lall, war.get(("rwp", i % 2))])
        ev = P.v(lambda v, _i=i, _pb=pb: v.tensor_scalar(
            s_rwo[:, _i * 512:_i * 512 + 512], _pb[0:1, 0:512], s_rb[:],
            None, op0=ALU.add), [mmtok, lall])
        war[("rwp", i % 2)] = ev
        rwevs.append(ev)
    P.gd(lambda g: g.dma_start(out=cc1_in, in_=s_rwo[:]), rwevs, ch="cc")
    cc1 = P.col(lambda g: g.collective_compute(
        "AllGather", ALU.bypass,
        replica_groups=[[0, 1, 2, 3], [4, 5, 6, 7]],
        ins=[cc1_in.opt()], outs=[cc1_out.opt()]),
        [P.total("gpd", "cc")])
    P.gd(lambda g: g.dma_start(
        out=s_rwf[:], in_=cc1_out.rearrange("r (p f) -> (r p) f", p=32)),
        [cc1], ch="g1")
    lrwf = P.total("gpd", "g1")

    red_toks = []
    for pi in range(8):
        reg = pi % 2
        ld = P.ad(lambda s, _pi=pi, _reg=reg: s.dma_start(
            out=AW[:, _reg * 1024:(_reg + 1) * 1024],
            in_=(WUPS[_pi:_pi + 1].rearrange("o p m -> (o p) m")
                 if _pi < 4 else
                 WDNS[:, (_pi - 4) * 1024:(_pi - 3) * 1024])),
            [red_toks[pi - 2] if pi >= 2 else None], ch=f"ws{reg}")
        rt = P.v(lambda v, _pi=pi, _reg=reg: v.tensor_reduce(
            s_acc[:, _pi:_pi + 1], AW[:, _reg * 1024:(_reg + 1) * 1024],
            axis=AX.X, op=ALU.add, apply_absolute_value=True),
            [P.total("scd", f"ws{reg}")])
        red_toks.append(rt)
    r1 = P.v(lambda v: v.tensor_reduce(
        s_red[:, 0:1], s_acc[:, 0:4], axis=AX.X, op=ALU.add), red_toks)
    r2 = P.v(lambda v: v.tensor_reduce(
        s_red[:, 1:2], s_acc[:, 4:8], axis=AX.X, op=ALU.add),
        red_toks + [r1])
    apm = P.t(lambda t: t.matmul(B[2][0:1, 0:2], lhsT=s_onec[:],
                                 rhs=s_red[:, 0:2], start=True, stop=True),
              [r2, m2])
    stv = P.v(lambda v: v.tensor_copy(s_th[:, 0:2], B[2][0:1, 0:2]), [apm])
    P.gd(lambda g: g.dma_start(out=ST_in, in_=s_th[:, 0:2]), [stv], ch="cs")
    ccS = P.col(lambda g: g.collective_compute(
        "AllGather", ALU.bypass,
        replica_groups=[[0, 1, 2, 3, 4, 5, 6, 7]],
        ins=[ST_in.opt()], outs=[ST_out.opt()]),
        [P.total("gpd", "cs")])
    P.gd(lambda g: g.dma_start(
        out=s_aps[:], in_=ST_out.rearrange("r c -> () (r c)")), [ccS],
        ch="g2")
    laps = P.total("gpd", "g2")
    a1 = P.v(lambda v: v.tensor_reduce(
        s_th[:, 0:1], s_aps[0:1, 0:16:2], axis=AX.X, op=ALU.add),
        [laps, stv])
    a2 = P.v(lambda v: v.tensor_reduce(
        s_th[:, 1:2], s_aps[0:1, 1:16:2], axis=AX.X, op=ALU.add),
        [laps, a1])
    a3 = P.v(lambda v: v.tensor_scalar(
        s_th[:, 0:2], s_th[:, 0:2], 0.5 / (D * DFF), None, op0=ALU.mult),
        [a1, a2])
    bb = P.t(lambda t: t.matmul(B[2][:, 2:4], lhsT=s_oner[:],
                                rhs=s_th[:, 0:2], start=True, stop=True),
             [a3, m1, stv])
    thc = P.v(lambda v: v.tensor_copy(s_thc[:], B[2][:, 2:4]), [bb])

    bis_last = emit_bisect(P, s_rwf, s_ones2, s_oner, B[3], B[4], st,
                           lrwf, m1, niter=NITER_BISECT)

    scr = {0: SC[:, 0:1024], 1: SC[:, 1024:2048]}
    prev_sg = {}
    prev_ld = {}
    for pi in range(8):
        reg = pi % 2
        slot = pi % 2
        which = 0 if pi < 4 else 1
        ld = P.ad(lambda s, _pi=pi, _reg=reg: s.dma_start(
            out=AW[:, _reg * 1024:(_reg + 1) * 1024],
            in_=(WUPS[_pi:_pi + 1].rearrange("o p m -> (o p) m")
                 if _pi < 4 else
                 WDNS[:, (_pi - 4) * 1024:(_pi - 3) * 1024])),
            [red_toks[6 + reg], prev_ld.get(reg)], ch=f"wt{reg}")
        ldt = P.total("scd", f"wt{reg}")
        ab = P.s(lambda s, _reg=reg: s.activation(
            scr[_reg], AW[:, _reg * 1024:(_reg + 1) * 1024], AF.Abs),
            [ldt, prev_sg.get(("mk", reg))])
        mk = P.v(lambda v, _reg=reg, _w=which: v.tensor_scalar(
            scr[_reg], scr[_reg],
            s_thc[:, _w:_w + 1], None, op0=ALU.is_gt), [thc, ab])
        wm = P.v(lambda v, _reg=reg: v.tensor_tensor(
            scr[_reg], scr[_reg],
            AW[:, _reg * 1024:(_reg + 1) * 1024], ALU.mult), [mk])
        prev_ld[reg] = wm
        sg = P.s(lambda s, _slot=slot, _reg=reg: s.activation(
            PJT[:, _slot * 1024:(_slot + 1) * 1024],
            scr[_reg], AF.Sign),
            [wm, prev_sg.get(("st", slot))])
        prev_sg[("mk", reg)] = sg
        stok = P.ad(lambda s, _pi=pi, _slot=slot: s.dma_start(
            out=((WUP_in[0:1, _pi * 131072:(_pi + 1) * 131072]
                  if _pi < 4 else
                  WDN_in[0:1, (_pi - 4) * 131072:(_pi - 3) * 131072])
                 .rearrange("o (p m) -> (o p) m", p=128)),
            in_=PJT[:, _slot * 1024:(_slot + 1) * 1024]),
            [sg], ch=f"ts{slot}")
        prev_sg[("st", slot)] = stok
        if pi == 3:
            ccWu = P.col(lambda g: g.collective_compute(
                "AllGather", ALU.bypass,
                replica_groups=[[0, 1, 2, 3, 4, 5, 6, 7]],
                ins=[WUP_in.opt()], outs=[WUP_out.opt()]),
                [P.total("scd", "ts0"), P.total("scd", "ts1")])
    ccWd = P.col(lambda g: g.collective_compute(
        "AllGather", ALU.bypass,
        replica_groups=[[0, 1, 2, 3, 4, 5, 6, 7]],
        ins=[WDN_in.opt()], outs=[WDN_out.opt()]),
        [P.total("scd", "ts0"), P.total("scd", "ts1")])

    tern_done = prev_sg[("st", 0)], prev_sg[("st", 1)]
    pjevs = []
    pjld = None
    for half in range(2):
        l1 = P.sd(lambda s, _h=half: s.dma_start(
            out=SE[0:64, 0:2048], in_=PJ[:, _h * 2048:(_h + 1) * 2048]),
            [pjld], ch="pj")
        l2 = P.sd(lambda s, _h=half: s.dma_start(
            out=SE[64:128, 0:2048], in_=PJ[:, _h * 2048:(_h + 1) * 2048]),
            ch="pj")
        lt = P.total("syd", "pj")
        ev = P.v(lambda v, _h=half: v.tensor_scalar(
            PJT[:, _h * 2048:(_h + 1) * 2048], SE[:, 0:2048],
            DATA_NORM, None, op0=ALU.mult),
            [lt, tern_done[0], tern_done[1]])
        pjevs.append(ev)
        pjld = ev
    pj = pjevs[-1]
    P.sd(lambda s: s.dma_start(out=s_bv, in_=BV), ch="bv")
    lbv = P.total("syd", "bv")

    x1toks = {}
    xbtoks = {}
    prev_mb = None
    prev_b0 = P.total("gpd", "cc")
    ln1 = {"sq": prev_sg[("mk", 0)], "s1": None, "s2": None}
    rowrd = {}
    ytoks_all = {}
    for i in range(nsub):
        mr = P.v(lambda v, _i=i: v.tensor_scalar(
            s_mrow[:], s_rwo[:, _i * 512:_i * 512 + 512],
            st["t_lo"][0:1, 0:1], None, op0=ALU.is_ge),
            [bis_last, rwevs[i], prev_mb])
        mb = P.t(lambda t: t.matmul(
            B[5][:, 0:512], lhsT=s_oner[:], rhs=s_mrow[:],
            start=True, stop=True), [mr, m1, war.get("B5rd")])
        prev_mb = mb
        ytoks = []
        for j in range(NC):
            off = j * n + i * 512
            ym = P.v(lambda v, _o=off: v.tensor_tensor(
                SB[:, _o:_o + 512], SB[:, _o:_o + 512], B[5][:, 0:512],
                ALU.mult), [mb, lxall])
            ya = P.v(lambda v, _o=off: v.tensor_tensor(
                SB[:, _o:_o + 512], SB[:, _o:_o + 512], SA[:, _o:_o + 512],
                ALU.add), [ym])
            ytoks.append(ya)
        war["B5rd"] = ytoks[-1]
        ytoks_all[i] = ytoks
    for i in range(nsub):
        ytoks = ytoks_all[i]
        s1tok = None
        s2tok = None
        sqprev = ln1.get("sq")
        for j in range(NC):
            off = j * n + i * 512
            s1tok = P.t(lambda t, _o=off, _j=j: t.matmul(
                B[6][0:1, 0:512], lhsT=s_onec[:], rhs=SB[:, _o:_o + 512],
                start=(_j == 0), stop=(_j == NC - 1)),
                [ytoks[j], m2, ln1.get("s1")])
            sq = P.s(lambda s, _o=off, _j=j: s.activation(
                s_tmpb[:, (_j % 2) * 512:(_j % 2) * 512 + 512],
                SB[:, _o:_o + 512], AF.Square), [ytoks[j], sqprev])
            s2tok = P.t(lambda t, _j=j: t.matmul(
                B[4][0:1, 0:512], lhsT=s_onecb[:],
                rhs=s_tmpb[:, (_j % 2) * 512:(_j % 2) * 512 + 512],
                start=(_j == 0), stop=(_j == NC - 1)),
                [sq, m2b, ln1.get("s2")])
            sqprev = s2tok
        ln1["sq"] = sqprev
        f1 = P.v(lambda v: v.tensor_scalar(
            s_m[:], B[6][0:1, 0:512], 1.0 / D, None, op0=ALU.mult),
            [s1tok, rowrd.get("m")])
        f2 = P.v(lambda v: v.tensor_scalar(
            s_r[:], B[4][0:1, 0:512], 1.0 / D, None, op0=ALU.mult),
            [s2tok, rowrd.get("r")])
        ln1["s1"] = f1
        ln1["s2"] = f2
        f3 = P.v(lambda v: v.tensor_tensor(
            s_trow[:], s_m[:], s_m[:], ALU.mult), [f1, rowrd.get("trow")])
        f4 = P.v(lambda v: v.tensor_tensor(
            s_r[:], s_r[:], s_trow[:], ALU.subtract), [f3, f2])
        rowrd["trow"] = f4
        f4b = P.s(lambda s: s.activation(s_vv[:], s_r[:], AF.Sqrt),
                  [f4, rowrd.get("vv")])
        f5 = P.s(lambda s: s.activation(
            s_r[:], s_r[:], AF.Ln, bias=s_eps[:]),
            [f4b, (m4 if i == 0 else None)])
        f6 = P.s(lambda s: s.activation(
            s_r[:], s_r[:], AF.Exp, scale=-0.5), [f5])
        g1 = P.v(lambda v: v.tensor_tensor(
            s_vv[:], s_vv[:], s_r[:], ALU.mult), [f6, f4b])
        g2 = P.s(lambda s: s.activation(
            s_vv[:], s_vv[:], AF.Identity, bias=s_eps8[:]),
            [g1, (m4b if i == 0 else None)])
        g3 = P.v(lambda v: v.reciprocal(s_vv[:], s_vv[:]), [g2])
        g4 = P.v(lambda v: v.tensor_tensor(
            s_c2[:], s_vv[:], s_r[:], ALU.mult), [g3, rowrd.get("c2")])
        b_m = P.t(lambda t: t.matmul(
            B[0][:, 0:512], lhsT=s_oner[:], rhs=s_m[:],
            start=True, stop=True), [f1, m1, prev_b0, rowrd.get("B0")])
        b_r = P.t(lambda t: t.matmul(
            B[1][:, 0:512], lhsT=s_oner[:], rhs=s_r[:],
            start=True, stop=True), [f6, prev_b0, rowrd.get("B1")])
        b_c = P.t(lambda t: t.matmul(
            B[2][:, 0:512], lhsT=s_oner[:], rhs=s_c2[:],
            start=True, stop=True), [g4, thc, rowrd.get("B2")])
        rowrd["m"] = b_m
        rowrd["r"] = b_r
        rowrd["c2"] = b_c
        rowrd["vv"] = g4
        t2l = t3l = None
        for j in range(NC):
            off = j * n + i * 512
            t1 = P.v(lambda v, _o=off: v.tensor_tensor(
                SB[:, _o:_o + 512], SB[:, _o:_o + 512], B[0][:, 0:512],
                ALU.subtract), [b_m, ytoks[j]])
            t2 = P.v(lambda v, _o=off: v.tensor_tensor(
                SA[:, _o:_o + 512], SB[:, _o:_o + 512], B[1][:, 0:512],
                ALU.mult), [t1, b_r])
            t3 = P.v(lambda v, _o=off, _j=j, _i=i: v.tensor_tensor(
                SCb[:, _j * 2048 + _i * 512:_j * 2048 + _i * 512 + 512],
                SB[:, _o:_o + 512], B[2][:, 0:512], ALU.mult),
                [t1, b_c,
                 (prev_sg[("mk", j)] if i == 0 and j < 2 else None)])
            x1toks[(i, j)] = t2
            xbtoks[(i, j)] = t3
            t2l, t3l = t2, t3
        rowrd["B0"] = t3l
        rowrd["B1"] = t2l
        rowrd["B2"] = t3l
    p3done = t3l
    bvc = P.v(lambda v: v.tensor_copy(s_bvb[:], s_bv), [lbv, p3done])
    snap1 = None
    if dump:
        P.sd(lambda s: s.dma_start(out=XD1, in_=SA[:]), [p3done], ch="dbg")
        snap1 = P.sd(lambda s: s.dma_start(
            out=XDH, in_=SCb[:, 0:NC * n]), [p3done], ch="dbg")
        P.sd(lambda s: s.dma_start(out=XDW, in_=WUP_out), [ccWu], ch="dbg")
        snap1 = P.sd(lambda s: s.dma_start(out=XDW2, in_=WDN_out),
                     [ccWd, snap1], ch="dbg")
        snap1 = P.total("syd", "dbg")

    awfree = {0: None, 1: None, 2: None, 3: None}
    fw = {"B01": rowrd["B1"], "B23": rowrd["B2"],
          "B4": ln1["s2"], "B5": war["B5rd"], "B6": ln1["s1"],
          "sq": ln1["sq"], "tmprd": (snap1 or ln1["sq"])}
    awfree[0] = awfree[1] = prev_ld[0]
    awfree[2] = awfree[3] = prev_ld[1]
    hwar = p3done
    ln2war = {"eps_dep": None, "sq": ln1["sq"], "trow": rowrd["trow"],
              "s1": None, "s2": None, "bc": None}
    x2toks = {}
    dnfree = {}
    sbcrd = {}
    for hh in range(2):
        tok0 = hh * H
        geltoks = {}
        stat_emit = []
        statA_last = statB_last = None
        upA_war = {0: fw["B01"], 1: fw["B23"]}
        for f in range(32):
            u = f % 4
            c, k = f // 4, f % 4
            ld = P.sd(lambda s, _u=u, _c=c, _k=k: s.dma_start(
                out=AWb[:, _u * 1024:(_u + 1) * 1024],
                in_=WUP_out[_c:_c + 1,
                            _k * 131072:(_k + 1) * 131072].rearrange(
                    "o (p m) -> (o p) m", p=128)),
                [ccWu if (hh == 0 and f < 4) else None, awfree[u],
                 (ln2war["bc"] if hh == 1 and f < 4 else None),
                 (ln2war["sq"] if hh == 1 and f < 4 else None)],
                ch=f"wu{u}")
            ldt = P.total("syd", f"wu{u}")
            pA = B[0 + 2 * (f % 2)]
            pB = B[1 + 2 * (f % 2)]
            mmA = mmB = None
            for j in range(NC):
                mmA = P.t(lambda t, _u=u, _j=j, _pA=pA, _t0=tok0: t.matmul(
                    _pA[:, 0:512],
                    lhsT=AWb[:, _u * 1024 + _j * 128:_u * 1024 + (_j + 1) * 128],
                    rhs=SCb[:, _j * 2048 + _t0:_j * 2048 + _t0 + 512],
                    start=(_j == 0), stop=(_j == NC - 1)),
                    [ldt, xbtoks[(2 * hh, j)], upA_war[f % 2]])
                mmB = P.t(lambda t, _u=u, _j=j, _pB=pB, _t0=tok0: t.matmul(
                    _pB[:, 0:512],
                    lhsT=AWb[:, _u * 1024 + _j * 128:_u * 1024 + (_j + 1) * 128],
                    rhs=SCb[:, _j * 2048 + _t0 + 512:_j * 2048 + _t0 + 1024],
                    start=(_j == 0), stop=(_j == NC - 1)),
                    [xbtoks[(2 * hh + 1, j)], mmA])
            awfree[u] = mmB
            if dump and hh == 1 and f == 0:
                P.sd(lambda s: s.dma_start(
                    out=XDAW, in_=AWb[:, 0:1024]), [mmB], ch="dbg")
                awfree[u] = P.sd(lambda s: s.dma_start(
                    out=XDH3, in_=SCb[:, 0:NC * n]), [mmB], ch="dbg")
            if stat_emit:
                statA_last, statB_last = stat_emit.pop()()
            gA = P.s(lambda s, _f=f, _pA=pA: s.activation(
                SBb[:, _f * 1024:_f * 1024 + 512], _pA[:, 0:512], AF.Gelu),
                [mmA, hwar])
            gB = P.s(lambda s, _f=f, _pB=pB: s.activation(
                SBb[:, _f * 1024 + 512:(_f + 1) * 1024], _pB[:, 0:512],
                AF.Gelu), [mmB])
            geltoks[f] = (gA, gB)
            upA_war[f % 2] = gB
            sqA = P.s(lambda s, _f=f: s.activation(
                s_tmpb[:, 0:512],
                SBb[:, _f * 1024:_f * 1024 + 512], AF.Square),
                [gA, fw["sq"], statA_last])
            sqB = P.s(lambda s, _f=f: s.activation(
                s_tmpb[:, 512:1024],
                SBb[:, _f * 1024 + 512:(_f + 1) * 1024], AF.Square),
                [sqA, statB_last])
            fw["sq"] = sqB

            def emit_stats(_f=f, _sqA=sqA, _sqB=sqB,
                           _sAp=statA_last, _sBp=statB_last,
                           _b4=fw["B4"], _b5=fw["B5"]):
                sA = P.t(lambda t: t.matmul(
                    B[4][0:1, 0:512], lhsT=s_onecb[:],
                    rhs=s_tmpb[:, 0:512],
                    start=(_f == 0), stop=(_f == 31)),
                    [_sqA, m2b, _sAp, (_b4 if _f == 0 else None)])
                sB = P.t(lambda t: t.matmul(
                    B[5][0:1, 0:512], lhsT=s_onecb[:],
                    rhs=s_tmpb[:, 512:1024],
                    start=(_f == 0), stop=(_f == 31)),
                    [_sqB, _sBp, (_b5 if _f == 0 else None)])
                return sA, sB
            stat_emit.append(emit_stats)
        statA_last, statB_last = stat_emit.pop()()
        fw["tmprd"] = statB_last
        fw["B01"] = upA_war[0]
        fw["B23"] = upA_war[1]
        hrA = P.v(lambda v: v.tensor_scalar(
            s_hr2[0:1, 0:512], B[4][0:1, 0:512], 1.0 / DFF, None,
            op0=ALU.mult), [statA_last, sbcrd.get("hr2"),
                            (prev_mb if hh == 0 else None)])
        hrB = P.v(lambda v: v.tensor_scalar(
            s_hr2[0:1, 512:1024], B[5][0:1, 0:512], 1.0 / DFF, None,
            op0=ALU.mult), [statB_last, hrA])
        h1 = P.s(lambda s: s.activation(s_hr2[:], s_hr2[:], AF.Sqrt),
                 [hrA, hrB])
        h2 = P.s(lambda s: s.activation(
            s_hr2[:], s_hr2[:], AF.Identity, bias=s_eps8[:]), [h1])
        h3 = P.v(lambda v: v.reciprocal(s_hr2[:], s_hr2[:]), [h2])
        b_sA = P.t(lambda t: t.matmul(
            B[4][:, 0:512], lhsT=s_oner[:], rhs=s_hr2[0:1, 0:512],
            start=True, stop=True), [h3, m1, hrA])
        b_sB = P.t(lambda t: t.matmul(
            B[5][:, 0:512], lhsT=s_oner[:], rhs=s_hr2[0:1, 512:1024],
            start=True, stop=True), [h3, hrB])
        cpA = P.v(lambda v: v.tensor_copy(SE[:, 0:512], B[4][:, 0:512]),
                  [b_sA, sbcrd.get("sbc"), (pj if hh == 0 else None)])
        cpB = P.v(lambda v: v.tensor_copy(SE[:, 512:1024], B[5][:, 0:512]),
                  [b_sB, cpA])
        fw["B4"] = cpA
        fw["B5"] = cpB
        sbcrd["hr2"] = b_sB
        ztoksA = {}
        ztoksB = {}
        for cj in range(8):
            ldq = {}
            for q in range(2):
                r = q
                ldq[q] = P.sd(lambda s, _cj=cj, _q=q, _r=r: s.dma_start(
                    out=AWb[:, _r * 2048:(_r + 1) * 2048].rearrange(
                        "p (j m) -> p j m", j=2),
                    in_=WDN_out[_cj:_cj + 1,
                                (2 * _q) * 131072:
                                (2 * _q + 2) * 131072].rearrange(
                        "o (j p m) -> (o p) j m", j=2, p=128, m=1024)),
                    [ccWd if (hh == 0 and cj == 0) else None,
                     awfree[2 * r], awfree[2 * r + 1],
                     dnfree.get((hh, cj - 1, r))],
                    ch=f"wd{r}")
            pA = B[0 + 2 * (cj % 2)]
            pB = B[1 + 2 * (cj % 2)]
            mmA = mmB = None
            for f in range(32):
                r = f // 16
                lt = P.total("syd", f"wd{r}") if f % 16 == 0 else None
                mmA = P.t(lambda t, _f=f, _r=r, _pA=pA, _lt=lt: t.matmul(
                    _pA[:, 0:512],
                    lhsT=AWb[:, _r * 2048 + (_f % 16) * 128:
                             _r * 2048 + (_f % 16 + 1) * 128],
                    rhs=SBb[:, _f * 1024:_f * 1024 + 512],
                    start=(_f == 0), stop=(_f == 31)),
                    [lt, geltoks[f][0],
                     (fw.get("B01" if cj % 2 == 0 else "B23")
                      if f == 0 else None)])
                mmB = P.t(lambda t, _f=f, _pB=pB: t.matmul(
                    _pB[:, 0:512],
                    lhsT=AWb[:, (_f // 16) * 2048 + (_f % 16) * 128:
                             (_f // 16) * 2048 + (_f % 16 + 1) * 128],
                    rhs=SBb[:, _f * 1024 + 512:(_f + 1) * 1024],
                    start=(_f == 0), stop=(_f == 31)),
                    [geltoks[f][1], mmA])
                if f == 15:
                    dnfree[(hh, cj, 0)] = mmB
                if f == 31:
                    dnfree[(hh, cj, 1)] = mmB
            tA = P.v(lambda v, _pA=pA: v.tensor_tensor(
                s_tmp[:, 0:512], _pA[:, 0:512], SE[:, 0:512],
                ALU.mult), [mmB, cpA, fw.get("tmp"), fw["tmprd"]])
            zA = P.v(lambda v, _cj=cj, _t0=tok0: v.tensor_tensor(
                SA[:, _cj * n + _t0:_cj * n + _t0 + 512],
                s_tmp[:, 0:512],
                SA[:, _cj * n + _t0:_cj * n + _t0 + 512], ALU.add), [tA])
            tB = P.v(lambda v, _pB=pB: v.tensor_tensor(
                s_tmp[:, 0:512], _pB[:, 0:512], SE[:, 512:1024],
                ALU.mult), [mmB, cpB, zA])
            zB = P.v(lambda v, _cj=cj, _t0=tok0: v.tensor_tensor(
                SA[:, _cj * n + _t0 + 512:_cj * n + _t0 + 1024],
                s_tmp[:, 0:512],
                SA[:, _cj * n + _t0 + 512:_cj * n + _t0 + 1024], ALU.add),
                [tB])
            fw["tmp"] = zB
            if cj % 2 == 0:
                fw["B01"] = zB
            else:
                fw["B23"] = zB
            ztoksA[cj] = zA
            ztoksB[cj] = zB
        sbcrd["sbc"] = fw["tmp"]
        ln2war["sq"] = ztoksB[7]
        ln2war["bc"] = ztoksB[7]
        for ts in range(2):
            gi = hh * 2 + ts
            ytl = [ztoksA[cj] if ts == 0 else ztoksB[cj] for cj in range(8)]
            ln2war["eps_dep"] = None
            outt = emit_ln(P, n, gi, 512, SA, SA, s_m, s_r, s_tmp, s_trow,
                           s_onec, s_oner, B[6][0:1, :], B[4][0:1, :],
                           B[5], B[2], ytl, ln2war, s_eps=s_eps,
                           s_onecb=s_onecb)
            x2toks[gi] = outt
        fw["sq"] = ln2war["sq"]
        fw["B4"] = ln2war["s2"]
        fw["B5"] = ln2war["bc"]
        fw["B6"] = ln2war["s1"]
        fw["B23"] = ln2war["bc"]
        fw["tmprd"] = ln2war["sq"]
        fw["tmp"] = ln2war["sq"]
        sbcrd["hr2"] = ln2war["s2"]
        sbcrd["sbc"] = ln2war["bc"]
        hwar = None
        if hh == 0:
            hwar = ln2war["bc"]
            upA_war = {0: ln2war["bc"], 1: ln2war["bc"]}
            for u in range(4):
                awfree[u] = dnfree[(0, 7, u // 2)]
    p4done = x2toks[3][-1]
    snap2 = None
    if dump:
        snap2 = P.sd(lambda s: s.dma_start(out=XD2, in_=SA[:]), [p4done],
                     ch="dbg")
        P.sd(lambda s: s.dma_start(out=XDHH, in_=SBb[:, 0:NFF * 1024]),
             [p4done], ch="dbg")
        P.sd(lambda s: s.dma_start(out=XDSB, in_=SE[:, 0:1024]),
             [p4done], ch="dbg")
        p4done = P.total("syd", "dbg")

    awf = [dnfree[(1, 7, 0)], dnfree[(1, 7, 1)]]
    for wi, (widx0, dst, sbias) in enumerate(((0, 0, s_bq), (2, KOFF, s_bk))):
        for half in range(2):
            wln = P.gd(lambda g, _w=widx0 + half: g.dma_start(
                out=AWb[:, 0:NC * 512].rearrange("p (a m) -> p a m", a=NC),
                in_=WQKVC[_w:_w + 1].rearrange("o p a m -> (o p) a m")),
                [war.get("awfree2"), p4done] + awf, ch="wqk")
            awf = []
            wlnt = P.total("gpd", "wqk")
            for i in range(nsub):
                casts = []
                for j in range(NC):
                    off = j * n + i * 512
                    casts.append(P.v(lambda v, _o=off, _j=j: v.tensor_copy(
                        SEb[:, _j * 512:(_j + 1) * 512], SA[:, _o:_o + 512]),
                        [x2toks[i][j], war.get("x2b")]))
                for hq in range(4):
                    hp = half * 4 + hq
                    mmt = None
                    for j in range(NC):
                        mmt = P.t(lambda t, _j=j, _hq=hq: t.matmul(
                            B[_hq % 2][:, 0:512],
                            lhsT=AWb[:, _j * 512 + _hq * 128:
                                     _j * 512 + _hq * 128 + 128],
                            rhs=SEb[:, _j * 512:(_j + 1) * 512],
                            start=(_j == 0), stop=(_j == NC - 1)),
                            [wlnt, casts[j], war.get(("pqk", hq % 2)),
                             (fw["B01"] if wi == 0 and half == 0 and i == 0
                              and hq < 2 else None)])
                    ev = P.s(lambda s, _i=i, _hp=hp, _hq=hq, _d=dst,
                             _sb=sbias: s.activation(
                        SBb[:, _d + _hp * n + _i * 512:
                            _d + _hp * n + _i * 512 + 512],
                        B[_hq % 2][:, 0:512], AF.Identity,
                        bias=_sb[:, _hp:_hp + 1]), [mmt, lall, p4done])
                    war[("pqk", hq % 2)] = ev
                    war[("qk", wi, hp)] = ev
                war["x2b"] = war[("pqk", 1)]
            war["awfree2"] = war[("pqk", 1)]
    for half in range(2):
        wlv = P.gd(lambda g, _h=half: g.dma_start(
            out=AWb[:, 0:NC * 512].rearrange("p (a m) -> p a m", a=NC),
            in_=WQKVC[4 + _h:5 + _h].rearrange("o p a m -> (o p) a m")), [war["awfree2"]], ch="wv")
        wlvt = P.total("gpd", "wv")
        for i in range(nsub):
            casts = []
            for j in range(NC):
                off = j * n + i * 512
                casts.append(P.v(lambda v, _o=off, _j=j: v.tensor_copy(
                    SEb[:, _j * 512:(_j + 1) * 512], SA[:, _o:_o + 512]),
                    [x2toks[i][j], war.get("x2b")]))
            for r in range(4):
                tt = i * 4 + r
                mmt = None
                for j in range(NC):
                    mmt = P.t(lambda t, _j=j, _r=r: t.matmul(
                        B[2][:, 0:512],
                        lhsT=SEb[:, _j * 512 + _r * 128:
                                 _j * 512 + _r * 128 + 128],
                        rhs=AWb[:, _j * 512:(_j + 1) * 512],
                        start=(_j == 0), stop=False),
                        [wlvt, casts[j], war.get("pv"),
                         (fw["B23"] if half == 0 and i == 0 and r == 0
                          else None)])
                mmt = P.t(lambda t, _h=half: t.matmul(
                    B[2][:, 0:512], lhsT=s_onerb[:],
                    rhs=s_bvb[0:1, _h * 512:(_h + 1) * 512],
                    start=False, stop=True), [mmt, m3b, bvc])
                vx = P.v(lambda v, _tt=tt, _h=half: v.tensor_copy(
                    SCb[:, _tt * 1024 + _h * 512:
                        _tt * 1024 + (_h + 1) * 512],
                    B[2][:, 0:512]), [mmt, war.get("vxw"), p4done])
                war["pv"] = vx
                war["vxw"] = vx
                war[("vx", tt, half)] = vx
            war["x2b"] = war["pv"]
        war["awfree2"] = war["pv"]
    LN16 = float(np.log(16.0))
    phs = [[B[4], B[5]][g // 2][:, (g % 2) * 256:(g % 2) * 256 + 256]
           for g in range(4)]
    exs = [s_p56[:, g * 256:(g + 1) * 256] for g in range(4)]
    pks4 = [SEb[:, t * 256:(t + 1) * 256] for t in range(4)]
    for h in range(HEADS):
        hp, hb = h // 2, (h % 2) * 64
        vhalf, hv = h // 8, h % 8
        kv0 = kv1 = None
        for tt in range(ntile):
            g = tt % 4
            rr = tt * 128
            xp = P.t(lambda t, _hp=hp, _hb=hb, _rr=rr, _h=h, _g=g:
                     t.matmul(
                phs[_g],
                lhsT=SBb[_hb:_hb + 64,
                         KOFF + _hp * n + _rr:KOFF + _hp * n + _rr + 128],
                rhs=s_pj[_hb:_hb + 64, _h * 256:(_h + 1) * 256],
                start=True, stop=True),
                [war[("qk", 1, hp)], pj, war.get(("xpfree", g)),
                 (ln2war["bc"] if h == 0 and tt < 4 else None)])
            rm = P.v(lambda v, _g=g: v.reduce_max(
                s_nmx4[:, _g:_g + 1], phs[_g], axis=AX.X,
                negate=True), [xp, war.get(("nmxrd", g))])
            ba = P.v(lambda v, _g=g: v.tensor_scalar(
                s_nmx4[:, _g:_g + 1], s_nmx4[:, _g:_g + 1], -LN16, None,
                op0=ALU.add), [rm])
            ex = P.s(lambda s, _g=g: s.activation(
                exs[_g], phs[_g], AF.Exp, bias=s_nmx4[:, _g:_g + 1]),
                [ba, war.get(("exrd", g))])
            war[("xpfree", g)] = ex
            war[("nmxrd", g)] = ex
            ph = P.v(lambda v, _g=g: v.tensor_scalar(
                pks4[_g], exs[_g], PHI_EPS / 16.0, None,
                op0=ALU.add), [ex, war.get(("pkrd", g))])
            war[("exrd", g)] = ph
            war["pkend"] = ex
            war["pkendv"] = ph
            vsl = SCb[:, tt * 1024 + vhalf * 512 + hv * 64:
                      tt * 1024 + vhalf * 512 + hv * 64 + 64]
            kv0 = P.t(lambda t, _g=g, _tt=tt, _vsl=vsl: t.matmul(
                B[6][:, 0:64], lhsT=pks4[_g][:, 0:128], rhs=_vsl,
                start=(_tt == 0), stop=False),
                [ph, war[("vx", tt, vhalf)],
                 (war.get("kv0ev") if tt == 0 else None),
                 (ln2war["s1"] if h == 0 and tt == 0 else None)])
            kv0 = P.t(lambda t, _g=g, _tt=tt: t.matmul(
                B[6][:, 64:65], lhsT=pks4[_g][:, 0:128], rhs=s_onecb[:],
                start=False, stop=(_tt == ntile - 1)), [kv0, m2b])
            kv1 = P.t(lambda t, _g=g, _tt=tt, _vsl=vsl: t.matmul(
                B[3][:, 0:64], lhsT=pks4[_g][:, 128:256], rhs=_vsl,
                start=(_tt == 0), stop=False),
                [ph, war[("vx", tt, vhalf)],
                 (war.get("kv1ev") if tt == 0 else None)])
            kv1 = P.t(lambda t, _g=g, _tt=tt: t.matmul(
                B[3][:, 64:65], lhsT=pks4[_g][:, 128:256], rhs=s_onecb[:],
                start=False, stop=(_tt == ntile - 1)), [kv1, m2b])
            war[("pkrd", g)] = kv1
        e1 = P.v(lambda v, _h=h: v.tensor_copy(
            SD[:, _h * 65:(_h + 1) * 65], B[6][:, 0:65]),
            [kv0, kv1, (p4done if h == 0 else None),
             (prev_mb if h == 0 else None)])
        e2 = P.v(lambda v, _h=h: v.tensor_copy(
            SD[:, 1040 + _h * 65:1040 + (_h + 1) * 65], B[3][:, 0:65]),
            [e1])
        war["kv0ev"] = e1
        war["kv1ev"] = e2
        war[("kvh", h)] = e2

    dkv = P.gd(lambda gg: gg.dma_start(out=kv_in, in_=SD[:, 0:2080]),
               [war[("kvh", hh)] for hh in range(HEADS)], ch="kvio")
    cc3 = P.col(lambda gg: gg.collective_compute(
        "AllReduce", ALU.add, replica_groups=[[0, 1, 2, 3], [4, 5, 6, 7]],
        ins=[kv_in.opt()], outs=[kv_out.opt()]), [P.total("gpd", "kvio")])
    P.gd(lambda gg: gg.dma_start(out=SD[:, 0:2080], in_=kv_out),
         [cc3], ch="kvb")
    lkvb = P.total("gpd", "kvb")

    if dump:
        P.sd(lambda s: s.dma_start(out=XDKV, in_=kv_out), [cc3], ch="dbg")
    pw = {"eps_dep": None, "sq": ln2war["sq"], "trow": ln2war["trow"]}
    if dump:
        for cj in range(NC):
            pw[("zr", cj)] = snap2
    war["pTread"] = war["x2b"]
    x3toks = {}
    for i in range(nsub):
        ndtoks = []
        for h in range(HEADS):
            hp, hb = h // 2, (h % 2) * 64
            kc = P.v(lambda v, _h=h: v.tensor_copy(
                s_kvs[:, 0:65], SD[:, _h * 65:(_h + 1) * 65]),
                [lkvb, war.get("kvsread")])
            kc2 = P.v(lambda v, _h=h: v.tensor_copy(
                s_kvs[:, 65:130],
                SD[:, 1040 + _h * 65:1040 + (_h + 1) * 65]), [kc])
            trtoks = []
            for r in range(4):
                n0 = i * 512 + r * 128
                xq = P.t(lambda t, _hp=hp, _hb=hb, _n0=n0, _r=r, _h=h:
                         t.matmul(
                    B[4 + _r % 2][:, 0:256],
                    lhsT=SBb[_hb:_hb + 64,
                             _hp * n + _n0:_hp * n + _n0 + 128],
                    rhs=s_pj[_hb:_hb + 64, _h * 256:(_h + 1) * 256],
                    start=True, stop=True), [pj, war[("qk", 0, hp)],
                                             war.get(("xpk", r % 2)),
                                             (war.get("pkendv")
                                              if i == 0 and h == 0
                                              else None)])
                par = r % 2
                rm = P.v(lambda v, _r=r, _p=par: v.reduce_max(
                    s_nmx2[_p][:], B[4 + _r % 2][:, 0:256], axis=AX.X,
                    negate=True), [xq, war.get(("nmxread", par))])
                ex = P.s(lambda s, _r=r, _p=par: s.activation(
                    s_exp2[_p][:], B[4 + _r % 2][:, 0:256], AF.Exp,
                    bias=s_nmx2[_p][:]), [rm, war.get(("exread", par))])
                war[("xpk", r % 2)] = ex
                war[("nmxread", par)] = ex
                phq = P.v(lambda v, _p=par: v.tensor_scalar(
                    s_phik2[_p][:], s_exp2[_p][:], 1.0 / 16.0,
                    PHI_EPS / 16.0, op0=ALU.mult, op1=ALU.add),
                    [ex, war.get(("phikread", par))])
                war[("exread", par)] = phq
                t1 = P.t(lambda t, _p=par: t.transpose(
                    B7[:, _p * 256:_p * 256 + 128], s_phik2[_p][:, 0:128],
                    s_idb[:]), [phq, lall, war.get(("trread", par))])
                t2 = P.t(lambda t, _p=par: t.transpose(
                    B7[:, _p * 256 + 128:_p * 256 + 256],
                    s_phik2[_p][:, 128:256], s_idb[:]), [t1])
                c1t = P.v(lambda v, _r=r, _p=par: v.tensor_copy(
                    s_phiqT[0][:, _r * 128:(_r + 1) * 128],
                    B7[:, _p * 256:_p * 256 + 128]), [t2, war.get("pTread")])
                c2t = P.v(lambda v, _r=r, _p=par: v.tensor_copy(
                    s_phiqT[1][:, _r * 128:(_r + 1) * 128],
                    B7[:, _p * 256 + 128:_p * 256 + 256]), [c1t])
                war[("phikread", par)] = c2t
                war[("trread", par)] = c2t
                trtoks.append(c2t)
            nm = P.t(lambda t: t.matmul(
                B[0][0:65, 0:512], lhsT=s_kvs[:, 0:65],
                rhs=s_phiqT[0][:], start=True, stop=False),
                trtoks + [kc2, war.get("numread")])
            nm = P.t(lambda t: t.matmul(
                B[0][0:65, 0:512],
                lhsT=s_kvs[:, 65:130],
                rhs=s_phiqT[1][:], start=False, stop=True), [nm])
            war["pTread"] = nm
            war["kvsread"] = nm
            d1 = P.s(lambda s: s.activation(
                s_den[:], B[0][64:65, 0:512], AF.Ln, bias=s_epsphi[:]),
                [nm, m4c, war.get("denread")])
            d2 = P.s(lambda s: s.activation(
                s_den[:], s_den[:], AF.Exp, scale=-1.0), [d1])
            dcb = P.v(lambda v: v.tensor_copy(s_denb[:], s_den[:]),
                      [d2, war.get("denbrd")])
            db = P.t(lambda t: t.matmul(
                B[1][:, 0:512], lhsT=s_onerb[:], rhs=s_denb[:],
                start=True, stop=True), [dcb, m3b, war.get("dbcread")])
            dc = P.v(lambda v: v.tensor_copy(s_dbc[:], B[1][:, 0:512]),
                     [db])
            war["denread"] = dc
            war["denbrd"] = db
            nd = P.v(lambda v, _hp=hp, _hb=hb, _i=i: v.tensor_tensor(
                SCb[_hb:_hb + 64,
                    _hp * n + _i * 512:_hp * n + _i * 512 + 512],
                B[0][0:64, 0:512], s_dbc[_hb:_hb + 64, :], ALU.mult),
                [dc, nm, war.get("ndw")])
            war["numread"] = nd
            war["dbcread"] = nd
            ndtoks.append(nd)
        ztoks = []
        for whalf in range(2):
            wlo = P.gd(lambda g, _h=whalf: g.dma_start(
                out=AWb[:, 0:NC * 512].rearrange("p (a m) -> p a m", a=NC),
                in_=WOC[_h:_h + 1].rearrange("o p a m -> (o p) a m")),
                [war.get("awfree2"), pw.get("wofree")], ch="wo")
            wlot = P.total("gpd", "wo")
            for cj4 in range(4):
                cj = whalf * 4 + cj4
                mmt = None
                for j in range(NC):
                    mmt = P.t(lambda t, _c4=cj4, _j=j, _i=i,
                              _lt=wlot: t.matmul(
                        B[2 + _c4 % 2][:, 0:512],
                        lhsT=AWb[:, _j * 512 + _c4 * 128:
                                 _j * 512 + (_c4 + 1) * 128],
                        rhs=SCb[:, _j * n + _i * 512:
                                _j * n + _i * 512 + 512],
                        start=(_j == 0), stop=(_j == NC - 1)),
                        ndtoks + [wlot, pw.get(("o", cj4 % 2))])
                pw["wofree"] = mmt
                ev = P.s(lambda s, _c=cj, _c4=cj4: s.activation(
                    s_tmp[:, 0:512], B[2 + _c4 % 2][:, 0:512], AF.Identity,
                    bias=s_bo[:, _c:_c + 1]), [mmt, lall, pw.get("tread")])
                zt = P.v(lambda v, _c=cj, _i=i: v.tensor_tensor(
                    SA[:, _c * n + _i * 512:_c * n + _i * 512 + 512],
                    s_tmp[:, 0:512],
                    SA[:, _c * n + _i * 512:_c * n + _i * 512 + 512],
                    ALU.add), [ev, pw.get(("zr", cj))])
                pw[("o", cj4 % 2)] = zt
                pw["tread"] = zt
                ztoks.append(zt)
        war["awfree2"] = pw["wofree"]
        pw["sq"] = ztoks[-1]
        pw["eps_dep"] = None
        outt = emit_ln(P, n, i, 512, SA, SA, s_m, s_r, s_tmp, s_trow,
                       s_onec, s_oner, B[4][0:1, :], B[5][0:1, :], B[6],
                       B[1], ztoks, pw, s_eps=s_eps, s_onecb=s_onecb)
        pw["tread"] = pw["sq"]
        for cj in range(NC):
            pw[("zr", cj)] = outt[-1]
        war[("xpk", 0)] = outt[-1]
        war[("xpk", 1)] = outt[-1]
        if i + 1 < nsub:
            war["ndw"] = outt[-1]
        x3toks[i] = outt

    x3done = x3toks[nsub - 1][-1]
    if dump:
        snap3 = P.sd(lambda s: s.dma_start(out=XD3, in_=SA[:]), [x3done],
                     ch="dbg")
        dwseed = {("zr", cj): snap3 for cj in range(NC)}
    else:
        dwseed = {}
    P.gd(lambda g: g.dma_start(
        out=s_winb[:].rearrange("p (a m) -> p a m", a=NC), in_=WINC),
        [lkvb], ch="wsm")
    P.gd(lambda g: g.dma_start(out=s_woutb[:], in_=WOUT), [lkvb],
         ch="wsm")
    lwsm = P.total("gpd", "wsm")
    dw = {"eps_dep": None, "sq": pw.get("sq"), "trow": pw.get("trow")}
    dw.update(dwseed)
    stoks = []
    for i in range(nsub):
        casts = []
        for j in range(NC):
            off = j * n + i * 512
            casts.append(P.v(lambda v, _o=off, _j=j: v.tensor_copy(
                SEb[:, _j * 512:(_j + 1) * 512], SA[:, _o:_o + 512]),
                [x3toks[i][j], dw.get("x3b"), war["pTread"]]))
        mmt = None
        for j in range(NC):
            mmt = P.t(lambda t, _j=j: t.matmul(
                B[6][0:64, 0:512], lhsT=s_winb[:, _j * 64:(_j + 1) * 64],
                rhs=SEb[:, _j * 512:(_j + 1) * 512],
                start=(_j == 0), stop=(_j == NC - 1)),
                [lwsm, casts[j], dw.get("psread"), war["numread"]])
        sv = P.s(lambda s: s.activation(
            s_s[:], B[6][0:64, 0:512], AF.Identity, bias=s_bin[:]),
            [mmt, lall, dw.get("sread"), war["numread"]])
        dw["psread"] = sv
        gt = None
        for ii in range(16):
            hi = P.s(lambda s, _ii=ii: s.activation(
                s_hi[:], s_s[:], AF.Relu,
                scale=s_w1[:, _ii:_ii + 1], bias=s_b1[:, _ii:_ii + 1]),
                [sv, lall, dw.get("hiread"), dw.get("hi_war"),
                 war["kvsread"]])
            if ii == 0:
                gt = P.v(lambda v: v.tensor_scalar(
                    s_g[:], s_hi[:], s_w2[:, 0:1], None, op0=ALU.mult),
                    [hi, dw.get("gread")])
            else:
                gt = P.v(lambda v, _ii=ii: v.scalar_tensor_tensor(
                    s_g[:], s_hi[:], s_w2[:, _ii:_ii + 1], s_g[:],
                    op0=ALU.mult, op1=ALU.add), [hi, gt])
            dw["hiread"] = gt
        dw["sread"] = gt
        g2 = P.v(lambda v: v.tensor_scalar(
            s_g[:], s_g[:], s_b2[:], None, op0=ALU.add), [gt])
        gb = P.v(lambda v: v.tensor_copy(s_gb[:], s_g[:]),
                 [g2, dw.get("gbread")])
        dw["gread"] = gb
        ztoks = []
        for cj in range(NC):
            mmt = P.t(lambda t, _c=cj: t.matmul(
                B[2 + _c % 2][:, 0:512],
                lhsT=s_woutb[:, _c * 128:(_c + 1) * 128],
                rhs=s_gb[:], start=True, stop=True),
                [gb, lwsm, dw.get(("o", cj % 2))])
            ev = P.s(lambda s, _c=cj: s.activation(
                s_tmp[:, 0:512], B[2 + _c % 2][:, 0:512], AF.Identity,
                bias=s_bout[:, _c:_c + 1]), [mmt, dw.get("tread")])
            zt = P.v(lambda v, _c=cj, _i=i: v.tensor_tensor(
                SA[:, _c * n + _i * 512:_c * n + _i * 512 + 512],
                s_tmp[:, 0:512],
                SA[:, _c * n + _i * 512:_c * n + _i * 512 + 512], ALU.add),
                [ev, dw.get(("zr", cj)), gb])
            dw[("o", cj % 2)] = zt
            dw["tread"] = zt
            ztoks.append(zt)
        dw["gbread"] = ztoks[-1]
        dw["x3b"] = ztoks[-1]
        dw["sq"] = ztoks[-1]
        outt = emit_ln(P, n, i, 512, SA, SA, s_m, s_r, s_tmp, s_trow,
                       s_onec, s_oner, B[4][0:1, :], B[5][0:1, :], B[0],
                       B[1], ztoks, dw, s_eps=s_eps, s_onecb=s_onecb)
        dw["tread"] = dw["sq"]
        stok = P.sd(lambda s, _i=i: s.dma_start(
            out=fmview(YT, n)[:, :, _i * 512:_i * 512 + 512],
            in_=SA[:].rearrange("p (a m) -> p a m", a=NC)[
                :, :, _i * 512:_i * 512 + 512]), outt, ch="st")
        for cj in range(NC):
            dw[("zr", cj)] = stok
        dw["hi_war"] = stok
        stoks.append(stok)
    if dump:
        stoks = stoks + [P.total("syd", "dbg")]
    P.finish(stoks)
    P.build()
    return nc


_V2_NC = None


def kernel(L0_attn_x, L0_attn_router_w, L0_attn_router_b, L0_attn_wq,
           L0_attn_wk, L0_attn_wv, L0_ffn_w_up, L0_ffn_w_down, L1_attn_w_qkv,
           L1_attn_b_qkv, L1_attn_w_out, L1_attn_b_out, L1_attn_proj_matrix,
           L1_ffn_Win, L1_ffn_bin_, L1_ffn_W1, L1_ffn_b1, L1_ffn_W2,
           L1_ffn_b2, L1_ffn_Wout, L1_ffn_bout):
    _ = (L0_attn_wq.shape, L0_attn_wk.shape, L0_attn_wv.shape)
    global _V2_NC
    if _V2_NC is None:
        _V2_NC = build_v2(NTOK)
    nc = _V2_NC
    x = np.ascontiguousarray(L0_attn_x, dtype=np.float32)
    identb = np.eye(128, dtype=np.float32).astype(ml_dtypes.bfloat16)
    bqkv = L1_attn_b_qkv
    wup = np.asarray(L0_ffn_w_up)
    wdn = np.asarray(L0_ffn_w_down)
    wqkv = np.asarray(L1_attn_w_qkv)
    wo = np.asarray(L1_attn_w_out)
    shared = {
        "rwc": np.ascontiguousarray(L0_attn_router_w.reshape(NC, 128).T),
        "rb": np.array([[L0_attn_router_b[0]]], dtype=np.float32),
        "identb": identb,
        "wqkvc": np.ascontiguousarray(
            wqkv.reshape(8, 128, 6, 512).transpose(2, 1, 0, 3)),
        "woc": np.ascontiguousarray(
            wo.reshape(8, 128, 2, 512).transpose(2, 1, 0, 3)),
        "bq": np.ascontiguousarray(bqkv[0:D].reshape(NC, 128).T),
        "bk": np.ascontiguousarray(bqkv[D:2 * D].reshape(NC, 128).T),
        "bvrow": np.ascontiguousarray(bqkv[2 * D:3 * D][None, :]),
        "projt": np.ascontiguousarray(np.concatenate(
            [L1_attn_proj_matrix[h].T for h in range(HEADS)], axis=1)),
        "bo": np.ascontiguousarray(L1_attn_b_out.reshape(NC, 128).T),
        "winc": np.ascontiguousarray(
            np.asarray(L1_ffn_Win).reshape(8, 128, 64).transpose(1, 0, 2)),
        "bin": np.ascontiguousarray(L1_ffn_bin_[:, None]),
        "w1": np.ascontiguousarray(L1_ffn_W1),
        "b1": np.ascontiguousarray(L1_ffn_b1),
        "w2": np.ascontiguousarray(L1_ffn_W2),
        "b2": np.ascontiguousarray(L1_ffn_b2[:, None]),
        "wout": np.ascontiguousarray(L1_ffn_Wout),
        "bout": np.ascontiguousarray(L1_ffn_bout.reshape(NC, 128).T),
    }
    wupc_full = np.ascontiguousarray(
        wup.reshape(32, 128, 8, 128).transpose(0, 3, 2, 1)
        .reshape(32, 128, 1024))
    wdnc_full = np.ascontiguousarray(
        wdn.reshape(8, 128, 32, 128).transpose(0, 3, 2, 1)
        .reshape(8, 128, 4096))
    maps = []
    for core in range(8):
        b, o = core // 4, (core % 4) * NTOK
        m = dict(shared)
        m["xt"] = np.ascontiguousarray(x[b, o:o + NTOK].T)
        m["xrevt"] = np.ascontiguousarray(x[b][::-1][o:o + NTOK].T)
        m["wups"] = np.ascontiguousarray(wupc_full[4 * core:4 * core + 4])
        m["wdns"] = np.ascontiguousarray(wdnc_full[core])
        maps.append(m)
    res = run_bass_kernel_spmd(nc, maps, core_ids=list(range(8)))
    out = np.zeros((2, NSEQ, D), np.float32)
    for core in range(8):
        b, o = core // 4, (core % 4) * NTOK
        out[b, o:o + NTOK] = res.results[core]["yt"].T
    return out


# version 14
# speedup vs baseline: 1.0952x; 1.0952x over previous
import numpy as np
import ml_dtypes
from contextlib import ExitStack

import concourse.bass as bass
import concourse.mybir as mybir
from concourse.bass_utils import run_bass_kernel_spmd

F32 = mybir.dt.float32
BF16 = mybir.dt.bfloat16
I32 = mybir.dt.int32
AF = mybir.ActivationFunctionType
ALU = mybir.AluOpType
AX = mybir.AxisListType

D = 1024
NC = 8
DFF = 4096
NFF = 32
NTOK = 2048
NSEQ = 8192
HEADS = 16
HDIM = 64
LN_EPS = 1e-5
RMS_EPS = 1e-8
PHI_EPS = 1e-6
DATA_NORM = float(np.float32(64.0 ** -0.25))
NITER_BISECT = 50
ROUTER_K = 1024.0
KOFF = 16384


class Prog:
    SECTIONS = {
        "tensor": ("pe", 1),
        "vector": ("ve", 1),
        "scalar": ("sc", 1),
        "gpsimd": ("gp", 1),
    }

    def __init__(self, nc):
        self.nc = nc
        self.es = ExitStack()
        self.items = {s: [] for s in
                      ("tensor", "vector", "scalar", "gpsimd", "sync")}
        self.counts = {}
        self.n_cc = 0
        self.final_deps = []

    def _add(self, section, sem, inc, fn, deps):
        n = self.counts.get(sem, 0) + inc
        self.counts[sem] = n
        self.items[section].append((fn, [d for d in deps if d is not None],
                                    sem, inc))
        return (sem, n)

    def t(self, fn, deps=()):
        return self._add("tensor", "pe", 1, fn, deps)

    def v(self, fn, deps=()):
        return self._add("vector", "ve", 1, fn, deps)

    def s(self, fn, deps=()):
        return self._add("scalar", "sc", 1, fn, deps)

    def g(self, fn, deps=()):
        return self._add("gpsimd", "gp", 1, fn, deps)

    def sd(self, fn, deps=(), ch="ld"):
        return self._add("sync", f"syd_{ch}", 16, fn, deps)

    def ad(self, fn, deps=(), ch="ld"):
        return self._add("scalar", f"scd_{ch}", 16, fn, deps)

    def gd(self, fn, deps=(), ch="ld"):
        return self._add("gpsimd", f"gpd_{ch}", 16, fn, deps)

    def total(self, queue, ch):
        name = f"{queue}_{ch}"
        return (name, self.counts.get(name, 0))

    def col(self, fn, deps=()):
        self.n_cc += 1
        sem = f"cc{self.n_cc}"
        return self._add("gpsimd", sem, 1, fn, deps)

    def finish(self, deps):
        self.final_deps = list(deps)

    def build(self):
        nc = self.nc
        sems = {}
        for name in self.counts:
            sems[name] = self.es.enter_context(nc.semaphore(name))
        block = self.es.enter_context(nc.Block())
        hw = {}

        def coalesce(deps):
            best = {}
            for dsem, dval in deps:
                if best.get(dsem, 0) < dval:
                    best[dsem] = dval
            return best.items()

        def emit(section, eng):
            for fn, deps, sem, inc in self.items[section]:
                for dsem, dval in coalesce(deps):
                    key = (section, dsem)
                    if hw.get(key, 0) < dval:
                        hw[key] = dval
                        eng.wait_ge(sems[dsem], dval)
                ins = fn(eng)
                ins.then_inc(sems[sem], inc)
            if section == "sync":
                for dsem, dval in coalesce(self.final_deps):
                    key = (section, dsem)
                    if hw.get(key, 0) < dval:
                        hw[key] = dval
                        eng.wait_ge(sems[dsem], dval)

        if self.items["tensor"]:
            block.tensor(lambda t: emit("tensor", t))
        if self.items["vector"]:
            block.vector(lambda v: emit("vector", v))
        if self.items["scalar"]:
            block.scalar(lambda s: emit("scalar", s))
        if self.items["gpsimd"]:
            block.gpsimd(lambda g: emit("gpsimd", g))
        block.sync(lambda s: emit("sync", s))
        return self


def sb(nc, es, name, shape, dt=F32):
    return es.enter_context(nc.sbuf_tensor(name, shape, dt))


def ps(nc, es, name, shape, dt=F32):
    return es.enter_context(nc.psum_tensor(name, shape, dt))


def fmview(ap, n):
    return ap.rearrange("(a p) n -> p a n", p=128)


def emit_bisect(P, s_rwf, s_ones2, s_oner, p_cnt, p_bc, st,
                dep_rwf, dep_ones2, k=ROUTER_K, niter=NITER_BISECT):
    t_lo, t_hi, t_mid = st["t_lo"], st["t_hi"], st["t_mid"]
    fs, fsn, rowcnt = st["fs"], st["fsn"], st["rowcnt"]
    mask64 = st["mask64"]

    c1 = P.v(lambda v: v.memset(t_hi[:], 1e4), [dep_rwf])
    c2 = P.v(lambda v: v.memset(t_lo[:], -1e4), [c1])
    last = P.v(lambda v: v.memset(t_mid[:], 0.0), [c2])
    prev_f1 = None
    for i in range(niter):
        slot = i % 2
        mk = P.v(lambda v: v.tensor_scalar(
            mask64[:], s_rwf[:], t_mid[:], 0.0, op0=ALU.is_ge, op1=ALU.add,
            accum_out=rowcnt[:]), [last, prev_f1])
        cm = P.t(lambda t, _s=slot: t.matmul(
            p_cnt[:, _s:_s + 1], lhsT=s_ones2[:], rhs=rowcnt[:],
            start=True, stop=True), [mk, dep_ones2, prev_f1])
        f1 = P.v(lambda v, _s=slot: v.tensor_scalar(
            fs[:], p_cnt[:, _s:_s + 1], k, None, op0=ALU.is_ge), [cm])
        f2 = P.v(lambda v: v.tensor_scalar(fsn[:], fs[:], -1, 1,
                                           op0=ALU.mult, op1=ALU.add), [f1])
        s1 = P.v(lambda v: v.copy_predicated(t_lo[:], fs[:], t_mid[:]), [f2])
        s2 = P.v(lambda v: v.copy_predicated(t_hi[:], fsn[:], t_mid[:]), [s1])
        s3 = P.v(lambda v: v.tensor_tensor(t_mid[:], t_lo[:], t_hi[:],
                                           ALU.add), [s2])
        last = P.v(lambda v: v.tensor_scalar(t_mid[:], t_mid[:], 0.5, None,
                                             op0=ALU.mult), [s3])
        prev_f1 = f1
    return last


def emit_ln(P, n, i, w, s_y, s_out, s_m, s_r, s_tmp, s_trow, s_onec, s_oner,
            p_s1, p_s2, p_mbc, p_rbc, ytoks, war, s_eps=None,
            extra_out_dep=None, s_onecb=None):
    s1tok = s2tok = None
    sqprev = war.get("sq")
    tmpb = s_tmp[:].bitcast(BF16) if s_onecb is not None else None
    for j in range(NC):
        off = j * n + i * 512
        s1tok = P.t(lambda t, _o=off, _w=w, _j=j: t.matmul(
            p_s1[:, 0:_w], lhsT=s_onec[:], rhs=s_y[:, _o:_o + _w],
            start=(_j == 0), stop=(_j == NC - 1)),
            [ytoks[j], war.get("s1")])
        sq = P.s(lambda s, _o=off, _w=w: s.activation(
            tmpb[:, 0:_w], s_y[:, _o:_o + _w], AF.Square),
            [ytoks[j], sqprev])
        s2tok = P.t(lambda t, _w=w, _j=j: t.matmul(
            p_s2[:, 0:_w], lhsT=s_onecb[:], rhs=tmpb[:, 0:_w],
            start=(_j == 0), stop=(_j == NC - 1)),
            [sq, war.get("s2")])
        sqprev = s2tok
    war["sq"] = sqprev
    o = 0
    f1 = P.v(lambda v, _w=w, _o=o: v.tensor_scalar(
        s_m[:, _o:_o + _w], p_s1[:, 0:_w], 1.0 / D, None, op0=ALU.mult),
        [s1tok])
    f2 = P.v(lambda v, _w=w, _o=o: v.tensor_scalar(
        s_r[:, _o:_o + _w], p_s2[:, 0:_w], 1.0 / D, None, op0=ALU.mult),
        [s2tok])
    war["s1"] = f1
    war["s2"] = f2
    f3 = P.v(lambda v, _w=w, _o=o: v.tensor_tensor(
        s_trow[:, 0:_w], s_m[:, _o:_o + _w], s_m[:, _o:_o + _w],
        ALU.mult), [f1, f2, war.get("trow")])
    f4 = P.v(lambda v, _w=w, _o=o: v.tensor_tensor(
        s_r[:, _o:_o + _w], s_r[:, _o:_o + _w], s_trow[:, 0:_w],
        ALU.subtract), [f3])
    war["trow"] = f4
    f5 = P.s(lambda s, _w=w, _o=o: s.activation(
        s_r[:, _o:_o + _w], s_r[:, _o:_o + _w], AF.Ln,
        bias=s_eps[:]), [f4, war.get("eps_dep")])
    war["eps_dep"] = None
    f6 = P.s(lambda s, _w=w, _o=o: s.activation(
        s_r[:, _o:_o + _w], s_r[:, _o:_o + _w], AF.Exp, scale=-0.5), [f5])
    b1 = P.t(lambda t, _w=w, _o=o: t.matmul(
        p_mbc[:, 0:_w], lhsT=s_oner[:], rhs=s_m[:, _o:_o + _w],
        start=True, stop=True), [f1, war.get("bc")])
    b2 = P.t(lambda t, _w=w, _o=o: t.matmul(
        p_rbc[:, 0:_w], lhsT=s_oner[:], rhs=s_r[:, _o:_o + _w],
        start=True, stop=True), [f6])
    outtoks = []
    for j in range(NC):
        off = j * n + i * 512
        n1 = P.v(lambda v, _o=off, _w=w: v.tensor_tensor(
            s_out[:, _o:_o + _w], s_y[:, _o:_o + _w], p_mbc[:, 0:_w],
            ALU.subtract), [b1, b2, ytoks[j], extra_out_dep])
        n2 = P.v(lambda v, _o=off, _w=w: v.tensor_tensor(
            s_out[:, _o:_o + _w], s_out[:, _o:_o + _w], p_rbc[:, 0:_w],
            ALU.mult), [n1])
        outtoks.append(n2)
    war["bc"] = outtoks[-1]
    return outtoks


def build_v2(n=NTOK, dump=False):
    nc = bass.Bass("TRN2", target_bir_lowering=False, debug=False)
    nsub = n // 512
    ntile = n // 128
    H = 1024

    XT = nc.dram_tensor("xt", [D, n], F32, kind="ExternalInput").ap()
    XRT = nc.dram_tensor("xrevt", [D, n], F32, kind="ExternalInput").ap()
    RWC = nc.dram_tensor("rwc", [128, NC], F32, kind="ExternalInput").ap()
    RB = nc.dram_tensor("rb", [1, 1], F32, kind="ExternalInput").ap()
    IDB = nc.dram_tensor("identb", [128, 128], BF16, kind="ExternalInput").ap()
    WUPS = nc.dram_tensor("wups", [4, 128, 1024], F32,
                          kind="ExternalInput").ap()
    WDNS = nc.dram_tensor("wdns", [128, 4096], F32,
                          kind="ExternalInput").ap()
    WQKVC = nc.dram_tensor("wqkvc", [6, 128, NC, 512], F32,
                           kind="ExternalInput").ap()
    BQ = nc.dram_tensor("bq", [128, 8], F32, kind="ExternalInput").ap()
    BK = nc.dram_tensor("bk", [128, 8], F32, kind="ExternalInput").ap()
    BV = nc.dram_tensor("bvrow", [1, D], F32, kind="ExternalInput").ap()
    PJ = nc.dram_tensor("projt", [64, HEADS * 256], F32,
                        kind="ExternalInput").ap()
    WOC = nc.dram_tensor("woc", [2, 128, NC, 512], F32,
                         kind="ExternalInput").ap()
    BO = nc.dram_tensor("bo", [128, NC], F32, kind="ExternalInput").ap()
    WINC = nc.dram_tensor("winc", [128, NC, 64], F32,
                          kind="ExternalInput").ap()
    BIN = nc.dram_tensor("bin", [64, 1], F32, kind="ExternalInput").ap()
    W1 = nc.dram_tensor("w1", [64, 16], F32, kind="ExternalInput").ap()
    B1 = nc.dram_tensor("b1", [64, 16], F32, kind="ExternalInput").ap()
    W2 = nc.dram_tensor("w2", [64, 16], F32, kind="ExternalInput").ap()
    B2_ = nc.dram_tensor("b2", [64, 1], F32, kind="ExternalInput").ap()
    WOUT = nc.dram_tensor("wout", [64, D], F32, kind="ExternalInput").ap()
    BOUT = nc.dram_tensor("bout", [128, NC], F32, kind="ExternalInput").ap()
    YT = nc.dram_tensor("yt", [D, n], F32, kind="ExternalOutput").ap()

    cc1_in = nc.dram_tensor("cc1_in", [1, n], F32, kind="Internal").ap()
    cc1_out = nc.dram_tensor("cc1_out", [4, n], F32, kind="Internal").ap()
    ST_in = nc.dram_tensor("st_in", [1, 2], F32, kind="Internal").ap()
    ST_out = nc.dram_tensor("st_out", [8, 2], F32, kind="Internal").ap()
    WUP_in = nc.dram_tensor("wup_in", [1, 4 * 128 * 1024], BF16,
                            kind="Internal").ap()
    WUP_out = nc.dram_tensor("wup_out", [8, 4 * 128 * 1024], BF16,
                             kind="Internal").ap()
    WDN_in = nc.dram_tensor("wdn_in", [1, 4 * 128 * 1024], BF16,
                            kind="Internal").ap()
    WDN_out = nc.dram_tensor("wdn_out", [8, 4 * 128 * 1024], BF16,
                             kind="Internal").ap()
    kv_in = nc.dram_tensor("kv_in", [128, 2080], F32, kind="Internal").ap()
    kv_out = nc.dram_tensor("kv_out", [128, 2080], F32, kind="Internal").ap()
    if dump:
        XD1 = nc.dram_tensor("xd1", [128, NC * n], F32,
                             kind="ExternalOutput").ap()
        XDH = nc.dram_tensor("xdh", [128, NC * n], BF16,
                             kind="ExternalOutput").ap()
        XD2 = nc.dram_tensor("xd2", [128, NC * n], F32,
                             kind="ExternalOutput").ap()
        XDW = nc.dram_tensor("xdw", [8, 4 * 128 * 1024], BF16,
                             kind="ExternalOutput").ap()
        XDW2 = nc.dram_tensor("xdw2", [8, 4 * 128 * 1024], BF16,
                              kind="ExternalOutput").ap()
        XD3 = nc.dram_tensor("xd3", [128, NC * n], F32,
                             kind="ExternalOutput").ap()
        XDKV = nc.dram_tensor("xdkv", [128, 2080], F32,
                              kind="ExternalOutput").ap()
        XDHH = nc.dram_tensor("xdhh", [128, NFF * 1024], BF16,
                              kind="ExternalOutput").ap()
        XDSB = nc.dram_tensor("xdsb", [128, 1024], F32,
                              kind="ExternalOutput").ap()
        XDAW = nc.dram_tensor("xdaw", [128, 1024], BF16,
                              kind="ExternalOutput").ap()
        XDH3 = nc.dram_tensor("xdh3", [128, NC * n], BF16,
                              kind="ExternalOutput").ap()

    P = Prog(nc)
    es = P.es
    SA = sb(nc, es, "SA", [128, NC * n])
    SB = sb(nc, es, "SB", [128, NC * n])
    SBb = SB[:].bitcast(BF16)
    SC = sb(nc, es, "SC", [128, 8192])
    SCb = SC[:].bitcast(BF16)
    SD = sb(nc, es, "SD", [128, 2080])
    SE = sb(nc, es, "SE", [128, 2048])
    SEb = SE[:].bitcast(BF16)
    AW = sb(nc, es, "AW", [128, 2048])
    AWb = AW[:].bitcast(BF16)
    PJT = sb(nc, es, "PJT", [128, HEADS * 256], BF16)
    s_pj = PJT
    s_tmp = sb(nc, es, "s_tmp", [128, 512])
    s_tmpb = s_tmp[:].bitcast(BF16)
    s_p56 = sb(nc, es, "s_p56", [128, 1024])
    s_rwc = sb(nc, es, "s_rwc", [128, NC])
    s_rb = sb(nc, es, "s_rb", [1, 1])
    s_rwf = sb(nc, es, "s_rwf", [128, 4 * n // 128])
    s_idb = sb(nc, es, "s_idb", [128, 128], BF16)
    s_onec = sb(nc, es, "s_onec", [128, 1])
    s_onecb = sb(nc, es, "s_onecb", [128, 1], BF16)
    s_ones2 = sb(nc, es, "s_ones2", [128, 128])
    s_oner = s_ones2[0:1, :]
    s_onerb = sb(nc, es, "s_onerb", [1, 128], BF16)
    s_eps = sb(nc, es, "s_eps", [1, 1])
    s_eps8 = sb(nc, es, "s_eps8", [1, 1])
    s_epsphi = sb(nc, es, "s_epsphi", [1, 1])
    s_m = sb(nc, es, "s_m", [1, 512])
    s_r = sb(nc, es, "s_r", [1, 512])
    s_trow = sb(nc, es, "s_trow", [1, 512])
    s_vv = SD[0:1, 0:512]
    s_c2 = SD[0:1, 512:1024]
    s_hr2 = SD[0:1, 1024:2048]
    s_mrow = s_m[0:1, 0:512]
    s_acc = sb(nc, es, "s_acc", [128, 8])
    s_red = sb(nc, es, "s_red", [128, 2])
    s_th = sb(nc, es, "s_th", [1, 2])
    s_thc = sb(nc, es, "s_thc", [128, 2])
    s_aps = sb(nc, es, "s_aps", [1, 16])
    st = {kk: sb(nc, es, kk, [128, 1]) for kk in
          ("t_lo", "t_hi", "t_mid", "rowcnt")}
    st["fs"] = sb(nc, es, "fs", [128, 1], I32)
    st["fsn"] = sb(nc, es, "fsn", [128, 1], I32)
    st["mask64"] = sb(nc, es, "mask64", [128, 4 * n // 128], BF16)
    s_bq = sb(nc, es, "s_bq", [128, 8])
    s_bk = sb(nc, es, "s_bk", [128, 8])
    s_bvb = SD[0:1, 0:512].bitcast(BF16)
    s_denb = s_trow[:].bitcast(BF16)[0:1, 0:512]
    s_phik2 = [sb(nc, es, f"s_phik{i}", [128, 256], BF16) for i in range(2)]
    s_nmx4 = sb(nc, es, "s_nmx4", [128, 4])
    s_nmx2 = [sb(nc, es, f"s_nmx{i}", [128, 1]) for i in range(2)]
    s_phiqT = [SEb[:, 0:512], SEb[:, 512:1024]]
    s_kvs = sb(nc, es, "s_kvs", [128, 132], BF16)
    s_bo = sb(nc, es, "s_bo", [128, NC])
    s_bin = sb(nc, es, "s_bin", [64, 1])
    s_w1 = sb(nc, es, "s_w1", [64, 16])
    s_b1 = sb(nc, es, "s_b1", [64, 16])
    s_w2 = sb(nc, es, "s_w2", [64, 16])
    s_b2 = sb(nc, es, "s_b2", [64, 1])
    s_bout = sb(nc, es, "s_bout", [128, NC])
    s_bv = s_p56[0:1, 0:D]
    s_exp2 = [s_p56[:, 512:768], s_p56[:, 768:1024]]
    s_den = s_p56[0:1, 0:512]
    s_dbc = s_p56[:, 0:512]
    s_sbc = SE[:, 0:1024]
    s_s = SC[0:64, 0:512]
    s_hi = SD[0:64, 0:512]
    s_g = SC[0:64, 512:1024]
    s_winb = SCb[:, 8192:8704]
    s_woutb = SCb[0:64, 8704:9728]
    s_gb = SCb[0:64, 9728:10240]
    P01 = ps(nc, es, "P01", [128, 1024])
    P23 = ps(nc, es, "P23", [128, 1024])
    B = [P01[:, 0:512], P01[:, 512:1024], P23[:, 0:512], P23[:, 512:1024]]
    B += [ps(nc, es, f"B{i}", [128, 512]) for i in range(4, 7)]
    B7 = ps(nc, es, "B7", [128, 1024], BF16)
    PP = [P01, P23]

    war = {}

    P.sd(lambda s: s.dma_start(
        out=SA[:].rearrange("p (a m) -> p a m", a=NC), in_=fmview(XT, n)),
        ch="x")
    P.sd(lambda s: s.dma_start(
        out=SB[:].rearrange("p (a m) -> p a m", a=NC), in_=fmview(XRT, n)),
        ch="x")
    lxall = P.total("syd", "x")
    for ap_, tile in ((RWC, s_rwc), (RB, s_rb), (IDB, s_idb), (BQ, s_bq),
                      (BK, s_bk), (BO, s_bo), (BIN, s_bin), (W1, s_w1),
                      (B1, s_b1), (W2, s_w2), (B2_, s_b2), (BOUT, s_bout)):
        P.sd(lambda s, _a=ap_, _t=tile: s.dma_start(out=_t[:], in_=_a))
    lall = P.total("syd", "ld")
    m2 = P.v(lambda v: v.memset(s_onec[:], 1.0))
    m2b = P.v(lambda v: v.memset(s_onecb[:], 1.0))
    m3 = P.v(lambda v: v.memset(s_ones2[:], 1.0))
    m1 = m3
    m3b = P.v(lambda v: v.memset(s_onerb[:], 1.0))
    m4 = P.v(lambda v: v.memset(s_eps[:], LN_EPS))
    m4b = P.v(lambda v: v.memset(s_eps8[:], RMS_EPS))
    m4c = P.v(lambda v: v.memset(s_epsphi[:], PHI_EPS))

    s_rwo = SD[0:1, 0:n]
    rwevs = []
    for i in range(nsub):
        pb = B[i % 2]
        mmtok = None
        for j in range(NC):
            mmtok = P.t(lambda t, _i=i, _j=j, _pb=pb: t.matmul(
                _pb[0:1, 0:512], lhsT=s_rwc[:, _j:_j + 1],
                rhs=SA[:, _j * n + _i * 512:_j * n + _i * 512 + 512],
                start=(_j == 0), stop=(_j == NC - 1)),
                [lxall, lall, war.get(("rwp", i % 2))])
        ev = P.v(lambda v, _i=i, _pb=pb: v.tensor_scalar(
            s_rwo[:, _i * 512:_i * 512 + 512], _pb[0:1, 0:512], s_rb[:],
            None, op0=ALU.add), [mmtok, lall])
        war[("rwp", i % 2)] = ev
        rwevs.append(ev)
    P.gd(lambda g: g.dma_start(out=cc1_in, in_=s_rwo[:]), rwevs, ch="cc")
    cc1 = P.col(lambda g: g.collective_compute(
        "AllGather", ALU.bypass,
        replica_groups=[[0, 1, 2, 3], [4, 5, 6, 7]],
        ins=[cc1_in.opt()], outs=[cc1_out.opt()]),
        [P.total("gpd", "cc")])
    P.gd(lambda g: g.dma_start(
        out=s_rwf[:], in_=cc1_out.rearrange("r (p f) -> (r p) f", p=32)),
        [cc1], ch="g1")
    lrwf = P.total("gpd", "g1")

    red_toks = []
    for pi in range(8):
        reg = pi % 2
        ld = P.ad(lambda s, _pi=pi, _reg=reg: s.dma_start(
            out=AW[:, _reg * 1024:(_reg + 1) * 1024],
            in_=(WUPS[_pi:_pi + 1].rearrange("o p m -> (o p) m")
                 if _pi < 4 else
                 WDNS[:, (_pi - 4) * 1024:(_pi - 3) * 1024])),
            [red_toks[pi - 2] if pi >= 2 else None], ch=f"ws{reg}")
        rt = P.v(lambda v, _pi=pi, _reg=reg: v.tensor_reduce(
            s_acc[:, _pi:_pi + 1], AW[:, _reg * 1024:(_reg + 1) * 1024],
            axis=AX.X, op=ALU.add, apply_absolute_value=True),
            [P.total("scd", f"ws{reg}")])
        red_toks.append(rt)
    r1 = P.v(lambda v: v.tensor_reduce(
        s_red[:, 0:1], s_acc[:, 0:4], axis=AX.X, op=ALU.add), red_toks)
    r2 = P.v(lambda v: v.tensor_reduce(
        s_red[:, 1:2], s_acc[:, 4:8], axis=AX.X, op=ALU.add),
        red_toks + [r1])
    apm = P.t(lambda t: t.matmul(B[2][0:1, 0:2], lhsT=s_onec[:],
                                 rhs=s_red[:, 0:2], start=True, stop=True),
              [r2, m2])
    stv = P.v(lambda v: v.tensor_copy(s_th[:, 0:2], B[2][0:1, 0:2]), [apm])
    P.gd(lambda g: g.dma_start(out=ST_in, in_=s_th[:, 0:2]), [stv], ch="cs")
    ccS = P.col(lambda g: g.collective_compute(
        "AllGather", ALU.bypass,
        replica_groups=[[0, 1, 2, 3, 4, 5, 6, 7]],
        ins=[ST_in.opt()], outs=[ST_out.opt()]),
        [P.total("gpd", "cs")])
    P.gd(lambda g: g.dma_start(
        out=s_aps[:], in_=ST_out.rearrange("r c -> () (r c)")), [ccS],
        ch="g2")
    laps = P.total("gpd", "g2")
    a1 = P.v(lambda v: v.tensor_reduce(
        s_th[:, 0:1], s_aps[0:1, 0:16:2], axis=AX.X, op=ALU.add),
        [laps, stv])
    a2 = P.v(lambda v: v.tensor_reduce(
        s_th[:, 1:2], s_aps[0:1, 1:16:2], axis=AX.X, op=ALU.add),
        [laps, a1])
    a3 = P.v(lambda v: v.tensor_scalar(
        s_th[:, 0:2], s_th[:, 0:2], 0.5 / (D * DFF), None, op0=ALU.mult),
        [a1, a2])
    bb = P.t(lambda t: t.matmul(B[2][:, 2:4], lhsT=s_oner[:],
                                rhs=s_th[:, 0:2], start=True, stop=True),
             [a3, m1, stv])
    thc = P.v(lambda v: v.tensor_copy(s_thc[:], B[2][:, 2:4]), [bb])

    bis_last = emit_bisect(P, s_rwf, s_ones2, s_oner, B[3], B[4], st,
                           lrwf, m1, niter=NITER_BISECT)

    scr = {0: SC[:, 0:1024], 1: SC[:, 1024:2048]}
    prev_sg = {}
    prev_ld = {}
    for pi in range(8):
        reg = pi % 2
        slot = pi % 2
        which = 0 if pi < 4 else 1
        ld = P.ad(lambda s, _pi=pi, _reg=reg: s.dma_start(
            out=AW[:, _reg * 1024:(_reg + 1) * 1024],
            in_=(WUPS[_pi:_pi + 1].rearrange("o p m -> (o p) m")
                 if _pi < 4 else
                 WDNS[:, (_pi - 4) * 1024:(_pi - 3) * 1024])),
            [red_toks[6 + reg], prev_ld.get(reg)], ch=f"wt{reg}")
        ldt = P.total("scd", f"wt{reg}")
        ab = P.s(lambda s, _reg=reg: s.activation(
            scr[_reg], AW[:, _reg * 1024:(_reg + 1) * 1024], AF.Abs),
            [ldt, prev_sg.get(("mk", reg))])
        mk = P.v(lambda v, _reg=reg, _w=which: v.tensor_scalar(
            scr[_reg], scr[_reg],
            s_thc[:, _w:_w + 1], None, op0=ALU.is_gt), [thc, ab])
        wm = P.v(lambda v, _reg=reg: v.tensor_tensor(
            scr[_reg], scr[_reg],
            AW[:, _reg * 1024:(_reg + 1) * 1024], ALU.mult), [mk])
        prev_ld[reg] = wm
        sg = P.s(lambda s, _slot=slot, _reg=reg: s.activation(
            PJT[:, _slot * 1024:(_slot + 1) * 1024],
            scr[_reg], AF.Sign),
            [wm, prev_sg.get(("st", slot))])
        prev_sg[("mk", reg)] = sg
        stok = P.ad(lambda s, _pi=pi, _slot=slot: s.dma_start(
            out=((WUP_in[0:1, _pi * 131072:(_pi + 1) * 131072]
                  if _pi < 4 else
                  WDN_in[0:1, (_pi - 4) * 131072:(_pi - 3) * 131072])
                 .rearrange("o (p m) -> (o p) m", p=128)),
            in_=PJT[:, _slot * 1024:(_slot + 1) * 1024]),
            [sg], ch=f"ts{slot}")
        prev_sg[("st", slot)] = stok
        if pi == 3:
            ccWu = P.col(lambda g: g.collective_compute(
                "AllGather", ALU.bypass,
                replica_groups=[[0, 1, 2, 3, 4, 5, 6, 7]],
                ins=[WUP_in.opt()], outs=[WUP_out.opt()]),
                [P.total("scd", "ts0"), P.total("scd", "ts1")])
    ccWd = P.col(lambda g: g.collective_compute(
        "AllGather", ALU.bypass,
        replica_groups=[[0, 1, 2, 3, 4, 5, 6, 7]],
        ins=[WDN_in.opt()], outs=[WDN_out.opt()]),
        [P.total("scd", "ts0"), P.total("scd", "ts1")])

    tern_done = prev_sg[("st", 0)], prev_sg[("st", 1)]
    pjevs = []
    pjld = None
    for half in range(2):
        l1 = P.sd(lambda s, _h=half: s.dma_start(
            out=SE[0:64, 0:2048], in_=PJ[:, _h * 2048:(_h + 1) * 2048]),
            [pjld], ch="pj")
        l2 = P.sd(lambda s, _h=half: s.dma_start(
            out=SE[64:128, 0:2048], in_=PJ[:, _h * 2048:(_h + 1) * 2048]),
            ch="pj")
        lt = P.total("syd", "pj")
        ev = P.v(lambda v, _h=half: v.tensor_scalar(
            PJT[:, _h * 2048:(_h + 1) * 2048], SE[:, 0:2048],
            DATA_NORM, None, op0=ALU.mult),
            [lt, tern_done[0], tern_done[1]])
        pjevs.append(ev)
        pjld = ev
    pj = pjevs[-1]
    P.sd(lambda s: s.dma_start(out=s_bv, in_=BV), ch="bv")
    lbv = P.total("syd", "bv")

    x1toks = {}
    xbtoks = {}
    prev_mb = None
    prev_b0 = P.total("gpd", "cc")
    ln1 = {"sq": prev_sg[("mk", 0)], "s1": None, "s2": None}
    rowrd = {}
    ytoks_all = {}
    for i in range(nsub):
        mr = P.v(lambda v, _i=i: v.tensor_scalar(
            s_mrow[:], s_rwo[:, _i * 512:_i * 512 + 512],
            st["t_lo"][0:1, 0:1], None, op0=ALU.is_ge),
            [bis_last, rwevs[i], prev_mb])
        mb = P.t(lambda t: t.matmul(
            B[5][:, 0:512], lhsT=s_oner[:], rhs=s_mrow[:],
            start=True, stop=True), [mr, m1, war.get("B5rd")])
        prev_mb = mb
        ytoks = []
        for j in range(NC):
            off = j * n + i * 512
            ym = P.v(lambda v, _o=off: v.tensor_tensor(
                SB[:, _o:_o + 512], SB[:, _o:_o + 512], B[5][:, 0:512],
                ALU.mult), [mb, lxall])
            ya = P.v(lambda v, _o=off: v.tensor_tensor(
                SB[:, _o:_o + 512], SB[:, _o:_o + 512], SA[:, _o:_o + 512],
                ALU.add), [ym])
            ytoks.append(ya)
        war["B5rd"] = ytoks[-1]
        ytoks_all[i] = ytoks
    for i in range(nsub):
        ytoks = ytoks_all[i]
        s1tok = None
        s2tok = None
        sqprev = ln1.get("sq")
        for j in range(NC):
            off = j * n + i * 512
            s1tok = P.t(lambda t, _o=off, _j=j: t.matmul(
                B[6][0:1, 0:512], lhsT=s_onec[:], rhs=SB[:, _o:_o + 512],
                start=(_j == 0), stop=(_j == NC - 1)),
                [ytoks[j], m2, ln1.get("s1")])
            sq = P.s(lambda s, _o=off, _j=j: s.activation(
                s_tmpb[:, (_j % 2) * 512:(_j % 2) * 512 + 512],
                SB[:, _o:_o + 512], AF.Square), [ytoks[j], sqprev])
            s2tok = P.t(lambda t, _j=j: t.matmul(
                B[4][0:1, 0:512], lhsT=s_onecb[:],
                rhs=s_tmpb[:, (_j % 2) * 512:(_j % 2) * 512 + 512],
                start=(_j == 0), stop=(_j == NC - 1)),
                [sq, m2b, ln1.get("s2")])
            sqprev = s2tok
        ln1["sq"] = sqprev
        f1 = P.v(lambda v: v.tensor_scalar(
            s_m[:], B[6][0:1, 0:512], 1.0 / D, None, op0=ALU.mult),
            [s1tok, rowrd.get("m")])
        f2 = P.v(lambda v: v.tensor_scalar(
            s_r[:], B[4][0:1, 0:512], 1.0 / D, None, op0=ALU.mult),
            [s2tok, rowrd.get("r")])
        ln1["s1"] = f1
        ln1["s2"] = f2
        f3 = P.v(lambda v: v.tensor_tensor(
            s_trow[:], s_m[:], s_m[:], ALU.mult), [f1, rowrd.get("trow")])
        f4 = P.v(lambda v: v.tensor_tensor(
            s_r[:], s_r[:], s_trow[:], ALU.subtract), [f3, f2])
        rowrd["trow"] = f4
        f4b = P.s(lambda s: s.activation(s_vv[:], s_r[:], AF.Sqrt),
                  [f4, rowrd.get("vv")])
        f5 = P.s(lambda s: s.activation(
            s_r[:], s_r[:], AF.Ln, bias=s_eps[:]),
            [f4b, (m4 if i == 0 else None)])
        f6 = P.s(lambda s: s.activation(
            s_r[:], s_r[:], AF.Exp, scale=-0.5), [f5])
        g1 = P.v(lambda v: v.tensor_tensor(
            s_vv[:], s_vv[:], s_r[:], ALU.mult), [f6, f4b])
        g2 = P.s(lambda s: s.activation(
            s_vv[:], s_vv[:], AF.Identity, bias=s_eps8[:]),
            [g1, (m4b if i == 0 else None)])
        g3 = P.v(lambda v: v.reciprocal(s_vv[:], s_vv[:]), [g2])
        g4 = P.v(lambda v: v.tensor_tensor(
            s_c2[:], s_vv[:], s_r[:], ALU.mult), [g3, rowrd.get("c2")])
        b_m = P.t(lambda t: t.matmul(
            B[0][:, 0:512], lhsT=s_oner[:], rhs=s_m[:],
            start=True, stop=True), [f1, m1, prev_b0, rowrd.get("B0")])
        b_r = P.t(lambda t: t.matmul(
            B[1][:, 0:512], lhsT=s_oner[:], rhs=s_r[:],
            start=True, stop=True), [f6, prev_b0, rowrd.get("B1")])
        b_c = P.t(lambda t: t.matmul(
            B[2][:, 0:512], lhsT=s_oner[:], rhs=s_c2[:],
            start=True, stop=True), [g4, thc, rowrd.get("B2")])
        rowrd["m"] = b_m
        rowrd["r"] = b_r
        rowrd["c2"] = b_c
        rowrd["vv"] = g4
        t2l = t3l = None
        for j in range(NC):
            off = j * n + i * 512
            t1 = P.v(lambda v, _o=off: v.tensor_tensor(
                SB[:, _o:_o + 512], SB[:, _o:_o + 512], B[0][:, 0:512],
                ALU.subtract), [b_m, ytoks[j]])
            t2 = P.v(lambda v, _o=off: v.tensor_tensor(
                SA[:, _o:_o + 512], SB[:, _o:_o + 512], B[1][:, 0:512],
                ALU.mult), [t1, b_r])
            t3 = P.v(lambda v, _o=off, _j=j, _i=i: v.tensor_tensor(
                SCb[:, _j * 2048 + _i * 512:_j * 2048 + _i * 512 + 512],
                SB[:, _o:_o + 512], B[2][:, 0:512], ALU.mult),
                [t1, b_c,
                 (prev_sg[("mk", j)] if i == 0 and j < 2 else None)])
            x1toks[(i, j)] = t2
            xbtoks[(i, j)] = t3
            t2l, t3l = t2, t3
        rowrd["B0"] = t3l
        rowrd["B1"] = t2l
        rowrd["B2"] = t3l
    p3done = t3l
    bvc = P.v(lambda v: v.tensor_copy(s_bvb[:], s_bv), [lbv, p3done])
    snap1 = None
    if dump:
        P.sd(lambda s: s.dma_start(out=XD1, in_=SA[:]), [p3done], ch="dbg")
        snap1 = P.sd(lambda s: s.dma_start(
            out=XDH, in_=SCb[:, 0:NC * n]), [p3done], ch="dbg")
        P.sd(lambda s: s.dma_start(out=XDW, in_=WUP_out), [ccWu], ch="dbg")
        snap1 = P.sd(lambda s: s.dma_start(out=XDW2, in_=WDN_out),
                     [ccWd, snap1], ch="dbg")
        snap1 = P.total("syd", "dbg")

    awfree = {0: None, 1: None, 2: None, 3: None}
    fw = {"B01": rowrd["B1"], "B23": rowrd["B2"],
          "B4": ln1["s2"], "B5": war["B5rd"], "B6": ln1["s1"],
          "sq": ln1["sq"], "tmprd": (snap1 or ln1["sq"])}
    awfree[0] = awfree[1] = prev_ld[0]
    awfree[2] = awfree[3] = prev_ld[1]
    hwar = p3done
    ln2war = {"eps_dep": None, "sq": ln1["sq"], "trow": rowrd["trow"],
              "s1": None, "s2": None, "bc": None}
    x2toks = {}
    dnfree = {}
    sbcrd = {}
    for hh in range(2):
        tok0 = hh * H
        geltoks = {}
        stat_emit = []
        statA_last = statB_last = None
        upA_war = {0: fw["B01"], 1: fw["B23"]}
        for f in range(32):
            u = f % 4
            c, k = f // 4, f % 4
            ld = P.sd(lambda s, _u=u, _c=c, _k=k: s.dma_start(
                out=AWb[:, _u * 1024:(_u + 1) * 1024],
                in_=WUP_out[_c:_c + 1,
                            _k * 131072:(_k + 1) * 131072].rearrange(
                    "o (p m) -> (o p) m", p=128)),
                [ccWu if (hh == 0 and f < 4) else None, awfree[u],
                 (ln2war["bc"] if hh == 1 and f < 4 else None),
                 (ln2war["sq"] if hh == 1 and f < 4 else None)],
                ch=f"wu{u}")
            ldt = P.total("syd", f"wu{u}")
            pA = B[0 + 2 * (f % 2)]
            pB = B[1 + 2 * (f % 2)]
            mmA = mmB = None
            for j in range(NC):
                mmA = P.t(lambda t, _u=u, _j=j, _pA=pA, _t0=tok0: t.matmul(
                    _pA[:, 0:512],
                    lhsT=AWb[:, _u * 1024 + _j * 128:_u * 1024 + (_j + 1) * 128],
                    rhs=SCb[:, _j * 2048 + _t0:_j * 2048 + _t0 + 512],
                    start=(_j == 0), stop=(_j == NC - 1)),
                    [ldt, xbtoks[(2 * hh, j)], upA_war[f % 2]])
                mmB = P.t(lambda t, _u=u, _j=j, _pB=pB, _t0=tok0: t.matmul(
                    _pB[:, 0:512],
                    lhsT=AWb[:, _u * 1024 + _j * 128:_u * 1024 + (_j + 1) * 128],
                    rhs=SCb[:, _j * 2048 + _t0 + 512:_j * 2048 + _t0 + 1024],
                    start=(_j == 0), stop=(_j == NC - 1)),
                    [xbtoks[(2 * hh + 1, j)], mmA])
            awfree[u] = mmB
            if dump and hh == 1 and f == 0:
                P.sd(lambda s: s.dma_start(
                    out=XDAW, in_=AWb[:, 0:1024]), [mmB], ch="dbg")
                awfree[u] = P.sd(lambda s: s.dma_start(
                    out=XDH3, in_=SCb[:, 0:NC * n]), [mmB], ch="dbg")
            if stat_emit:
                statA_last, statB_last = stat_emit.pop()()
            gA = P.s(lambda s, _f=f, _pA=pA: s.activation(
                SBb[:, _f * 1024:_f * 1024 + 512], _pA[:, 0:512], AF.Gelu),
                [mmA, hwar])
            gB = P.s(lambda s, _f=f, _pB=pB: s.activation(
                SBb[:, _f * 1024 + 512:(_f + 1) * 1024], _pB[:, 0:512],
                AF.Gelu), [mmB])
            geltoks[f] = (gA, gB)
            upA_war[f % 2] = gB
            sqA = P.s(lambda s, _f=f: s.activation(
                s_tmpb[:, 0:512],
                SBb[:, _f * 1024:_f * 1024 + 512], AF.Square),
                [gA, fw["sq"], statA_last])
            sqB = P.s(lambda s, _f=f: s.activation(
                s_tmpb[:, 512:1024],
                SBb[:, _f * 1024 + 512:(_f + 1) * 1024], AF.Square),
                [sqA, statB_last])
            fw["sq"] = sqB

            def emit_stats(_f=f, _sqA=sqA, _sqB=sqB,
                           _sAp=statA_last, _sBp=statB_last,
                           _b4=fw["B4"], _b5=fw["B5"]):
                sA = P.t(lambda t: t.matmul(
                    B[4][0:1, 0:512], lhsT=s_onecb[:],
                    rhs=s_tmpb[:, 0:512],
                    start=(_f == 0), stop=(_f == 31)),
                    [_sqA, m2b, _sAp, (_b4 if _f == 0 else None)])
                sB = P.t(lambda t: t.matmul(
                    B[5][0:1, 0:512], lhsT=s_onecb[:],
                    rhs=s_tmpb[:, 512:1024],
                    start=(_f == 0), stop=(_f == 31)),
                    [_sqB, _sBp, (_b5 if _f == 0 else None)])
                return sA, sB
            stat_emit.append(emit_stats)
        statA_last, statB_last = stat_emit.pop()()
        fw["tmprd"] = statB_last
        fw["B01"] = upA_war[0]
        fw["B23"] = upA_war[1]
        hrA = P.v(lambda v: v.tensor_scalar(
            s_hr2[0:1, 0:512], B[4][0:1, 0:512], 1.0 / DFF, None,
            op0=ALU.mult), [statA_last, sbcrd.get("hr2"),
                            (prev_mb if hh == 0 else None)])
        hrB = P.v(lambda v: v.tensor_scalar(
            s_hr2[0:1, 512:1024], B[5][0:1, 0:512], 1.0 / DFF, None,
            op0=ALU.mult), [statB_last, hrA])
        h1 = P.s(lambda s: s.activation(s_hr2[:], s_hr2[:], AF.Sqrt),
                 [hrA, hrB])
        h2 = P.s(lambda s: s.activation(
            s_hr2[:], s_hr2[:], AF.Identity, bias=s_eps8[:]), [h1])
        h3 = P.v(lambda v: v.reciprocal(s_hr2[:], s_hr2[:]), [h2])
        b_sA = P.t(lambda t: t.matmul(
            B[4][:, 0:512], lhsT=s_oner[:], rhs=s_hr2[0:1, 0:512],
            start=True, stop=True), [h3, m1, hrA])
        b_sB = P.t(lambda t: t.matmul(
            B[5][:, 0:512], lhsT=s_oner[:], rhs=s_hr2[0:1, 512:1024],
            start=True, stop=True), [h3, hrB])
        cpA = P.v(lambda v: v.tensor_copy(SE[:, 0:512], B[4][:, 0:512]),
                  [b_sA, sbcrd.get("sbc"), (pj if hh == 0 else None)])
        cpB = P.v(lambda v: v.tensor_copy(SE[:, 512:1024], B[5][:, 0:512]),
                  [b_sB, cpA])
        fw["B4"] = cpA
        fw["B5"] = cpB
        sbcrd["hr2"] = b_sB
        ztoksA = {}
        ztoksB = {}
        for cj in range(8):
            ldq = {}
            for q in range(2):
                r = q
                ldq[q] = P.sd(lambda s, _cj=cj, _q=q, _r=r: s.dma_start(
                    out=AWb[:, _r * 2048:(_r + 1) * 2048].rearrange(
                        "p (j m) -> p j m", j=2),
                    in_=WDN_out[_cj:_cj + 1,
                                (2 * _q) * 131072:
                                (2 * _q + 2) * 131072].rearrange(
                        "o (j p m) -> (o p) j m", j=2, p=128, m=1024)),
                    [ccWd if (hh == 0 and cj == 0) else None,
                     awfree[2 * r], awfree[2 * r + 1],
                     dnfree.get((hh, cj - 1, r))],
                    ch=f"wd{r}")
            pA = B[0 + 2 * (cj % 2)]
            pB = B[1 + 2 * (cj % 2)]
            mmA = mmB = None
            for f in range(32):
                r = f // 16
                lt = P.total("syd", f"wd{r}") if f % 16 == 0 else None
                mmA = P.t(lambda t, _f=f, _r=r, _pA=pA, _lt=lt: t.matmul(
                    _pA[:, 0:512],
                    lhsT=AWb[:, _r * 2048 + (_f % 16) * 128:
                             _r * 2048 + (_f % 16 + 1) * 128],
                    rhs=SBb[:, _f * 1024:_f * 1024 + 512],
                    start=(_f == 0), stop=(_f == 31)),
                    [lt, geltoks[f][0],
                     (fw.get("B01" if cj % 2 == 0 else "B23")
                      if f == 0 else None)])
                mmB = P.t(lambda t, _f=f, _pB=pB: t.matmul(
                    _pB[:, 0:512],
                    lhsT=AWb[:, (_f // 16) * 2048 + (_f % 16) * 128:
                             (_f // 16) * 2048 + (_f % 16 + 1) * 128],
                    rhs=SBb[:, _f * 1024 + 512:(_f + 1) * 1024],
                    start=(_f == 0), stop=(_f == 31)),
                    [geltoks[f][1], mmA])
                if f == 15:
                    dnfree[(hh, cj, 0)] = mmB
                if f == 31:
                    dnfree[(hh, cj, 1)] = mmB
            tA = P.v(lambda v, _pA=pA: v.tensor_tensor(
                s_tmp[:, 0:512], _pA[:, 0:512], SE[:, 0:512],
                ALU.mult), [mmB, cpA, fw.get("tmp"), fw["tmprd"]])
            zA = P.v(lambda v, _cj=cj, _t0=tok0: v.tensor_tensor(
                SA[:, _cj * n + _t0:_cj * n + _t0 + 512],
                s_tmp[:, 0:512],
                SA[:, _cj * n + _t0:_cj * n + _t0 + 512], ALU.add), [tA])
            tB = P.v(lambda v, _pB=pB: v.tensor_tensor(
                s_tmp[:, 0:512], _pB[:, 0:512], SE[:, 512:1024],
                ALU.mult), [mmB, cpB, zA])
            zB = P.v(lambda v, _cj=cj, _t0=tok0: v.tensor_tensor(
                SA[:, _cj * n + _t0 + 512:_cj * n + _t0 + 1024],
                s_tmp[:, 0:512],
                SA[:, _cj * n + _t0 + 512:_cj * n + _t0 + 1024], ALU.add),
                [tB])
            fw["tmp"] = zB
            if cj % 2 == 0:
                fw["B01"] = zB
            else:
                fw["B23"] = zB
            ztoksA[cj] = zA
            ztoksB[cj] = zB
        sbcrd["sbc"] = fw["tmp"]
        ln2war["sq"] = ztoksB[7]
        ln2war["bc"] = ztoksB[7]
        for ts in range(2):
            gi = hh * 2 + ts
            ytl = [ztoksA[cj] if ts == 0 else ztoksB[cj] for cj in range(8)]
            ln2war["eps_dep"] = None
            outt = emit_ln(P, n, gi, 512, SA, SA, s_m, s_r, s_tmp, s_trow,
                           s_onec, s_oner, B[6][0:1, :], B[4][0:1, :],
                           B[5], B[2], ytl, ln2war, s_eps=s_eps,
                           s_onecb=s_onecb)
            x2toks[gi] = outt
        fw["sq"] = ln2war["sq"]
        fw["B4"] = ln2war["s2"]
        fw["B5"] = ln2war["bc"]
        fw["B6"] = ln2war["s1"]
        fw["B23"] = ln2war["bc"]
        fw["tmprd"] = ln2war["sq"]
        fw["tmp"] = ln2war["sq"]
        sbcrd["hr2"] = ln2war["s2"]
        sbcrd["sbc"] = ln2war["bc"]
        hwar = None
        if hh == 0:
            hwar = ln2war["bc"]
            upA_war = {0: ln2war["bc"], 1: ln2war["bc"]}
            for u in range(4):
                awfree[u] = dnfree[(0, 7, u // 2)]
    p4done = x2toks[3][-1]
    snap2 = None
    if dump:
        snap2 = P.sd(lambda s: s.dma_start(out=XD2, in_=SA[:]), [p4done],
                     ch="dbg")
        P.sd(lambda s: s.dma_start(out=XDHH, in_=SBb[:, 0:NFF * 1024]),
             [p4done], ch="dbg")
        P.sd(lambda s: s.dma_start(out=XDSB, in_=SE[:, 0:1024]),
             [p4done], ch="dbg")
        p4done = P.total("syd", "dbg")

    awf = [dnfree[(1, 7, 0)], dnfree[(1, 7, 1)]]
    for wi, (widx0, dst, sbias) in enumerate(((0, 0, s_bq), (2, KOFF, s_bk))):
        for half in range(2):
            wln = P.gd(lambda g, _w=widx0 + half: g.dma_start(
                out=AWb[:, 0:NC * 512].rearrange("p (a m) -> p a m", a=NC),
                in_=WQKVC[_w:_w + 1].rearrange("o p a m -> (o p) a m")),
                [war.get("awfree2"), p4done] + awf, ch="wqk")
            awf = []
            wlnt = P.total("gpd", "wqk")
            for i in range(nsub):
                casts = []
                for j in range(NC):
                    off = j * n + i * 512
                    casts.append(P.v(lambda v, _o=off, _j=j: v.tensor_copy(
                        SEb[:, _j * 512:(_j + 1) * 512], SA[:, _o:_o + 512]),
                        [x2toks[i][j], war.get("x2b")]))
                for hq in range(4):
                    hp = half * 4 + hq
                    mmt = None
                    for j in range(NC):
                        mmt = P.t(lambda t, _j=j, _hq=hq: t.matmul(
                            B[_hq % 2][:, 0:512],
                            lhsT=AWb[:, _j * 512 + _hq * 128:
                                     _j * 512 + _hq * 128 + 128],
                            rhs=SEb[:, _j * 512:(_j + 1) * 512],
                            start=(_j == 0), stop=(_j == NC - 1)),
                            [wlnt, casts[j], war.get(("pqk", hq % 2)),
                             (fw["B01"] if wi == 0 and half == 0 and i == 0
                              and hq < 2 else None)])
                    ev = P.s(lambda s, _i=i, _hp=hp, _hq=hq, _d=dst,
                             _sb=sbias: s.activation(
                        SBb[:, _d + _hp * n + _i * 512:
                            _d + _hp * n + _i * 512 + 512],
                        B[_hq % 2][:, 0:512], AF.Identity,
                        bias=_sb[:, _hp:_hp + 1]), [mmt, lall, p4done])
                    war[("pqk", hq % 2)] = ev
                    war[("qk", wi, hp)] = ev
                war["x2b"] = war[("pqk", 1)]
            war["awfree2"] = war[("pqk", 1)]
    for half in range(2):
        wlv = P.gd(lambda g, _h=half: g.dma_start(
            out=AWb[:, 0:NC * 512].rearrange("p (a m) -> p a m", a=NC),
            in_=WQKVC[4 + _h:5 + _h].rearrange("o p a m -> (o p) a m")), [war["awfree2"]], ch="wv")
        wlvt = P.total("gpd", "wv")
        for i in range(nsub):
            casts = []
            for j in range(NC):
                off = j * n + i * 512
                casts.append(P.v(lambda v, _o=off, _j=j: v.tensor_copy(
                    SEb[:, _j * 512:(_j + 1) * 512], SA[:, _o:_o + 512]),
                    [x2toks[i][j], war.get("x2b")]))
            for r in range(4):
                tt = i * 4 + r
                mmt = None
                for j in range(NC):
                    mmt = P.t(lambda t, _j=j, _r=r: t.matmul(
                        B[2][:, 0:512],
                        lhsT=SEb[:, _j * 512 + _r * 128:
                                 _j * 512 + _r * 128 + 128],
                        rhs=AWb[:, _j * 512:(_j + 1) * 512],
                        start=(_j == 0), stop=False),
                        [wlvt, casts[j], war.get("pv"),
                         (fw["B23"] if half == 0 and i == 0 and r == 0
                          else None)])
                mmt = P.t(lambda t, _h=half: t.matmul(
                    B[2][:, 0:512], lhsT=s_onerb[:],
                    rhs=s_bvb[0:1, _h * 512:(_h + 1) * 512],
                    start=False, stop=True), [mmt, m3b, bvc])
                vx = P.v(lambda v, _tt=tt, _h=half: v.tensor_copy(
                    SCb[:, _tt * 1024 + _h * 512:
                        _tt * 1024 + (_h + 1) * 512],
                    B[2][:, 0:512]), [mmt, war.get("vxw"), p4done])
                war["pv"] = vx
                war["vxw"] = vx
                war[("vx", tt, half)] = vx
            war["x2b"] = war["pv"]
        war["awfree2"] = war["pv"]
    LN16 = float(np.log(16.0))
    phs = [B[0][:, 0:256], B[1][:, 0:256], B[4][:, 0:256], B[5][:, 0:256]]
    exs = [s_p56[:, g * 256:(g + 1) * 256] for g in range(4)]
    pks4 = [SEb[:, t * 256:(t + 1) * 256] for t in range(4)]
    kvq = []
    kvlast = {}
    for h in range(HEADS):
        hp, hb = h // 2, (h % 2) * 64
        vhalf, hv = h // 8, h % 8
        for tt in range(ntile):
            g = tt % 4
            rr = tt * 128
            xp = P.t(lambda t, _hp=hp, _hb=hb, _rr=rr, _h=h, _g=g:
                     t.matmul(
                phs[_g],
                lhsT=SBb[_hb:_hb + 64,
                         KOFF + _hp * n + _rr:KOFF + _hp * n + _rr + 128],
                rhs=s_pj[_hb:_hb + 64, _h * 256:(_h + 1) * 256],
                start=True, stop=True),
                [war[("qk", 1, hp)], pj, war.get(("xpfree", g)),
                 (war[("qk", 1, 3)] if h == 0 and tt < 4 else None),
                 (ln2war["bc"] if h == 0 and tt < 4 else None)])
            rm = P.v(lambda v, _g=g: v.reduce_max(
                s_nmx4[:, _g:_g + 1], phs[_g], axis=AX.X,
                negate=True), [xp, war.get(("nmxrd", g))])
            ba = P.v(lambda v, _g=g: v.tensor_scalar(
                s_nmx4[:, _g:_g + 1], s_nmx4[:, _g:_g + 1], -LN16, None,
                op0=ALU.add), [rm])
            ex = P.s(lambda s, _g=g: s.activation(
                exs[_g], phs[_g], AF.Exp, bias=s_nmx4[:, _g:_g + 1]),
                [ba, war.get(("exrd", g))])
            war[("xpfree", g)] = ex
            war[("nmxrd", g)] = ex
            ph = P.v(lambda v, _g=g: v.tensor_scalar(
                pks4[_g], exs[_g], PHI_EPS / 16.0, None,
                op0=ALU.add), [ex, war.get(("pkrd", g))])
            war[("exrd", g)] = ph

            def mk_kv(_h=h, _tt=tt, _g=g, _ph=ph, _vh=vhalf, _hv=hv):
                vsl = SCb[:, _tt * 1024 + _vh * 512 + _hv * 64:
                          _tt * 1024 + _vh * 512 + _hv * 64 + 64]
                kv0 = P.t(lambda t: t.matmul(
                    B[6][:, 0:64], lhsT=pks4[_g][:, 0:128], rhs=vsl,
                    start=(_tt == 0), stop=False),
                    [_ph, war[("vx", _tt, _vh)],
                     (war.get("kv0ev") if _tt == 0 else None),
                     (ln2war["s1"] if _h == 0 and _tt == 0 else None)])
                kv0 = P.t(lambda t: t.matmul(
                    B[6][:, 64:65], lhsT=pks4[_g][:, 0:128], rhs=s_onecb[:],
                    start=False, stop=(_tt == ntile - 1)), [kv0, m2b])
                kv1 = P.t(lambda t: t.matmul(
                    B[3][:, 0:64], lhsT=pks4[_g][:, 128:256], rhs=vsl,
                    start=(_tt == 0), stop=False),
                    [_ph, war[("vx", _tt, _vh)],
                     (war.get("kv1ev") if _tt == 0 else None)])
                kv1 = P.t(lambda t: t.matmul(
                    B[3][:, 64:65], lhsT=pks4[_g][:, 128:256],
                    rhs=s_onecb[:],
                    start=False, stop=(_tt == ntile - 1)), [kv1, m2b])
                war[("pkrd", _g)] = kv1
                if _tt == ntile - 1:
                    kvlast[_h] = (kv0, kv1)
            kvq.append(mk_kv)
            if tt == ntile - 1:
                def mk_e(_h=h):
                    e1 = P.v(lambda v: v.tensor_copy(
                        SD[:, _h * 65:(_h + 1) * 65], B[6][:, 0:65]),
                        [kvlast[_h][0], kvlast[_h][1],
                         (p4done if _h == 0 else None),
                         (prev_mb if _h == 0 else None)])
                    e2 = P.v(lambda v: v.tensor_copy(
                        SD[:, 1040 + _h * 65:1040 + (_h + 1) * 65],
                        B[3][:, 0:65]), [e1])
                    war["kv0ev"] = e1
                    war["kv1ev"] = e2
                    war[("kvh", _h)] = e2
                kvq.append(mk_e)
            while len(kvq) > 2:
                kvq.pop(0)()
    while kvq:
        kvq.pop(0)()

    dkv = P.gd(lambda gg: gg.dma_start(out=kv_in, in_=SD[:, 0:2080]),
               [war[("kvh", hh)] for hh in range(HEADS)], ch="kvio")
    cc3 = P.col(lambda gg: gg.collective_compute(
        "AllReduce", ALU.add, replica_groups=[[0, 1, 2, 3], [4, 5, 6, 7]],
        ins=[kv_in.opt()], outs=[kv_out.opt()]), [P.total("gpd", "kvio")])
    P.gd(lambda gg: gg.dma_start(out=SD[:, 0:2080], in_=kv_out),
         [cc3], ch="kvb")
    lkvb = P.total("gpd", "kvb")

    if dump:
        P.sd(lambda s: s.dma_start(out=XDKV, in_=kv_out), [cc3], ch="dbg")
    pw = {"eps_dep": None, "sq": ln2war["sq"], "trow": ln2war["trow"]}
    if dump:
        for cj in range(NC):
            pw[("zr", cj)] = snap2
    war["pTread"] = war["x2b"]
    x3toks = {}
    for i in range(nsub):
        ndtoks = []
        for h in range(HEADS):
            hp, hb = h // 2, (h % 2) * 64
            kc = P.v(lambda v, _h=h: v.tensor_copy(
                s_kvs[:, 0:65], SD[:, _h * 65:(_h + 1) * 65]),
                [lkvb, war.get("kvsread")])
            kc2 = P.v(lambda v, _h=h: v.tensor_copy(
                s_kvs[:, 65:130],
                SD[:, 1040 + _h * 65:1040 + (_h + 1) * 65]), [kc])
            trtoks = []
            for r in range(4):
                n0 = i * 512 + r * 128
                xq = P.t(lambda t, _hp=hp, _hb=hb, _n0=n0, _r=r, _h=h:
                         t.matmul(
                    B[4 + _r % 2][:, 0:256],
                    lhsT=SBb[_hb:_hb + 64,
                             _hp * n + _n0:_hp * n + _n0 + 128],
                    rhs=s_pj[_hb:_hb + 64, _h * 256:(_h + 1) * 256],
                    start=True, stop=True), [pj, war[("qk", 0, hp)],
                                             war.get(("xpk", r % 2))])
                par = r % 2
                rm = P.v(lambda v, _r=r, _p=par: v.reduce_max(
                    s_nmx2[_p][:], B[4 + _r % 2][:, 0:256], axis=AX.X,
                    negate=True), [xq, war.get(("nmxread", par))])
                ex = P.s(lambda s, _r=r, _p=par: s.activation(
                    s_exp2[_p][:], B[4 + _r % 2][:, 0:256], AF.Exp,
                    bias=s_nmx2[_p][:]), [rm, war.get(("exread", par))])
                war[("xpk", r % 2)] = ex
                war[("nmxread", par)] = ex
                phq = P.v(lambda v, _p=par: v.tensor_scalar(
                    s_phik2[_p][:], s_exp2[_p][:], 1.0 / 16.0,
                    PHI_EPS / 16.0, op0=ALU.mult, op1=ALU.add),
                    [ex, war.get(("phikread", par))])
                war[("exread", par)] = phq
                t1 = P.t(lambda t, _p=par: t.transpose(
                    B7[:, _p * 256:_p * 256 + 128], s_phik2[_p][:, 0:128],
                    s_idb[:]), [phq, lall, war.get(("trread", par))])
                t2 = P.t(lambda t, _p=par: t.transpose(
                    B7[:, _p * 256 + 128:_p * 256 + 256],
                    s_phik2[_p][:, 128:256], s_idb[:]), [t1])
                c1t = P.v(lambda v, _r=r, _p=par: v.tensor_copy(
                    s_phiqT[0][:, _r * 128:(_r + 1) * 128],
                    B7[:, _p * 256:_p * 256 + 128]), [t2, war.get("pTread")])
                c2t = P.v(lambda v, _r=r, _p=par: v.tensor_copy(
                    s_phiqT[1][:, _r * 128:(_r + 1) * 128],
                    B7[:, _p * 256 + 128:_p * 256 + 256]), [c1t])
                war[("phikread", par)] = c2t
                war[("trread", par)] = c2t
                trtoks.append(c2t)
            nm = P.t(lambda t: t.matmul(
                B[0][0:65, 0:512], lhsT=s_kvs[:, 0:65],
                rhs=s_phiqT[0][:], start=True, stop=False),
                trtoks + [kc2, war.get("numread")])
            nm = P.t(lambda t: t.matmul(
                B[0][0:65, 0:512],
                lhsT=s_kvs[:, 65:130],
                rhs=s_phiqT[1][:], start=False, stop=True), [nm])
            war["pTread"] = nm
            war["kvsread"] = nm
            d1 = P.s(lambda s: s.activation(
                s_den[:], B[0][64:65, 0:512], AF.Ln, bias=s_epsphi[:]),
                [nm, m4c, war.get("denread")])
            d2 = P.s(lambda s: s.activation(
                s_den[:], s_den[:], AF.Exp, scale=-1.0), [d1])
            dcb = P.v(lambda v: v.tensor_copy(s_denb[:], s_den[:]),
                      [d2, war.get("denbrd")])
            db = P.t(lambda t: t.matmul(
                B[1][:, 0:512], lhsT=s_onerb[:], rhs=s_denb[:],
                start=True, stop=True), [dcb, m3b, war.get("dbcread")])
            dc = P.v(lambda v: v.tensor_copy(s_dbc[:], B[1][:, 0:512]),
                     [db])
            war["denread"] = dc
            war["denbrd"] = db
            nd = P.v(lambda v, _hp=hp, _hb=hb, _i=i: v.tensor_tensor(
                SCb[_hb:_hb + 64,
                    _hp * n + _i * 512:_hp * n + _i * 512 + 512],
                B[0][0:64, 0:512], s_dbc[_hb:_hb + 64, :], ALU.mult),
                [dc, nm, war.get("ndw")])
            war["numread"] = nd
            war["dbcread"] = nd
            ndtoks.append(nd)
        ztoks = []
        for whalf in range(2):
            wlo = P.gd(lambda g, _h=whalf: g.dma_start(
                out=AWb[:, 0:NC * 512].rearrange("p (a m) -> p a m", a=NC),
                in_=WOC[_h:_h + 1].rearrange("o p a m -> (o p) a m")),
                [war.get("awfree2"), pw.get("wofree")], ch="wo")
            wlot = P.total("gpd", "wo")
            for cj4 in range(4):
                cj = whalf * 4 + cj4
                mmt = None
                for j in range(NC):
                    mmt = P.t(lambda t, _c4=cj4, _j=j, _i=i,
                              _lt=wlot: t.matmul(
                        B[2 + _c4 % 2][:, 0:512],
                        lhsT=AWb[:, _j * 512 + _c4 * 128:
                                 _j * 512 + (_c4 + 1) * 128],
                        rhs=SCb[:, _j * n + _i * 512:
                                _j * n + _i * 512 + 512],
                        start=(_j == 0), stop=(_j == NC - 1)),
                        ndtoks + [wlot, pw.get(("o", cj4 % 2))])
                pw["wofree"] = mmt
                ev = P.s(lambda s, _c=cj, _c4=cj4: s.activation(
                    s_tmp[:, 0:512], B[2 + _c4 % 2][:, 0:512], AF.Identity,
                    bias=s_bo[:, _c:_c + 1]), [mmt, lall, pw.get("tread")])
                zt = P.v(lambda v, _c=cj, _i=i: v.tensor_tensor(
                    SA[:, _c * n + _i * 512:_c * n + _i * 512 + 512],
                    s_tmp[:, 0:512],
                    SA[:, _c * n + _i * 512:_c * n + _i * 512 + 512],
                    ALU.add), [ev, pw.get(("zr", cj))])
                pw[("o", cj4 % 2)] = zt
                pw["tread"] = zt
                ztoks.append(zt)
        war["awfree2"] = pw["wofree"]
        pw["sq"] = ztoks[-1]
        pw["eps_dep"] = None
        outt = emit_ln(P, n, i, 512, SA, SA, s_m, s_r, s_tmp, s_trow,
                       s_onec, s_oner, B[4][0:1, :], B[5][0:1, :], B[6],
                       B[1], ztoks, pw, s_eps=s_eps, s_onecb=s_onecb)
        pw["tread"] = pw["sq"]
        for cj in range(NC):
            pw[("zr", cj)] = outt[-1]
        war[("xpk", 0)] = outt[-1]
        war[("xpk", 1)] = outt[-1]
        if i + 1 < nsub:
            war["ndw"] = outt[-1]
        x3toks[i] = outt

    x3done = x3toks[nsub - 1][-1]
    if dump:
        snap3 = P.sd(lambda s: s.dma_start(out=XD3, in_=SA[:]), [x3done],
                     ch="dbg")
        dwseed = {("zr", cj): snap3 for cj in range(NC)}
    else:
        dwseed = {}
    P.gd(lambda g: g.dma_start(
        out=s_winb[:].rearrange("p (a m) -> p a m", a=NC), in_=WINC),
        [lkvb], ch="wsm")
    P.gd(lambda g: g.dma_start(out=s_woutb[:], in_=WOUT), [lkvb],
         ch="wsm")
    lwsm = P.total("gpd", "wsm")
    dw = {"eps_dep": None, "sq": pw.get("sq"), "trow": pw.get("trow")}
    dw.update(dwseed)
    stoks = []
    for i in range(nsub):
        casts = []
        for j in range(NC):
            off = j * n + i * 512
            casts.append(P.v(lambda v, _o=off, _j=j: v.tensor_copy(
                SEb[:, _j * 512:(_j + 1) * 512], SA[:, _o:_o + 512]),
                [x3toks[i][j], dw.get("x3b"), war["pTread"]]))
        mmt = None
        for j in range(NC):
            mmt = P.t(lambda t, _j=j: t.matmul(
                B[6][0:64, 0:512], lhsT=s_winb[:, _j * 64:(_j + 1) * 64],
                rhs=SEb[:, _j * 512:(_j + 1) * 512],
                start=(_j == 0), stop=(_j == NC - 1)),
                [lwsm, casts[j], dw.get("psread"), war["numread"]])
        sv = P.s(lambda s: s.activation(
            s_s[:], B[6][0:64, 0:512], AF.Identity, bias=s_bin[:]),
            [mmt, lall, dw.get("sread"), war["numread"]])
        dw["psread"] = sv
        gt = None
        for ii in range(16):
            hi = P.s(lambda s, _ii=ii: s.activation(
                s_hi[:], s_s[:], AF.Relu,
                scale=s_w1[:, _ii:_ii + 1], bias=s_b1[:, _ii:_ii + 1]),
                [sv, lall, dw.get("hiread"), dw.get("hi_war"),
                 war["kvsread"]])
            if ii == 0:
                gt = P.v(lambda v: v.tensor_scalar(
                    s_g[:], s_hi[:], s_w2[:, 0:1], None, op0=ALU.mult),
                    [hi, dw.get("gread")])
            else:
                gt = P.v(lambda v, _ii=ii: v.scalar_tensor_tensor(
                    s_g[:], s_hi[:], s_w2[:, _ii:_ii + 1], s_g[:],
                    op0=ALU.mult, op1=ALU.add), [hi, gt])
            dw["hiread"] = gt
        dw["sread"] = gt
        g2 = P.v(lambda v: v.tensor_scalar(
            s_g[:], s_g[:], s_b2[:], None, op0=ALU.add), [gt])
        gb = P.v(lambda v: v.tensor_copy(s_gb[:], s_g[:]),
                 [g2, dw.get("gbread")])
        dw["gread"] = gb
        ztoks = []
        for cj in range(NC):
            mmt = P.t(lambda t, _c=cj: t.matmul(
                B[2 + _c % 2][:, 0:512],
                lhsT=s_woutb[:, _c * 128:(_c + 1) * 128],
                rhs=s_gb[:], start=True, stop=True),
                [gb, lwsm, dw.get(("o", cj % 2))])
            ev = P.s(lambda s, _c=cj: s.activation(
                s_tmp[:, 0:512], B[2 + _c % 2][:, 0:512], AF.Identity,
                bias=s_bout[:, _c:_c + 1]), [mmt, dw.get("tread")])
            zt = P.v(lambda v, _c=cj, _i=i: v.tensor_tensor(
                SA[:, _c * n + _i * 512:_c * n + _i * 512 + 512],
                s_tmp[:, 0:512],
                SA[:, _c * n + _i * 512:_c * n + _i * 512 + 512], ALU.add),
                [ev, dw.get(("zr", cj)), gb])
            dw[("o", cj % 2)] = zt
            dw["tread"] = zt
            ztoks.append(zt)
        dw["gbread"] = ztoks[-1]
        dw["x3b"] = ztoks[-1]
        dw["sq"] = ztoks[-1]
        outt = emit_ln(P, n, i, 512, SA, SA, s_m, s_r, s_tmp, s_trow,
                       s_onec, s_oner, B[4][0:1, :], B[5][0:1, :], B[0],
                       B[1], ztoks, dw, s_eps=s_eps, s_onecb=s_onecb)
        dw["tread"] = dw["sq"]
        stok = P.sd(lambda s, _i=i: s.dma_start(
            out=fmview(YT, n)[:, :, _i * 512:_i * 512 + 512],
            in_=SA[:].rearrange("p (a m) -> p a m", a=NC)[
                :, :, _i * 512:_i * 512 + 512]), outt, ch="st")
        for cj in range(NC):
            dw[("zr", cj)] = stok
        dw["hi_war"] = stok
        stoks.append(stok)
    if dump:
        stoks = stoks + [P.total("syd", "dbg")]
    P.finish(stoks)
    P.build()
    return nc


_V2_NC = None


def kernel(L0_attn_x, L0_attn_router_w, L0_attn_router_b, L0_attn_wq,
           L0_attn_wk, L0_attn_wv, L0_ffn_w_up, L0_ffn_w_down, L1_attn_w_qkv,
           L1_attn_b_qkv, L1_attn_w_out, L1_attn_b_out, L1_attn_proj_matrix,
           L1_ffn_Win, L1_ffn_bin_, L1_ffn_W1, L1_ffn_b1, L1_ffn_W2,
           L1_ffn_b2, L1_ffn_Wout, L1_ffn_bout):
    _ = (L0_attn_wq.shape, L0_attn_wk.shape, L0_attn_wv.shape)
    global _V2_NC
    if _V2_NC is None:
        _V2_NC = build_v2(NTOK)
    nc = _V2_NC
    x = np.ascontiguousarray(L0_attn_x, dtype=np.float32)
    identb = np.eye(128, dtype=np.float32).astype(ml_dtypes.bfloat16)
    bqkv = L1_attn_b_qkv
    wup = np.asarray(L0_ffn_w_up)
    wdn = np.asarray(L0_ffn_w_down)
    wqkv = np.asarray(L1_attn_w_qkv)
    wo = np.asarray(L1_attn_w_out)
    shared = {
        "rwc": np.ascontiguousarray(L0_attn_router_w.reshape(NC, 128).T),
        "rb": np.array([[L0_attn_router_b[0]]], dtype=np.float32),
        "identb": identb,
        "wqkvc": np.ascontiguousarray(
            wqkv.reshape(8, 128, 6, 512).transpose(2, 1, 0, 3)),
        "woc": np.ascontiguousarray(
            wo.reshape(8, 128, 2, 512).transpose(2, 1, 0, 3)),
        "bq": np.ascontiguousarray(bqkv[0:D].reshape(NC, 128).T),
        "bk": np.ascontiguousarray(bqkv[D:2 * D].reshape(NC, 128).T),
        "bvrow": np.ascontiguousarray(bqkv[2 * D:3 * D][None, :]),
        "projt": np.ascontiguousarray(np.concatenate(
            [L1_attn_proj_matrix[h].T for h in range(HEADS)], axis=1)),
        "bo": np.ascontiguousarray(L1_attn_b_out.reshape(NC, 128).T),
        "winc": np.ascontiguousarray(
            np.asarray(L1_ffn_Win).reshape(8, 128, 64).transpose(1, 0, 2)),
        "bin": np.ascontiguousarray(L1_ffn_bin_[:, None]),
        "w1": np.ascontiguousarray(L1_ffn_W1),
        "b1": np.ascontiguousarray(L1_ffn_b1),
        "w2": np.ascontiguousarray(L1_ffn_W2),
        "b2": np.ascontiguousarray(L1_ffn_b2[:, None]),
        "wout": np.ascontiguousarray(L1_ffn_Wout),
        "bout": np.ascontiguousarray(L1_ffn_bout.reshape(NC, 128).T),
    }
    wupc_full = np.ascontiguousarray(
        wup.reshape(32, 128, 8, 128).transpose(0, 3, 2, 1)
        .reshape(32, 128, 1024))
    wdnc_full = np.ascontiguousarray(
        wdn.reshape(8, 128, 32, 128).transpose(0, 3, 2, 1)
        .reshape(8, 128, 4096))
    maps = []
    for core in range(8):
        b, o = core // 4, (core % 4) * NTOK
        m = dict(shared)
        m["xt"] = np.ascontiguousarray(x[b, o:o + NTOK].T)
        m["xrevt"] = np.ascontiguousarray(x[b][::-1][o:o + NTOK].T)
        m["wups"] = np.ascontiguousarray(wupc_full[4 * core:4 * core + 4])
        m["wdns"] = np.ascontiguousarray(wdnc_full[core])
        maps.append(m)
    res = run_bass_kernel_spmd(nc, maps, core_ids=list(range(8)))
    out = np.zeros((2, NSEQ, D), np.float32)
    for core in range(8):
        b, o = core // 4, (core % 4) * NTOK
        out[b, o:o + NTOK] = res.results[core]["yt"].T
    return out


# version 15
# speedup vs baseline: 1.0997x; 1.0041x over previous
import numpy as np
import ml_dtypes
from contextlib import ExitStack

import concourse.bass as bass
import concourse.mybir as mybir
from concourse.bass_utils import run_bass_kernel_spmd

F32 = mybir.dt.float32
BF16 = mybir.dt.bfloat16
I32 = mybir.dt.int32
AF = mybir.ActivationFunctionType
ALU = mybir.AluOpType
AX = mybir.AxisListType

D = 1024
NC = 8
DFF = 4096
NFF = 32
NTOK = 2048
NSEQ = 8192
HEADS = 16
HDIM = 64
LN_EPS = 1e-5
RMS_EPS = 1e-8
PHI_EPS = 1e-6
DATA_NORM = float(np.float32(64.0 ** -0.25))
NITER_BISECT = 42
ROUTER_K = 1024.0
KOFF = 16384


class Prog:
    SECTIONS = {
        "tensor": ("pe", 1),
        "vector": ("ve", 1),
        "scalar": ("sc", 1),
        "gpsimd": ("gp", 1),
    }

    def __init__(self, nc):
        self.nc = nc
        self.es = ExitStack()
        self.items = {s: [] for s in
                      ("tensor", "vector", "scalar", "gpsimd", "sync")}
        self.counts = {}
        self.n_cc = 0
        self.final_deps = []

    def _add(self, section, sem, inc, fn, deps):
        n = self.counts.get(sem, 0) + inc
        self.counts[sem] = n
        self.items[section].append((fn, [d for d in deps if d is not None],
                                    sem, inc))
        return (sem, n)

    def t(self, fn, deps=()):
        return self._add("tensor", "pe", 1, fn, deps)

    def v(self, fn, deps=()):
        return self._add("vector", "ve", 1, fn, deps)

    def s(self, fn, deps=()):
        return self._add("scalar", "sc", 1, fn, deps)

    def g(self, fn, deps=()):
        return self._add("gpsimd", "gp", 1, fn, deps)

    def sd(self, fn, deps=(), ch="ld"):
        return self._add("sync", f"syd_{ch}", 16, fn, deps)

    def ad(self, fn, deps=(), ch="ld"):
        return self._add("scalar", f"scd_{ch}", 16, fn, deps)

    def gd(self, fn, deps=(), ch="ld"):
        return self._add("gpsimd", f"gpd_{ch}", 16, fn, deps)

    def total(self, queue, ch):
        name = f"{queue}_{ch}"
        return (name, self.counts.get(name, 0))

    def col(self, fn, deps=()):
        self.n_cc += 1
        sem = f"cc{self.n_cc}"
        return self._add("gpsimd", sem, 1, fn, deps)

    def finish(self, deps):
        self.final_deps = list(deps)

    def build(self):
        nc = self.nc
        sems = {}
        for name in self.counts:
            sems[name] = self.es.enter_context(nc.semaphore(name))
        block = self.es.enter_context(nc.Block())
        hw = {}

        def coalesce(deps):
            best = {}
            for dsem, dval in deps:
                if best.get(dsem, 0) < dval:
                    best[dsem] = dval
            return best.items()

        def emit(section, eng):
            for fn, deps, sem, inc in self.items[section]:
                for dsem, dval in coalesce(deps):
                    key = (section, dsem)
                    if hw.get(key, 0) < dval:
                        hw[key] = dval
                        eng.wait_ge(sems[dsem], dval)
                ins = fn(eng)
                ins.then_inc(sems[sem], inc)
            if section == "sync":
                for dsem, dval in coalesce(self.final_deps):
                    key = (section, dsem)
                    if hw.get(key, 0) < dval:
                        hw[key] = dval
                        eng.wait_ge(sems[dsem], dval)

        if self.items["tensor"]:
            block.tensor(lambda t: emit("tensor", t))
        if self.items["vector"]:
            block.vector(lambda v: emit("vector", v))
        if self.items["scalar"]:
            block.scalar(lambda s: emit("scalar", s))
        if self.items["gpsimd"]:
            block.gpsimd(lambda g: emit("gpsimd", g))
        block.sync(lambda s: emit("sync", s))
        return self


def sb(nc, es, name, shape, dt=F32):
    return es.enter_context(nc.sbuf_tensor(name, shape, dt))


def ps(nc, es, name, shape, dt=F32):
    return es.enter_context(nc.psum_tensor(name, shape, dt))


def fmview(ap, n):
    return ap.rearrange("(a p) n -> p a n", p=128)


def emit_bisect(P, s_rwf, s_ones2, s_oner, p_cnt, p_bc, st,
                dep_rwf, dep_ones2, k=ROUTER_K, niter=NITER_BISECT):
    t_lo, t_hi, t_mid = st["t_lo"], st["t_hi"], st["t_mid"]
    fs, fsn, rowcnt = st["fs"], st["fsn"], st["rowcnt"]
    mask64 = st["mask64"]

    c1 = P.v(lambda v: v.memset(t_hi[:], 1e4), [dep_rwf])
    c2 = P.v(lambda v: v.memset(t_lo[:], -1e4), [c1])
    last = P.v(lambda v: v.memset(t_mid[:], 0.0), [c2])
    prev_f1 = None
    for i in range(niter):
        slot = i % 2
        mk = P.v(lambda v: v.tensor_scalar(
            mask64[:], s_rwf[:], t_mid[:], 0.0, op0=ALU.is_ge, op1=ALU.add,
            accum_out=rowcnt[:]), [last, prev_f1])
        cm = P.t(lambda t, _s=slot: t.matmul(
            p_cnt[:, _s:_s + 1], lhsT=s_ones2[:], rhs=rowcnt[:],
            start=True, stop=True), [mk, dep_ones2, prev_f1])
        f1 = P.v(lambda v, _s=slot: v.tensor_scalar(
            fs[:], p_cnt[:, _s:_s + 1], k, None, op0=ALU.is_ge), [cm])
        f2 = P.v(lambda v: v.tensor_scalar(fsn[:], fs[:], -1, 1,
                                           op0=ALU.mult, op1=ALU.add), [f1])
        s1 = P.v(lambda v: v.copy_predicated(t_lo[:], fs[:], t_mid[:]), [f2])
        s2 = P.v(lambda v: v.copy_predicated(t_hi[:], fsn[:], t_mid[:]), [s1])
        s3 = P.v(lambda v: v.tensor_tensor(t_mid[:], t_lo[:], t_hi[:],
                                           ALU.add), [s2])
        last = P.v(lambda v: v.tensor_scalar(t_mid[:], t_mid[:], 0.5, None,
                                             op0=ALU.mult), [s3])
        prev_f1 = f1
    return last


def emit_ln(P, n, i, w, s_y, s_out, s_m, s_r, s_tmp, s_trow, s_onec, s_oner,
            p_s1, p_s2, p_mbc, p_rbc, ytoks, war, s_eps=None,
            extra_out_dep=None, s_onecb=None):
    s1tok = s2tok = None
    sqprev = war.get("sq")
    tmpb = s_tmp[:].bitcast(BF16) if s_onecb is not None else None
    for j in range(NC):
        off = j * n + i * 512
        s1tok = P.t(lambda t, _o=off, _w=w, _j=j: t.matmul(
            p_s1[:, 0:_w], lhsT=s_onec[:], rhs=s_y[:, _o:_o + _w],
            start=(_j == 0), stop=(_j == NC - 1)),
            [ytoks[j], war.get("s1")])
        sq = P.s(lambda s, _o=off, _w=w: s.activation(
            tmpb[:, 0:_w], s_y[:, _o:_o + _w], AF.Square),
            [ytoks[j], sqprev])
        s2tok = P.t(lambda t, _w=w, _j=j: t.matmul(
            p_s2[:, 0:_w], lhsT=s_onecb[:], rhs=tmpb[:, 0:_w],
            start=(_j == 0), stop=(_j == NC - 1)),
            [sq, war.get("s2")])
        sqprev = s2tok
    war["sq"] = sqprev
    o = 0
    f1 = P.v(lambda v, _w=w, _o=o: v.tensor_scalar(
        s_m[:, _o:_o + _w], p_s1[:, 0:_w], 1.0 / D, None, op0=ALU.mult),
        [s1tok])
    f2 = P.v(lambda v, _w=w, _o=o: v.tensor_scalar(
        s_r[:, _o:_o + _w], p_s2[:, 0:_w], 1.0 / D, None, op0=ALU.mult),
        [s2tok])
    war["s1"] = f1
    war["s2"] = f2
    f3 = P.v(lambda v, _w=w, _o=o: v.tensor_tensor(
        s_trow[:, 0:_w], s_m[:, _o:_o + _w], s_m[:, _o:_o + _w],
        ALU.mult), [f1, f2, war.get("trow")])
    f4 = P.v(lambda v, _w=w, _o=o: v.tensor_tensor(
        s_r[:, _o:_o + _w], s_r[:, _o:_o + _w], s_trow[:, 0:_w],
        ALU.subtract), [f3])
    war["trow"] = f4
    f5 = P.s(lambda s, _w=w, _o=o: s.activation(
        s_r[:, _o:_o + _w], s_r[:, _o:_o + _w], AF.Ln,
        bias=s_eps[:]), [f4, war.get("eps_dep")])
    war["eps_dep"] = None
    f6 = P.s(lambda s, _w=w, _o=o: s.activation(
        s_r[:, _o:_o + _w], s_r[:, _o:_o + _w], AF.Exp, scale=-0.5), [f5])
    b1 = P.t(lambda t, _w=w, _o=o: t.matmul(
        p_mbc[:, 0:_w], lhsT=s_oner[:], rhs=s_m[:, _o:_o + _w],
        start=True, stop=True), [f1, war.get("bc")])
    b2 = P.t(lambda t, _w=w, _o=o: t.matmul(
        p_rbc[:, 0:_w], lhsT=s_oner[:], rhs=s_r[:, _o:_o + _w],
        start=True, stop=True), [f6])
    outtoks = []
    for j in range(NC):
        off = j * n + i * 512
        n1 = P.v(lambda v, _o=off, _w=w: v.tensor_tensor(
            s_out[:, _o:_o + _w], s_y[:, _o:_o + _w], p_mbc[:, 0:_w],
            ALU.subtract), [b1, b2, ytoks[j], extra_out_dep])
        n2 = P.v(lambda v, _o=off, _w=w: v.tensor_tensor(
            s_out[:, _o:_o + _w], s_out[:, _o:_o + _w], p_rbc[:, 0:_w],
            ALU.mult), [n1])
        outtoks.append(n2)
    war["bc"] = outtoks[-1]
    return outtoks


def build_v2(n=NTOK, dump=False):
    nc = bass.Bass("TRN2", target_bir_lowering=False, debug=False)
    nsub = n // 512
    ntile = n // 128
    H = 1024

    XT = nc.dram_tensor("xt", [D, n], F32, kind="ExternalInput").ap()
    XRT = nc.dram_tensor("xrevt", [D, n], F32, kind="ExternalInput").ap()
    RWC = nc.dram_tensor("rwc", [128, NC], F32, kind="ExternalInput").ap()
    RB = nc.dram_tensor("rb", [1, 1], F32, kind="ExternalInput").ap()
    IDB = nc.dram_tensor("identb", [128, 128], BF16, kind="ExternalInput").ap()
    WUPS = nc.dram_tensor("wups", [4, 128, 1024], F32,
                          kind="ExternalInput").ap()
    WDNS = nc.dram_tensor("wdns", [128, 4096], F32,
                          kind="ExternalInput").ap()
    WQKVC = nc.dram_tensor("wqkvc", [6, 128, NC, 512], F32,
                           kind="ExternalInput").ap()
    BQ = nc.dram_tensor("bq", [128, 8], F32, kind="ExternalInput").ap()
    BK = nc.dram_tensor("bk", [128, 8], F32, kind="ExternalInput").ap()
    BV = nc.dram_tensor("bvrow", [1, D], F32, kind="ExternalInput").ap()
    PJ = nc.dram_tensor("projt", [64, HEADS * 256], F32,
                        kind="ExternalInput").ap()
    WOC = nc.dram_tensor("woc", [2, 128, NC, 512], F32,
                         kind="ExternalInput").ap()
    BO = nc.dram_tensor("bo", [128, NC], F32, kind="ExternalInput").ap()
    WINC = nc.dram_tensor("winc", [128, NC, 64], F32,
                          kind="ExternalInput").ap()
    BIN = nc.dram_tensor("bin", [64, 1], F32, kind="ExternalInput").ap()
    W1 = nc.dram_tensor("w1", [64, 16], F32, kind="ExternalInput").ap()
    B1 = nc.dram_tensor("b1", [64, 16], F32, kind="ExternalInput").ap()
    W2 = nc.dram_tensor("w2", [64, 16], F32, kind="ExternalInput").ap()
    B2_ = nc.dram_tensor("b2", [64, 1], F32, kind="ExternalInput").ap()
    WOUT = nc.dram_tensor("wout", [64, D], F32, kind="ExternalInput").ap()
    BOUT = nc.dram_tensor("bout", [128, NC], F32, kind="ExternalInput").ap()
    YT = nc.dram_tensor("yt", [D, n], F32, kind="ExternalOutput").ap()

    cc1_in = nc.dram_tensor("cc1_in", [1, n], F32, kind="Internal").ap()
    cc1_out = nc.dram_tensor("cc1_out", [4, n], F32, kind="Internal").ap()
    ST_in = nc.dram_tensor("st_in", [1, 2], F32, kind="Internal").ap()
    ST_out = nc.dram_tensor("st_out", [8, 2], F32, kind="Internal").ap()
    WUP_in = nc.dram_tensor("wup_in", [1, 4 * 128 * 1024], BF16,
                            kind="Internal").ap()
    WUP_out = nc.dram_tensor("wup_out", [8, 4 * 128 * 1024], BF16,
                             kind="Internal").ap()
    WDN_in = nc.dram_tensor("wdn_in", [1, 4 * 128 * 1024], BF16,
                            kind="Internal").ap()
    WDN_out = nc.dram_tensor("wdn_out", [8, 4 * 128 * 1024], BF16,
                             kind="Internal").ap()
    kv_in = nc.dram_tensor("kv_in", [128, 2080], F32, kind="Internal").ap()
    kv_out = nc.dram_tensor("kv_out", [128, 2080], F32, kind="Internal").ap()
    if dump:
        XD1 = nc.dram_tensor("xd1", [128, NC * n], F32,
                             kind="ExternalOutput").ap()
        XDH = nc.dram_tensor("xdh", [128, NC * n], BF16,
                             kind="ExternalOutput").ap()
        XD2 = nc.dram_tensor("xd2", [128, NC * n], F32,
                             kind="ExternalOutput").ap()
        XDW = nc.dram_tensor("xdw", [8, 4 * 128 * 1024], BF16,
                             kind="ExternalOutput").ap()
        XDW2 = nc.dram_tensor("xdw2", [8, 4 * 128 * 1024], BF16,
                              kind="ExternalOutput").ap()
        XD3 = nc.dram_tensor("xd3", [128, NC * n], F32,
                             kind="ExternalOutput").ap()
        XDKV = nc.dram_tensor("xdkv", [128, 2080], F32,
                              kind="ExternalOutput").ap()
        XDHH = nc.dram_tensor("xdhh", [128, NFF * 1024], BF16,
                              kind="ExternalOutput").ap()
        XDSB = nc.dram_tensor("xdsb", [128, 1024], F32,
                              kind="ExternalOutput").ap()
        XDAW = nc.dram_tensor("xdaw", [128, 1024], BF16,
                              kind="ExternalOutput").ap()
        XDH3 = nc.dram_tensor("xdh3", [128, NC * n], BF16,
                              kind="ExternalOutput").ap()

    P = Prog(nc)
    es = P.es
    SA = sb(nc, es, "SA", [128, NC * n])
    SB = sb(nc, es, "SB", [128, NC * n])
    SBb = SB[:].bitcast(BF16)
    SC = sb(nc, es, "SC", [128, 8192])
    SCb = SC[:].bitcast(BF16)
    SD = sb(nc, es, "SD", [128, 2080])
    SE = sb(nc, es, "SE", [128, 2048])
    SEb = SE[:].bitcast(BF16)
    AW = sb(nc, es, "AW", [128, 2048])
    AWb = AW[:].bitcast(BF16)
    PJT = sb(nc, es, "PJT", [128, HEADS * 256], BF16)
    s_pj = PJT
    s_tmp = sb(nc, es, "s_tmp", [128, 512])
    s_tmpb = s_tmp[:].bitcast(BF16)
    s_p56 = sb(nc, es, "s_p56", [128, 1024])
    s_rwc = sb(nc, es, "s_rwc", [128, NC])
    s_rb = sb(nc, es, "s_rb", [1, 1])
    s_rwf = sb(nc, es, "s_rwf", [128, 4 * n // 128])
    s_idb = sb(nc, es, "s_idb", [128, 128], BF16)
    s_onec = sb(nc, es, "s_onec", [128, 1])
    s_onecb = sb(nc, es, "s_onecb", [128, 1], BF16)
    s_ones2 = sb(nc, es, "s_ones2", [128, 128])
    s_oner = s_ones2[0:1, :]
    s_onerb = sb(nc, es, "s_onerb", [1, 128], BF16)
    s_eps = sb(nc, es, "s_eps", [1, 1])
    s_eps8 = sb(nc, es, "s_eps8", [1, 1])
    s_epsphi = sb(nc, es, "s_epsphi", [1, 1])
    s_m = sb(nc, es, "s_m", [1, 512])
    s_r = sb(nc, es, "s_r", [1, 512])
    s_trow = sb(nc, es, "s_trow", [1, 512])
    s_vv = SD[0:1, 0:512]
    s_c2 = SD[0:1, 512:1024]
    s_hr2 = SD[0:1, 1024:2048]
    s_mrow = s_m[0:1, 0:512]
    s_acc = sb(nc, es, "s_acc", [128, 8])
    s_red = sb(nc, es, "s_red", [128, 2])
    s_th = sb(nc, es, "s_th", [1, 2])
    s_thc = sb(nc, es, "s_thc", [128, 2])
    s_aps = sb(nc, es, "s_aps", [1, 16])
    st = {kk: sb(nc, es, kk, [128, 1]) for kk in
          ("t_lo", "t_hi", "t_mid", "rowcnt")}
    st["fs"] = sb(nc, es, "fs", [128, 1], I32)
    st["fsn"] = sb(nc, es, "fsn", [128, 1], I32)
    st["mask64"] = sb(nc, es, "mask64", [128, 4 * n // 128], BF16)
    s_bq = sb(nc, es, "s_bq", [128, 8])
    s_bk = sb(nc, es, "s_bk", [128, 8])
    s_bvb = SD[0:1, 0:512].bitcast(BF16)
    s_denb = s_trow[:].bitcast(BF16)[0:1, 0:512]
    s_phik2 = [SEb[:, 1024:1280], SEb[:, 1280:1536]]
    s_nmx4 = sb(nc, es, "s_nmx4", [128, 4])
    s_nmx2 = [sb(nc, es, f"s_nmx{i}", [128, 1]) for i in range(2)]
    s_phiqT = [SEb[:, 0:512], SEb[:, 512:1024]]
    s_kvs = sb(nc, es, "s_kvs", [128, 264], BF16)
    s_bo = sb(nc, es, "s_bo", [128, NC])
    s_bin = sb(nc, es, "s_bin", [64, 1])
    s_w1 = sb(nc, es, "s_w1", [64, 16])
    s_b1 = sb(nc, es, "s_b1", [64, 16])
    s_w2 = sb(nc, es, "s_w2", [64, 16])
    s_b2 = sb(nc, es, "s_b2", [64, 1])
    s_bout = sb(nc, es, "s_bout", [128, NC])
    s_bv = s_p56[0:1, 0:D]
    s_exp2 = [s_p56[:, 512:768], s_p56[:, 768:1024]]
    s_den = SB[0:1, 9216:9728]
    s_dbc2 = [s_p56[:, 0:512], s_p56[:, 512:1024]]
    s_sbc = SE[:, 0:1024]
    s_s = SC[0:64, 0:512]
    s_hi = SD[0:64, 0:512]
    s_g = SC[0:64, 512:1024]
    s_winb = SCb[:, 8192:8704]
    s_woutb = SCb[0:64, 8704:9728]
    s_gb = SCb[0:64, 9728:10240]
    P01 = ps(nc, es, "P01", [128, 1024])
    P23 = ps(nc, es, "P23", [128, 1024])
    B = [P01[:, 0:512], P01[:, 512:1024], P23[:, 0:512], P23[:, 512:1024]]
    B += [ps(nc, es, f"B{i}", [128, 512]) for i in range(4, 7)]
    B7 = ps(nc, es, "B7", [128, 1024], BF16)
    PP = [P01, P23]

    war = {}

    P.sd(lambda s: s.dma_start(
        out=SA[:].rearrange("p (a m) -> p a m", a=NC), in_=fmview(XT, n)),
        ch="x")
    P.sd(lambda s: s.dma_start(
        out=SB[:].rearrange("p (a m) -> p a m", a=NC), in_=fmview(XRT, n)),
        ch="x")
    lxall = P.total("syd", "x")
    for ap_, tile in ((RWC, s_rwc), (RB, s_rb), (IDB, s_idb), (BQ, s_bq),
                      (BK, s_bk), (BO, s_bo), (BIN, s_bin), (W1, s_w1),
                      (B1, s_b1), (W2, s_w2), (B2_, s_b2), (BOUT, s_bout)):
        P.sd(lambda s, _a=ap_, _t=tile: s.dma_start(out=_t[:], in_=_a))
    lall = P.total("syd", "ld")
    m2 = P.v(lambda v: v.memset(s_onec[:], 1.0))
    m2b = P.v(lambda v: v.memset(s_onecb[:], 1.0))
    m3 = P.v(lambda v: v.memset(s_ones2[:], 1.0))
    m1 = m3
    m3b = P.v(lambda v: v.memset(s_onerb[:], 1.0))
    m4 = P.v(lambda v: v.memset(s_eps[:], LN_EPS))
    m4b = P.v(lambda v: v.memset(s_eps8[:], RMS_EPS))
    m4c = P.v(lambda v: v.memset(s_epsphi[:], PHI_EPS))

    s_rwo = SD[0:1, 0:n]
    rwevs = []
    for i in range(nsub):
        pb = B[i % 2]
        mmtok = None
        for j in range(NC):
            mmtok = P.t(lambda t, _i=i, _j=j, _pb=pb: t.matmul(
                _pb[0:1, 0:512], lhsT=s_rwc[:, _j:_j + 1],
                rhs=SA[:, _j * n + _i * 512:_j * n + _i * 512 + 512],
                start=(_j == 0), stop=(_j == NC - 1)),
                [lxall, lall, war.get(("rwp", i % 2))])
        ev = P.v(lambda v, _i=i, _pb=pb: v.tensor_scalar(
            s_rwo[:, _i * 512:_i * 512 + 512], _pb[0:1, 0:512], s_rb[:],
            None, op0=ALU.add), [mmtok, lall])
        war[("rwp", i % 2)] = ev
        rwevs.append(ev)
    P.gd(lambda g: g.dma_start(out=cc1_in, in_=s_rwo[:]), rwevs, ch="cc")
    cc1 = P.col(lambda g: g.collective_compute(
        "AllGather", ALU.bypass,
        replica_groups=[[0, 1, 2, 3], [4, 5, 6, 7]],
        ins=[cc1_in.opt()], outs=[cc1_out.opt()]),
        [P.total("gpd", "cc")])
    P.gd(lambda g: g.dma_start(
        out=s_rwf[:], in_=cc1_out.rearrange("r (p f) -> (r p) f", p=32)),
        [cc1], ch="g1")
    lrwf = P.total("gpd", "g1")

    red_toks = []
    for pi in range(8):
        reg = pi % 2
        ld = P.ad(lambda s, _pi=pi, _reg=reg: s.dma_start(
            out=AW[:, _reg * 1024:(_reg + 1) * 1024],
            in_=(WUPS[_pi:_pi + 1].rearrange("o p m -> (o p) m")
                 if _pi < 4 else
                 WDNS[:, (_pi - 4) * 1024:(_pi - 3) * 1024])),
            [red_toks[pi - 2] if pi >= 2 else None], ch=f"ws{reg}")
        rt = P.v(lambda v, _pi=pi, _reg=reg: v.tensor_reduce(
            s_acc[:, _pi:_pi + 1], AW[:, _reg * 1024:(_reg + 1) * 1024],
            axis=AX.X, op=ALU.add, apply_absolute_value=True),
            [P.total("scd", f"ws{reg}")])
        red_toks.append(rt)
    r1 = P.v(lambda v: v.tensor_reduce(
        s_red[:, 0:1], s_acc[:, 0:4], axis=AX.X, op=ALU.add), red_toks)
    r2 = P.v(lambda v: v.tensor_reduce(
        s_red[:, 1:2], s_acc[:, 4:8], axis=AX.X, op=ALU.add),
        red_toks + [r1])
    apm = P.t(lambda t: t.matmul(B[2][0:1, 0:2], lhsT=s_onec[:],
                                 rhs=s_red[:, 0:2], start=True, stop=True),
              [r2, m2])
    stv = P.v(lambda v: v.tensor_copy(s_th[:, 0:2], B[2][0:1, 0:2]), [apm])
    P.gd(lambda g: g.dma_start(out=ST_in, in_=s_th[:, 0:2]), [stv], ch="cs")
    ccS = P.col(lambda g: g.collective_compute(
        "AllGather", ALU.bypass,
        replica_groups=[[0, 1, 2, 3, 4, 5, 6, 7]],
        ins=[ST_in.opt()], outs=[ST_out.opt()]),
        [P.total("gpd", "cs")])
    P.gd(lambda g: g.dma_start(
        out=s_aps[:], in_=ST_out.rearrange("r c -> () (r c)")), [ccS],
        ch="g2")
    laps = P.total("gpd", "g2")
    a1 = P.v(lambda v: v.tensor_reduce(
        s_th[:, 0:1], s_aps[0:1, 0:16:2], axis=AX.X, op=ALU.add),
        [laps, stv])
    a2 = P.v(lambda v: v.tensor_reduce(
        s_th[:, 1:2], s_aps[0:1, 1:16:2], axis=AX.X, op=ALU.add),
        [laps, a1])
    a3 = P.v(lambda v: v.tensor_scalar(
        s_th[:, 0:2], s_th[:, 0:2], 0.5 / (D * DFF), None, op0=ALU.mult),
        [a1, a2])
    bb = P.t(lambda t: t.matmul(B[2][:, 2:4], lhsT=s_oner[:],
                                rhs=s_th[:, 0:2], start=True, stop=True),
             [a3, m1, stv])
    thc = P.v(lambda v: v.tensor_copy(s_thc[:], B[2][:, 2:4]), [bb])

    bis_last = emit_bisect(P, s_rwf, s_ones2, s_oner, B[3], B[4], st,
                           lrwf, m1, niter=NITER_BISECT)

    scr = {0: SC[:, 0:1024], 1: SC[:, 1024:2048]}
    prev_sg = {}
    prev_ld = {}
    for pi in range(8):
        reg = pi % 2
        slot = pi % 2
        which = 0 if pi < 4 else 1
        ld = P.ad(lambda s, _pi=pi, _reg=reg: s.dma_start(
            out=AW[:, _reg * 1024:(_reg + 1) * 1024],
            in_=(WUPS[_pi:_pi + 1].rearrange("o p m -> (o p) m")
                 if _pi < 4 else
                 WDNS[:, (_pi - 4) * 1024:(_pi - 3) * 1024])),
            [red_toks[6 + reg], prev_ld.get(reg)], ch=f"wt{reg}")
        ldt = P.total("scd", f"wt{reg}")
        ab = P.s(lambda s, _reg=reg: s.activation(
            scr[_reg], AW[:, _reg * 1024:(_reg + 1) * 1024], AF.Abs),
            [ldt, prev_sg.get(("mk", reg))])
        mk = P.v(lambda v, _reg=reg, _w=which: v.tensor_scalar(
            scr[_reg], scr[_reg],
            s_thc[:, _w:_w + 1], None, op0=ALU.is_gt), [thc, ab])
        wm = P.v(lambda v, _reg=reg: v.tensor_tensor(
            scr[_reg], scr[_reg],
            AW[:, _reg * 1024:(_reg + 1) * 1024], ALU.mult), [mk])
        prev_ld[reg] = wm
        sg = P.s(lambda s, _slot=slot, _reg=reg: s.activation(
            PJT[:, _slot * 1024:(_slot + 1) * 1024],
            scr[_reg], AF.Sign),
            [wm, prev_sg.get(("st", slot))])
        prev_sg[("mk", reg)] = sg
        stok = P.ad(lambda s, _pi=pi, _slot=slot: s.dma_start(
            out=((WUP_in[0:1, _pi * 131072:(_pi + 1) * 131072]
                  if _pi < 4 else
                  WDN_in[0:1, (_pi - 4) * 131072:(_pi - 3) * 131072])
                 .rearrange("o (p m) -> (o p) m", p=128)),
            in_=PJT[:, _slot * 1024:(_slot + 1) * 1024]),
            [sg], ch=f"ts{slot}")
        prev_sg[("st", slot)] = stok
        if pi == 3:
            ccWu = P.col(lambda g: g.collective_compute(
                "AllGather", ALU.bypass,
                replica_groups=[[0, 1, 2, 3, 4, 5, 6, 7]],
                ins=[WUP_in.opt()], outs=[WUP_out.opt()]),
                [P.total("scd", "ts0"), P.total("scd", "ts1")])
    ccWd = P.col(lambda g: g.collective_compute(
        "AllGather", ALU.bypass,
        replica_groups=[[0, 1, 2, 3, 4, 5, 6, 7]],
        ins=[WDN_in.opt()], outs=[WDN_out.opt()]),
        [P.total("scd", "ts0"), P.total("scd", "ts1")])

    tern_done = prev_sg[("st", 0)], prev_sg[("st", 1)]
    pjevs = []
    pjld = None
    for half in range(2):
        l1 = P.sd(lambda s, _h=half: s.dma_start(
            out=SE[0:64, 0:2048], in_=PJ[:, _h * 2048:(_h + 1) * 2048]),
            [pjld], ch="pj")
        l2 = P.sd(lambda s, _h=half: s.dma_start(
            out=SE[64:128, 0:2048], in_=PJ[:, _h * 2048:(_h + 1) * 2048]),
            ch="pj")
        lt = P.total("syd", "pj")
        ev = P.v(lambda v, _h=half: v.tensor_scalar(
            PJT[:, _h * 2048:(_h + 1) * 2048], SE[:, 0:2048],
            DATA_NORM, None, op0=ALU.mult),
            [lt, tern_done[0], tern_done[1]])
        pjevs.append(ev)
        pjld = ev
    pj = pjevs[-1]
    P.sd(lambda s: s.dma_start(out=s_bv, in_=BV), ch="bv")
    lbv = P.total("syd", "bv")

    x1toks = {}
    xbtoks = {}
    prev_mb = None
    prev_b0 = P.total("gpd", "cc")
    ln1 = {"sq": prev_sg[("mk", 0)], "s1": None, "s2": None}
    rowrd = {}
    ytoks_all = {}
    for i in range(nsub):
        mr = P.v(lambda v, _i=i: v.tensor_scalar(
            s_mrow[:], s_rwo[:, _i * 512:_i * 512 + 512],
            st["t_lo"][0:1, 0:1], None, op0=ALU.is_ge),
            [bis_last, rwevs[i], prev_mb])
        mb = P.t(lambda t: t.matmul(
            B[5][:, 0:512], lhsT=s_oner[:], rhs=s_mrow[:],
            start=True, stop=True), [mr, m1, war.get("B5rd")])
        prev_mb = mb
        ytoks = []
        for j in range(NC):
            off = j * n + i * 512
            ym = P.v(lambda v, _o=off: v.tensor_tensor(
                SB[:, _o:_o + 512], SB[:, _o:_o + 512], B[5][:, 0:512],
                ALU.mult), [mb, lxall])
            ya = P.v(lambda v, _o=off: v.tensor_tensor(
                SB[:, _o:_o + 512], SB[:, _o:_o + 512], SA[:, _o:_o + 512],
                ALU.add), [ym])
            ytoks.append(ya)
        war["B5rd"] = ytoks[-1]
        ytoks_all[i] = ytoks
    for i in range(nsub):
        ytoks = ytoks_all[i]
        s1tok = None
        s2tok = None
        sqprev = ln1.get("sq")
        for j in range(NC):
            off = j * n + i * 512
            s1tok = P.t(lambda t, _o=off, _j=j: t.matmul(
                B[6][0:1, 0:512], lhsT=s_onec[:], rhs=SB[:, _o:_o + 512],
                start=(_j == 0), stop=(_j == NC - 1)),
                [ytoks[j], m2, ln1.get("s1")])
            sq = P.s(lambda s, _o=off, _j=j: s.activation(
                s_tmpb[:, (_j % 2) * 512:(_j % 2) * 512 + 512],
                SB[:, _o:_o + 512], AF.Square), [ytoks[j], sqprev])
            s2tok = P.t(lambda t, _j=j: t.matmul(
                B[4][0:1, 0:512], lhsT=s_onecb[:],
                rhs=s_tmpb[:, (_j % 2) * 512:(_j % 2) * 512 + 512],
                start=(_j == 0), stop=(_j == NC - 1)),
                [sq, m2b, ln1.get("s2")])
            sqprev = s2tok
        ln1["sq"] = sqprev
        f1 = P.v(lambda v: v.tensor_scalar(
            s_m[:], B[6][0:1, 0:512], 1.0 / D, None, op0=ALU.mult),
            [s1tok, rowrd.get("m")])
        f2 = P.v(lambda v: v.tensor_scalar(
            s_r[:], B[4][0:1, 0:512], 1.0 / D, None, op0=ALU.mult),
            [s2tok, rowrd.get("r")])
        ln1["s1"] = f1
        ln1["s2"] = f2
        f3 = P.v(lambda v: v.tensor_tensor(
            s_trow[:], s_m[:], s_m[:], ALU.mult), [f1, rowrd.get("trow")])
        f4 = P.v(lambda v: v.tensor_tensor(
            s_r[:], s_r[:], s_trow[:], ALU.subtract), [f3, f2])
        rowrd["trow"] = f4
        f4b = P.s(lambda s: s.activation(s_vv[:], s_r[:], AF.Sqrt),
                  [f4, rowrd.get("vv")])
        f5 = P.s(lambda s: s.activation(
            s_r[:], s_r[:], AF.Ln, bias=s_eps[:]),
            [f4b, (m4 if i == 0 else None)])
        f6 = P.s(lambda s: s.activation(
            s_r[:], s_r[:], AF.Exp, scale=-0.5), [f5])
        g1 = P.v(lambda v: v.tensor_tensor(
            s_vv[:], s_vv[:], s_r[:], ALU.mult), [f6, f4b])
        g2 = P.s(lambda s: s.activation(
            s_vv[:], s_vv[:], AF.Identity, bias=s_eps8[:]),
            [g1, (m4b if i == 0 else None)])
        g3 = P.v(lambda v: v.reciprocal(s_vv[:], s_vv[:]), [g2])
        g4 = P.v(lambda v: v.tensor_tensor(
            s_c2[:], s_vv[:], s_r[:], ALU.mult), [g3, rowrd.get("c2")])
        b_m = P.t(lambda t: t.matmul(
            B[0][:, 0:512], lhsT=s_oner[:], rhs=s_m[:],
            start=True, stop=True), [f1, m1, prev_b0, rowrd.get("B0")])
        b_r = P.t(lambda t: t.matmul(
            B[1][:, 0:512], lhsT=s_oner[:], rhs=s_r[:],
            start=True, stop=True), [f6, prev_b0, rowrd.get("B1")])
        b_c = P.t(lambda t: t.matmul(
            B[2][:, 0:512], lhsT=s_oner[:], rhs=s_c2[:],
            start=True, stop=True), [g4, thc, rowrd.get("B2")])
        rowrd["m"] = b_m
        rowrd["r"] = b_r
        rowrd["c2"] = b_c
        rowrd["vv"] = g4
        t2l = t3l = None
        for j in range(NC):
            off = j * n + i * 512
            t1 = P.v(lambda v, _o=off: v.tensor_tensor(
                SB[:, _o:_o + 512], SB[:, _o:_o + 512], B[0][:, 0:512],
                ALU.subtract), [b_m, ytoks[j]])
            t2 = P.v(lambda v, _o=off: v.tensor_tensor(
                SA[:, _o:_o + 512], SB[:, _o:_o + 512], B[1][:, 0:512],
                ALU.mult), [t1, b_r])
            t3 = P.v(lambda v, _o=off, _j=j, _i=i: v.tensor_tensor(
                SCb[:, _j * 2048 + _i * 512:_j * 2048 + _i * 512 + 512],
                SB[:, _o:_o + 512], B[2][:, 0:512], ALU.mult),
                [t1, b_c,
                 (prev_sg[("mk", j)] if i == 0 and j < 2 else None)])
            x1toks[(i, j)] = t2
            xbtoks[(i, j)] = t3
            t2l, t3l = t2, t3
        rowrd["B0"] = t3l
        rowrd["B1"] = t2l
        rowrd["B2"] = t3l
    p3done = t3l
    bvc = P.v(lambda v: v.tensor_copy(s_bvb[:], s_bv), [lbv, p3done])
    snap1 = None
    if dump:
        P.sd(lambda s: s.dma_start(out=XD1, in_=SA[:]), [p3done], ch="dbg")
        snap1 = P.sd(lambda s: s.dma_start(
            out=XDH, in_=SCb[:, 0:NC * n]), [p3done], ch="dbg")
        P.sd(lambda s: s.dma_start(out=XDW, in_=WUP_out), [ccWu], ch="dbg")
        snap1 = P.sd(lambda s: s.dma_start(out=XDW2, in_=WDN_out),
                     [ccWd, snap1], ch="dbg")
        snap1 = P.total("syd", "dbg")

    awfree = {0: None, 1: None, 2: None, 3: None}
    fw = {"B01": rowrd["B1"], "B23": rowrd["B2"],
          "B4": ln1["s2"], "B5": war["B5rd"], "B6": ln1["s1"],
          "sq": ln1["sq"], "tmprd": (snap1 or ln1["sq"])}
    awfree[0] = awfree[1] = prev_ld[0]
    awfree[2] = awfree[3] = prev_ld[1]
    hwar = p3done
    ln2war = {"eps_dep": None, "sq": ln1["sq"], "trow": rowrd["trow"],
              "s1": None, "s2": None, "bc": None}
    x2toks = {}
    dnfree = {}
    sbcrd = {}
    for hh in range(2):
        tok0 = hh * H
        geltoks = {}
        stat_emit = []
        statA_last = statB_last = None
        upA_war = {0: fw["B01"], 1: fw["B23"]}
        for f in range(32):
            u = f % 4
            c, k = f // 4, f % 4
            ld = P.sd(lambda s, _u=u, _c=c, _k=k: s.dma_start(
                out=AWb[:, _u * 1024:(_u + 1) * 1024],
                in_=WUP_out[_c:_c + 1,
                            _k * 131072:(_k + 1) * 131072].rearrange(
                    "o (p m) -> (o p) m", p=128)),
                [ccWu if (hh == 0 and f < 4) else None, awfree[u],
                 (ln2war["bc"] if hh == 1 and f < 4 else None),
                 (ln2war["sq"] if hh == 1 and f < 4 else None)],
                ch=f"wu{u}")
            ldt = P.total("syd", f"wu{u}")
            pA = B[0 + 2 * (f % 2)]
            pB = B[1 + 2 * (f % 2)]
            mmA = mmB = None
            for j in range(NC):
                mmA = P.t(lambda t, _u=u, _j=j, _pA=pA, _t0=tok0: t.matmul(
                    _pA[:, 0:512],
                    lhsT=AWb[:, _u * 1024 + _j * 128:_u * 1024 + (_j + 1) * 128],
                    rhs=SCb[:, _j * 2048 + _t0:_j * 2048 + _t0 + 512],
                    start=(_j == 0), stop=(_j == NC - 1)),
                    [ldt, xbtoks[(2 * hh, j)], upA_war[f % 2]])
                mmB = P.t(lambda t, _u=u, _j=j, _pB=pB, _t0=tok0: t.matmul(
                    _pB[:, 0:512],
                    lhsT=AWb[:, _u * 1024 + _j * 128:_u * 1024 + (_j + 1) * 128],
                    rhs=SCb[:, _j * 2048 + _t0 + 512:_j * 2048 + _t0 + 1024],
                    start=(_j == 0), stop=(_j == NC - 1)),
                    [xbtoks[(2 * hh + 1, j)], mmA])
            awfree[u] = mmB
            if dump and hh == 1 and f == 0:
                P.sd(lambda s: s.dma_start(
                    out=XDAW, in_=AWb[:, 0:1024]), [mmB], ch="dbg")
                awfree[u] = P.sd(lambda s: s.dma_start(
                    out=XDH3, in_=SCb[:, 0:NC * n]), [mmB], ch="dbg")
            if stat_emit:
                statA_last, statB_last = stat_emit.pop()()
            gA = P.s(lambda s, _f=f, _pA=pA: s.activation(
                SBb[:, _f * 1024:_f * 1024 + 512], _pA[:, 0:512], AF.Gelu),
                [mmA, hwar])
            gB = P.s(lambda s, _f=f, _pB=pB: s.activation(
                SBb[:, _f * 1024 + 512:(_f + 1) * 1024], _pB[:, 0:512],
                AF.Gelu), [mmB])
            geltoks[f] = (gA, gB)
            upA_war[f % 2] = gB
            sqA = P.s(lambda s, _f=f: s.activation(
                s_tmpb[:, 0:512],
                SBb[:, _f * 1024:_f * 1024 + 512], AF.Square),
                [gA, fw["sq"], statA_last])
            sqB = P.s(lambda s, _f=f: s.activation(
                s_tmpb[:, 512:1024],
                SBb[:, _f * 1024 + 512:(_f + 1) * 1024], AF.Square),
                [sqA, statB_last])
            fw["sq"] = sqB

            def emit_stats(_f=f, _sqA=sqA, _sqB=sqB,
                           _sAp=statA_last, _sBp=statB_last,
                           _b4=fw["B4"], _b5=fw["B5"]):
                sA = P.t(lambda t: t.matmul(
                    B[4][0:1, 0:512], lhsT=s_onecb[:],
                    rhs=s_tmpb[:, 0:512],
                    start=(_f == 0), stop=(_f == 31)),
                    [_sqA, m2b, _sAp, (_b4 if _f == 0 else None)])
                sB = P.t(lambda t: t.matmul(
                    B[5][0:1, 0:512], lhsT=s_onecb[:],
                    rhs=s_tmpb[:, 512:1024],
                    start=(_f == 0), stop=(_f == 31)),
                    [_sqB, _sBp, (_b5 if _f == 0 else None)])
                return sA, sB
            stat_emit.append(emit_stats)
        statA_last, statB_last = stat_emit.pop()()
        fw["tmprd"] = statB_last
        fw["B01"] = upA_war[0]
        fw["B23"] = upA_war[1]
        hrA = P.v(lambda v: v.tensor_scalar(
            s_hr2[0:1, 0:512], B[4][0:1, 0:512], 1.0 / DFF, None,
            op0=ALU.mult), [statA_last, sbcrd.get("hr2"),
                            (prev_mb if hh == 0 else None)])
        hrB = P.v(lambda v: v.tensor_scalar(
            s_hr2[0:1, 512:1024], B[5][0:1, 0:512], 1.0 / DFF, None,
            op0=ALU.mult), [statB_last, hrA])
        h1 = P.s(lambda s: s.activation(s_hr2[:], s_hr2[:], AF.Sqrt),
                 [hrA, hrB])
        h2 = P.s(lambda s: s.activation(
            s_hr2[:], s_hr2[:], AF.Identity, bias=s_eps8[:]), [h1])
        h3 = P.v(lambda v: v.reciprocal(s_hr2[:], s_hr2[:]), [h2])
        b_sA = P.t(lambda t: t.matmul(
            B[4][:, 0:512], lhsT=s_oner[:], rhs=s_hr2[0:1, 0:512],
            start=True, stop=True), [h3, m1, hrA])
        b_sB = P.t(lambda t: t.matmul(
            B[5][:, 0:512], lhsT=s_oner[:], rhs=s_hr2[0:1, 512:1024],
            start=True, stop=True), [h3, hrB])
        cpA = P.v(lambda v: v.tensor_copy(SE[:, 0:512], B[4][:, 0:512]),
                  [b_sA, sbcrd.get("sbc"), (pj if hh == 0 else None)])
        cpB = P.v(lambda v: v.tensor_copy(SE[:, 512:1024], B[5][:, 0:512]),
                  [b_sB, cpA])
        fw["B4"] = cpA
        fw["B5"] = cpB
        sbcrd["hr2"] = b_sB
        ztoksA = {}
        ztoksB = {}
        for cj in range(8):
            ldq = {}
            for q in range(2):
                r = q
                ldq[q] = P.sd(lambda s, _cj=cj, _q=q, _r=r: s.dma_start(
                    out=AWb[:, _r * 2048:(_r + 1) * 2048].rearrange(
                        "p (j m) -> p j m", j=2),
                    in_=WDN_out[_cj:_cj + 1,
                                (2 * _q) * 131072:
                                (2 * _q + 2) * 131072].rearrange(
                        "o (j p m) -> (o p) j m", j=2, p=128, m=1024)),
                    [ccWd if (hh == 0 and cj == 0) else None,
                     awfree[2 * r], awfree[2 * r + 1],
                     dnfree.get((hh, cj - 1, r))],
                    ch=f"wd{r}")
            pA = B[0 + 2 * (cj % 2)]
            pB = B[1 + 2 * (cj % 2)]
            mmA = mmB = None
            for f in range(32):
                r = f // 16
                lt = P.total("syd", f"wd{r}") if f % 16 == 0 else None
                mmA = P.t(lambda t, _f=f, _r=r, _pA=pA, _lt=lt: t.matmul(
                    _pA[:, 0:512],
                    lhsT=AWb[:, _r * 2048 + (_f % 16) * 128:
                             _r * 2048 + (_f % 16 + 1) * 128],
                    rhs=SBb[:, _f * 1024:_f * 1024 + 512],
                    start=(_f == 0), stop=(_f == 31)),
                    [lt, geltoks[f][0],
                     (fw.get("B01" if cj % 2 == 0 else "B23")
                      if f == 0 else None)])
                mmB = P.t(lambda t, _f=f, _pB=pB: t.matmul(
                    _pB[:, 0:512],
                    lhsT=AWb[:, (_f // 16) * 2048 + (_f % 16) * 128:
                             (_f // 16) * 2048 + (_f % 16 + 1) * 128],
                    rhs=SBb[:, _f * 1024 + 512:(_f + 1) * 1024],
                    start=(_f == 0), stop=(_f == 31)),
                    [geltoks[f][1], mmA])
                if f == 15:
                    dnfree[(hh, cj, 0)] = mmB
                if f == 31:
                    dnfree[(hh, cj, 1)] = mmB
            tA = P.v(lambda v, _pA=pA: v.tensor_tensor(
                s_tmp[:, 0:512], _pA[:, 0:512], SE[:, 0:512],
                ALU.mult), [mmB, cpA, fw.get("tmp"), fw["tmprd"]])
            zA = P.v(lambda v, _cj=cj, _t0=tok0: v.tensor_tensor(
                SA[:, _cj * n + _t0:_cj * n + _t0 + 512],
                s_tmp[:, 0:512],
                SA[:, _cj * n + _t0:_cj * n + _t0 + 512], ALU.add), [tA])
            tB = P.v(lambda v, _pB=pB: v.tensor_tensor(
                s_tmp[:, 0:512], _pB[:, 0:512], SE[:, 512:1024],
                ALU.mult), [mmB, cpB, zA])
            zB = P.v(lambda v, _cj=cj, _t0=tok0: v.tensor_tensor(
                SA[:, _cj * n + _t0 + 512:_cj * n + _t0 + 1024],
                s_tmp[:, 0:512],
                SA[:, _cj * n + _t0 + 512:_cj * n + _t0 + 1024], ALU.add),
                [tB])
            fw["tmp"] = zB
            if cj % 2 == 0:
                fw["B01"] = zB
            else:
                fw["B23"] = zB
            ztoksA[cj] = zA
            ztoksB[cj] = zB
        sbcrd["sbc"] = fw["tmp"]
        ln2war["sq"] = ztoksB[7]
        ln2war["bc"] = ztoksB[7]
        for ts in range(2):
            gi = hh * 2 + ts
            ytl = [ztoksA[cj] if ts == 0 else ztoksB[cj] for cj in range(8)]
            ln2war["eps_dep"] = None
            outt = emit_ln(P, n, gi, 512, SA, SA, s_m, s_r, s_tmp, s_trow,
                           s_onec, s_oner, B[6][0:1, :], B[4][0:1, :],
                           B[5], B[2], ytl, ln2war, s_eps=s_eps,
                           s_onecb=s_onecb)
            x2toks[gi] = outt
        fw["sq"] = ln2war["sq"]
        fw["B4"] = ln2war["s2"]
        fw["B5"] = ln2war["bc"]
        fw["B6"] = ln2war["s1"]
        fw["B23"] = ln2war["bc"]
        fw["tmprd"] = ln2war["sq"]
        fw["tmp"] = ln2war["sq"]
        sbcrd["hr2"] = ln2war["s2"]
        sbcrd["sbc"] = ln2war["bc"]
        hwar = None
        if hh == 0:
            hwar = ln2war["bc"]
            upA_war = {0: ln2war["bc"], 1: ln2war["bc"]}
            for u in range(4):
                awfree[u] = dnfree[(0, 7, u // 2)]
    p4done = x2toks[3][-1]
    snap2 = None
    if dump:
        snap2 = P.sd(lambda s: s.dma_start(out=XD2, in_=SA[:]), [p4done],
                     ch="dbg")
        P.sd(lambda s: s.dma_start(out=XDHH, in_=SBb[:, 0:NFF * 1024]),
             [p4done], ch="dbg")
        P.sd(lambda s: s.dma_start(out=XDSB, in_=SE[:, 0:1024]),
             [p4done], ch="dbg")
        p4done = P.total("syd", "dbg")

    awf = [dnfree[(1, 7, 0)], dnfree[(1, 7, 1)]]
    for wi, (widx0, dst, sbias) in enumerate(((0, 0, s_bq), (2, KOFF, s_bk))):
        for half in range(2):
            wln = P.gd(lambda g, _w=widx0 + half: g.dma_start(
                out=AWb[:, 0:NC * 512].rearrange("p (a m) -> p a m", a=NC),
                in_=WQKVC[_w:_w + 1].rearrange("o p a m -> (o p) a m")),
                [war.get("awfree2"), p4done] + awf, ch="wqk")
            awf = []
            wlnt = P.total("gpd", "wqk")
            for i in range(nsub):
                casts = []
                for j in range(NC):
                    off = j * n + i * 512
                    casts.append(P.v(lambda v, _o=off, _j=j: v.tensor_copy(
                        SEb[:, _j * 512:(_j + 1) * 512], SA[:, _o:_o + 512]),
                        [x2toks[i][j], war.get("x2b")]))
                for hq in range(4):
                    hp = half * 4 + hq
                    mmt = None
                    for j in range(NC):
                        mmt = P.t(lambda t, _j=j, _hq=hq: t.matmul(
                            B[_hq % 2][:, 0:512],
                            lhsT=AWb[:, _j * 512 + _hq * 128:
                                     _j * 512 + _hq * 128 + 128],
                            rhs=SEb[:, _j * 512:(_j + 1) * 512],
                            start=(_j == 0), stop=(_j == NC - 1)),
                            [wlnt, casts[j], war.get(("pqk", hq % 2)),
                             (fw["B01"] if wi == 0 and half == 0 and i == 0
                              and hq < 2 else None)])
                    ev = P.s(lambda s, _i=i, _hp=hp, _hq=hq, _d=dst,
                             _sb=sbias: s.activation(
                        SBb[:, _d + _hp * n + _i * 512:
                            _d + _hp * n + _i * 512 + 512],
                        B[_hq % 2][:, 0:512], AF.Identity,
                        bias=_sb[:, _hp:_hp + 1]), [mmt, lall, p4done])
                    war[("pqk", hq % 2)] = ev
                    war[("qk", wi, hp)] = ev
                war["x2b"] = war[("pqk", 1)]
            war["awfree2"] = war[("pqk", 1)]
    for half in range(2):
        wlv = P.gd(lambda g, _h=half: g.dma_start(
            out=AWb[:, 0:NC * 512].rearrange("p (a m) -> p a m", a=NC),
            in_=WQKVC[4 + _h:5 + _h].rearrange("o p a m -> (o p) a m")), [war["awfree2"]], ch="wv")
        wlvt = P.total("gpd", "wv")
        for i in range(nsub):
            casts = []
            for j in range(NC):
                off = j * n + i * 512
                casts.append(P.v(lambda v, _o=off, _j=j: v.tensor_copy(
                    SEb[:, _j * 512:(_j + 1) * 512], SA[:, _o:_o + 512]),
                    [x2toks[i][j], war.get("x2b")]))
            for r in range(4):
                tt = i * 4 + r
                mmt = None
                for j in range(NC):
                    mmt = P.t(lambda t, _j=j, _r=r: t.matmul(
                        B[2][:, 0:512],
                        lhsT=SEb[:, _j * 512 + _r * 128:
                                 _j * 512 + _r * 128 + 128],
                        rhs=AWb[:, _j * 512:(_j + 1) * 512],
                        start=(_j == 0), stop=False),
                        [wlvt, casts[j], war.get("pv"),
                         (fw["B23"] if half == 0 and i == 0 and r == 0
                          else None)])
                mmt = P.t(lambda t, _h=half: t.matmul(
                    B[2][:, 0:512], lhsT=s_onerb[:],
                    rhs=s_bvb[0:1, _h * 512:(_h + 1) * 512],
                    start=False, stop=True), [mmt, m3b, bvc])
                vx = P.v(lambda v, _tt=tt, _h=half: v.tensor_copy(
                    SCb[:, _tt * 1024 + _h * 512:
                        _tt * 1024 + (_h + 1) * 512],
                    B[2][:, 0:512]), [mmt, war.get("vxw"), p4done])
                war["pv"] = vx
                war["vxw"] = vx
                war[("vx", tt, half)] = vx
            war["x2b"] = war["pv"]
        war["awfree2"] = war["pv"]
    LN16 = float(np.log(16.0))
    phs = [B[0][:, 0:256], B[1][:, 0:256], B[4][:, 0:256], B[5][:, 0:256]]
    exs = [s_p56[:, g * 256:(g + 1) * 256] for g in range(4)]
    pks4 = [SEb[:, t * 256:(t + 1) * 256] for t in range(4)]
    kvq = []
    kvlast = {}
    for h in range(HEADS):
        hp, hb = h // 2, (h % 2) * 64
        vhalf, hv = h // 8, h % 8
        for tt in range(ntile):
            g = tt % 4
            rr = tt * 128
            xp = P.t(lambda t, _hp=hp, _hb=hb, _rr=rr, _h=h, _g=g:
                     t.matmul(
                phs[_g],
                lhsT=SBb[_hb:_hb + 64,
                         KOFF + _hp * n + _rr:KOFF + _hp * n + _rr + 128],
                rhs=s_pj[_hb:_hb + 64, _h * 256:(_h + 1) * 256],
                start=True, stop=True),
                [war[("qk", 1, hp)], pj, war.get(("xpfree", g)),
                 (war[("qk", 1, 3)] if h == 0 and tt < 4 else None),
                 (ln2war["bc"] if h == 0 and tt < 4 else None)])
            rm = P.v(lambda v, _g=g: v.reduce_max(
                s_nmx4[:, _g:_g + 1], phs[_g], axis=AX.X,
                negate=True), [xp, war.get(("nmxrd", g))])
            ba = P.v(lambda v, _g=g: v.tensor_scalar(
                s_nmx4[:, _g:_g + 1], s_nmx4[:, _g:_g + 1], -LN16, None,
                op0=ALU.add), [rm])
            ex = P.s(lambda s, _g=g: s.activation(
                exs[_g], phs[_g], AF.Exp, bias=s_nmx4[:, _g:_g + 1]),
                [ba, war.get(("exrd", g))])
            war[("xpfree", g)] = ex
            war[("nmxrd", g)] = ex
            ph = P.v(lambda v, _g=g: v.tensor_scalar(
                pks4[_g], exs[_g], PHI_EPS / 16.0, None,
                op0=ALU.add), [ex, war.get(("pkrd", g))])
            war[("exrd", g)] = ph

            def mk_kv(_h=h, _tt=tt, _g=g, _ph=ph, _vh=vhalf, _hv=hv):
                vsl = SCb[:, _tt * 1024 + _vh * 512 + _hv * 64:
                          _tt * 1024 + _vh * 512 + _hv * 64 + 64]
                kv0 = P.t(lambda t: t.matmul(
                    B[6][:, 0:64], lhsT=pks4[_g][:, 0:128], rhs=vsl,
                    start=(_tt == 0), stop=False),
                    [_ph, war[("vx", _tt, _vh)],
                     (war.get("kv0ev") if _tt == 0 else None),
                     (ln2war["s1"] if _h == 0 and _tt == 0 else None)])
                kv0 = P.t(lambda t: t.matmul(
                    B[6][:, 64:65], lhsT=pks4[_g][:, 0:128], rhs=s_onecb[:],
                    start=False, stop=(_tt == ntile - 1)), [kv0, m2b])
                kv1 = P.t(lambda t: t.matmul(
                    B[3][:, 0:64], lhsT=pks4[_g][:, 128:256], rhs=vsl,
                    start=(_tt == 0), stop=False),
                    [_ph, war[("vx", _tt, _vh)],
                     (war.get("kv1ev") if _tt == 0 else None)])
                kv1 = P.t(lambda t: t.matmul(
                    B[3][:, 64:65], lhsT=pks4[_g][:, 128:256],
                    rhs=s_onecb[:],
                    start=False, stop=(_tt == ntile - 1)), [kv1, m2b])
                war[("pkrd", _g)] = kv1
                if _tt == ntile - 1:
                    kvlast[_h] = (kv0, kv1)
            kvq.append(mk_kv)
            if tt == ntile - 1:
                def mk_e(_h=h):
                    e1 = P.v(lambda v: v.tensor_copy(
                        SD[:, _h * 65:(_h + 1) * 65], B[6][:, 0:65]),
                        [kvlast[_h][0], kvlast[_h][1],
                         (p4done if _h == 0 else None),
                         (prev_mb if _h == 0 else None)])
                    e2 = P.v(lambda v: v.tensor_copy(
                        SD[:, 1040 + _h * 65:1040 + (_h + 1) * 65],
                        B[3][:, 0:65]), [e1])
                    war["kv0ev"] = e1
                    war["kv1ev"] = e2
                    war[("kvh", _h)] = e2
                kvq.append(mk_e)
            while len(kvq) > 2:
                kvq.pop(0)()
    while kvq:
        kvq.pop(0)()

    dkv = P.gd(lambda gg: gg.dma_start(out=kv_in, in_=SD[:, 0:2080]),
               [war[("kvh", hh)] for hh in range(HEADS)], ch="kvio")
    cc3 = P.col(lambda gg: gg.collective_compute(
        "AllReduce", ALU.add, replica_groups=[[0, 1, 2, 3], [4, 5, 6, 7]],
        ins=[kv_in.opt()], outs=[kv_out.opt()]), [P.total("gpd", "kvio")])
    P.gd(lambda gg: gg.dma_start(out=SD[:, 0:2080], in_=kv_out),
         [cc3], ch="kvb")
    lkvb = P.total("gpd", "kvb")

    if dump:
        P.sd(lambda s: s.dma_start(out=XDKV, in_=kv_out), [cc3], ch="dbg")
    pw = {"eps_dep": None, "sq": ln2war["sq"], "trow": ln2war["trow"]}
    if dump:
        for cj in range(NC):
            pw[("zr", cj)] = snap2
    war["pTread"] = war["x2b"]
    x3toks = {}
    for i in range(nsub):
        ndtoks = []
        for h in range(HEADS):
            hp, hb = h // 2, (h % 2) * 64
            nh = h % 2
            ks = s_kvs[:, nh * 132:nh * 132 + 132]
            pn = [B[0], B[2]][nh]
            pdb = [B[1], B[3]][nh]
            kc = P.v(lambda v, _h=h, _ks=ks: v.tensor_copy(
                _ks[:, 0:65], SD[:, _h * 65:(_h + 1) * 65]),
                [lkvb, war.get(("kvsread", nh))])
            kc2 = P.v(lambda v, _h=h, _ks=ks: v.tensor_copy(
                _ks[:, 65:130],
                SD[:, 1040 + _h * 65:1040 + (_h + 1) * 65]), [kc])
            trtoks = []
            for r in range(4):
                n0 = i * 512 + r * 128
                xq = P.t(lambda t, _hp=hp, _hb=hb, _n0=n0, _r=r, _h=h:
                         t.matmul(
                    B[4 + _r % 2][:, 0:256],
                    lhsT=SBb[_hb:_hb + 64,
                             _hp * n + _n0:_hp * n + _n0 + 128],
                    rhs=s_pj[_hb:_hb + 64, _h * 256:(_h + 1) * 256],
                    start=True, stop=True), [pj, war[("qk", 0, hp)],
                                             war.get(("xpk", r % 2))])
                par = r % 2
                rm = P.v(lambda v, _r=r, _p=par: v.reduce_max(
                    s_nmx2[_p][:], B[4 + _r % 2][:, 0:256], axis=AX.X,
                    negate=True), [xq, war.get(("nmxread", par))])
                ex = P.s(lambda s, _r=r, _p=par: s.activation(
                    s_exp2[_p][:], B[4 + _r % 2][:, 0:256], AF.Exp,
                    bias=s_nmx2[_p][:]), [rm, war.get(("exread", par))])
                war[("xpk", r % 2)] = ex
                war[("nmxread", par)] = ex
                phq = P.v(lambda v, _p=par: v.tensor_scalar(
                    s_phik2[_p][:], s_exp2[_p][:], 1.0 / 16.0,
                    PHI_EPS / 16.0, op0=ALU.mult, op1=ALU.add),
                    [ex, war.get(("phikread", par))])
                war[("exread", par)] = phq
                t1 = P.t(lambda t, _p=par: t.transpose(
                    B7[:, _p * 256:_p * 256 + 128], s_phik2[_p][:, 0:128],
                    s_idb[:]), [phq, lall, war.get(("trread", par))])
                t2 = P.t(lambda t, _p=par: t.transpose(
                    B7[:, _p * 256 + 128:_p * 256 + 256],
                    s_phik2[_p][:, 128:256], s_idb[:]), [t1])
                c1t = P.v(lambda v, _r=r, _p=par: v.tensor_copy(
                    s_phiqT[0][:, _r * 128:(_r + 1) * 128],
                    B7[:, _p * 256:_p * 256 + 128]), [t2, war.get("pTread")])
                c2t = P.v(lambda v, _r=r, _p=par: v.tensor_copy(
                    s_phiqT[1][:, _r * 128:(_r + 1) * 128],
                    B7[:, _p * 256 + 128:_p * 256 + 256]), [c1t])
                war[("phikread", par)] = c2t
                war[("trread", par)] = c2t
                trtoks.append(c2t)
            nm = P.t(lambda t, _ks=ks, _pn=pn: t.matmul(
                _pn[0:65, 0:512], lhsT=_ks[:, 0:65],
                rhs=s_phiqT[0][:], start=True, stop=False),
                trtoks + [kc2, war.get(("numread", nh))])
            nm = P.t(lambda t, _ks=ks, _pn=pn: t.matmul(
                _pn[0:65, 0:512],
                lhsT=_ks[:, 65:130],
                rhs=s_phiqT[1][:], start=False, stop=True), [nm])
            war["pTread"] = nm
            war[("kvsread", nh)] = nm
            d1 = P.s(lambda s, _pn=pn: s.activation(
                s_den[:], _pn[64:65, 0:512], AF.Ln, bias=s_epsphi[:]),
                [nm, m4c, war.get("denread")])
            d2 = P.s(lambda s: s.activation(
                s_den[:], s_den[:], AF.Exp, scale=-1.0), [d1])
            dcb = P.v(lambda v: v.tensor_copy(s_denb[:], s_den[:]),
                      [d2, war.get("denbrd")])
            db = P.t(lambda t, _pdb=pdb: t.matmul(
                _pdb[:, 0:512], lhsT=s_onerb[:], rhs=s_denb[:],
                start=True, stop=True), [dcb, m3b,
                                         war.get(("dbcread", nh))])
            dc = P.v(lambda v, _pdb=pdb: v.tensor_copy(
                s_dbc2[nh][:], _pdb[:, 0:512]), [db])
            war["denread"] = dc
            war["denbrd"] = db
            nd = P.v(lambda v, _hp=hp, _hb=hb, _i=i, _pn=pn: v.tensor_tensor(
                SCb[_hb:_hb + 64,
                    _hp * n + _i * 512:_hp * n + _i * 512 + 512],
                _pn[0:64, 0:512], s_dbc2[nh][_hb:_hb + 64, :], ALU.mult),
                [dc, nm, war.get("ndw")])
            war[("numread", nh)] = nd
            war[("dbcread", nh)] = nd
            ndtoks.append(nd)
        ztoks = []
        for whalf in range(2):
            wlo = P.gd(lambda g, _h=whalf: g.dma_start(
                out=AWb[:, 0:NC * 512].rearrange("p (a m) -> p a m", a=NC),
                in_=WOC[_h:_h + 1].rearrange("o p a m -> (o p) a m")),
                [war.get("awfree2"), pw.get("wofree")], ch="wo")
            wlot = P.total("gpd", "wo")
            for cj4 in range(4):
                cj = whalf * 4 + cj4
                mmt = None
                for j in range(NC):
                    mmt = P.t(lambda t, _c4=cj4, _j=j, _i=i,
                              _lt=wlot: t.matmul(
                        B[2 + _c4 % 2][:, 0:512],
                        lhsT=AWb[:, _j * 512 + _c4 * 128:
                                 _j * 512 + (_c4 + 1) * 128],
                        rhs=SCb[:, _j * n + _i * 512:
                                _j * n + _i * 512 + 512],
                        start=(_j == 0), stop=(_j == NC - 1)),
                        ndtoks + [wlot, pw.get(("o", cj4 % 2))])
                pw["wofree"] = mmt
                ev = P.s(lambda s, _c=cj, _c4=cj4: s.activation(
                    s_tmp[:, 0:512], B[2 + _c4 % 2][:, 0:512], AF.Identity,
                    bias=s_bo[:, _c:_c + 1]), [mmt, lall, pw.get("tread")])
                zt = P.v(lambda v, _c=cj, _i=i: v.tensor_tensor(
                    SA[:, _c * n + _i * 512:_c * n + _i * 512 + 512],
                    s_tmp[:, 0:512],
                    SA[:, _c * n + _i * 512:_c * n + _i * 512 + 512],
                    ALU.add), [ev, pw.get(("zr", cj))])
                pw[("o", cj4 % 2)] = zt
                pw["tread"] = zt
                ztoks.append(zt)
        war["awfree2"] = pw["wofree"]
        pw["sq"] = ztoks[-1]
        pw["eps_dep"] = None
        outt = emit_ln(P, n, i, 512, SA, SA, s_m, s_r, s_tmp, s_trow,
                       s_onec, s_oner, B[4][0:1, :], B[5][0:1, :], B[6],
                       B[1], ztoks, pw, s_eps=s_eps, s_onecb=s_onecb)
        pw["tread"] = pw["sq"]
        for cj in range(NC):
            pw[("zr", cj)] = outt[-1]
        war[("xpk", 0)] = outt[-1]
        war[("xpk", 1)] = outt[-1]
        if i + 1 < nsub:
            war["ndw"] = outt[-1]
        x3toks[i] = outt

    x3done = x3toks[nsub - 1][-1]
    if dump:
        snap3 = P.sd(lambda s: s.dma_start(out=XD3, in_=SA[:]), [x3done],
                     ch="dbg")
        dwseed = {("zr", cj): snap3 for cj in range(NC)}
    else:
        dwseed = {}
    P.gd(lambda g: g.dma_start(
        out=s_winb[:].rearrange("p (a m) -> p a m", a=NC), in_=WINC),
        [lkvb], ch="wsm")
    P.gd(lambda g: g.dma_start(out=s_woutb[:], in_=WOUT), [lkvb],
         ch="wsm")
    lwsm = P.total("gpd", "wsm")
    dw = {"eps_dep": None, "sq": pw.get("sq"), "trow": pw.get("trow")}
    dw.update(dwseed)
    stoks = []
    for i in range(nsub):
        casts = []
        for j in range(NC):
            off = j * n + i * 512
            casts.append(P.v(lambda v, _o=off, _j=j: v.tensor_copy(
                SEb[:, _j * 512:(_j + 1) * 512], SA[:, _o:_o + 512]),
                [x3toks[i][j], dw.get("x3b"), war["pTread"]]))
        mmt = None
        for j in range(NC):
            mmt = P.t(lambda t, _j=j: t.matmul(
                B[6][0:64, 0:512], lhsT=s_winb[:, _j * 64:(_j + 1) * 64],
                rhs=SEb[:, _j * 512:(_j + 1) * 512],
                start=(_j == 0), stop=(_j == NC - 1)),
                [lwsm, casts[j], dw.get("psread"),
                 war.get(("numread", 0)), war.get(("numread", 1))])
        sv = P.s(lambda s: s.activation(
            s_s[:], B[6][0:64, 0:512], AF.Identity, bias=s_bin[:]),
            [mmt, lall, dw.get("sread")])
        dw["psread"] = sv
        gt = None
        for ii in range(16):
            hi = P.s(lambda s, _ii=ii: s.activation(
                s_hi[:], s_s[:], AF.Relu,
                scale=s_w1[:, _ii:_ii + 1], bias=s_b1[:, _ii:_ii + 1]),
                [sv, lall, dw.get("hiread"), dw.get("hi_war"),
                 war.get(("kvsread", 1))])
            if ii == 0:
                gt = P.v(lambda v: v.tensor_scalar(
                    s_g[:], s_hi[:], s_w2[:, 0:1], None, op0=ALU.mult),
                    [hi, dw.get("gread")])
            else:
                gt = P.v(lambda v, _ii=ii: v.scalar_tensor_tensor(
                    s_g[:], s_hi[:], s_w2[:, _ii:_ii + 1], s_g[:],
                    op0=ALU.mult, op1=ALU.add), [hi, gt])
            dw["hiread"] = gt
        dw["sread"] = gt
        g2 = P.v(lambda v: v.tensor_scalar(
            s_g[:], s_g[:], s_b2[:], None, op0=ALU.add), [gt])
        gb = P.v(lambda v: v.tensor_copy(s_gb[:], s_g[:]),
                 [g2, dw.get("gbread")])
        dw["gread"] = gb
        ztoks = []
        for cj in range(NC):
            mmt = P.t(lambda t, _c=cj: t.matmul(
                B[2 + _c % 2][:, 0:512],
                lhsT=s_woutb[:, _c * 128:(_c + 1) * 128],
                rhs=s_gb[:], start=True, stop=True),
                [gb, lwsm, dw.get(("o", cj % 2))])
            ev = P.s(lambda s, _c=cj: s.activation(
                s_tmp[:, 0:512], B[2 + _c % 2][:, 0:512], AF.Identity,
                bias=s_bout[:, _c:_c + 1]), [mmt, dw.get("tread")])
            zt = P.v(lambda v, _c=cj, _i=i: v.tensor_tensor(
                SA[:, _c * n + _i * 512:_c * n + _i * 512 + 512],
                s_tmp[:, 0:512],
                SA[:, _c * n + _i * 512:_c * n + _i * 512 + 512], ALU.add),
                [ev, dw.get(("zr", cj)), gb])
            dw[("o", cj % 2)] = zt
            dw["tread"] = zt
            ztoks.append(zt)
        dw["gbread"] = ztoks[-1]
        dw["x3b"] = ztoks[-1]
        dw["sq"] = ztoks[-1]
        outt = emit_ln(P, n, i, 512, SA, SA, s_m, s_r, s_tmp, s_trow,
                       s_onec, s_oner, B[4][0:1, :], B[5][0:1, :], B[0],
                       B[1], ztoks, dw, s_eps=s_eps, s_onecb=s_onecb)
        dw["tread"] = dw["sq"]
        stok = P.sd(lambda s, _i=i: s.dma_start(
            out=fmview(YT, n)[:, :, _i * 512:_i * 512 + 512],
            in_=SA[:].rearrange("p (a m) -> p a m", a=NC)[
                :, :, _i * 512:_i * 512 + 512]), outt, ch="st")
        for cj in range(NC):
            dw[("zr", cj)] = stok
        dw["hi_war"] = stok
        stoks.append(stok)
    if dump:
        stoks = stoks + [P.total("syd", "dbg")]
    P.finish(stoks)
    P.build()
    return nc


_V2_NC = None


def kernel(L0_attn_x, L0_attn_router_w, L0_attn_router_b, L0_attn_wq,
           L0_attn_wk, L0_attn_wv, L0_ffn_w_up, L0_ffn_w_down, L1_attn_w_qkv,
           L1_attn_b_qkv, L1_attn_w_out, L1_attn_b_out, L1_attn_proj_matrix,
           L1_ffn_Win, L1_ffn_bin_, L1_ffn_W1, L1_ffn_b1, L1_ffn_W2,
           L1_ffn_b2, L1_ffn_Wout, L1_ffn_bout):
    _ = (L0_attn_wq.shape, L0_attn_wk.shape, L0_attn_wv.shape)
    global _V2_NC
    if _V2_NC is None:
        _V2_NC = build_v2(NTOK)
    nc = _V2_NC
    x = np.ascontiguousarray(L0_attn_x, dtype=np.float32)
    identb = np.eye(128, dtype=np.float32).astype(ml_dtypes.bfloat16)
    bqkv = L1_attn_b_qkv
    wup = np.asarray(L0_ffn_w_up)
    wdn = np.asarray(L0_ffn_w_down)
    wqkv = np.asarray(L1_attn_w_qkv)
    wo = np.asarray(L1_attn_w_out)
    shared = {
        "rwc": np.ascontiguousarray(L0_attn_router_w.reshape(NC, 128).T),
        "rb": np.array([[L0_attn_router_b[0]]], dtype=np.float32),
        "identb": identb,
        "wqkvc": np.ascontiguousarray(
            wqkv.reshape(8, 128, 6, 512).transpose(2, 1, 0, 3)),
        "woc": np.ascontiguousarray(
            wo.reshape(8, 128, 2, 512).transpose(2, 1, 0, 3)),
        "bq": np.ascontiguousarray(bqkv[0:D].reshape(NC, 128).T),
        "bk": np.ascontiguousarray(bqkv[D:2 * D].reshape(NC, 128).T),
        "bvrow": np.ascontiguousarray(bqkv[2 * D:3 * D][None, :]),
        "projt": np.ascontiguousarray(np.concatenate(
            [L1_attn_proj_matrix[h].T for h in range(HEADS)], axis=1)),
        "bo": np.ascontiguousarray(L1_attn_b_out.reshape(NC, 128).T),
        "winc": np.ascontiguousarray(
            np.asarray(L1_ffn_Win).reshape(8, 128, 64).transpose(1, 0, 2)),
        "bin": np.ascontiguousarray(L1_ffn_bin_[:, None]),
        "w1": np.ascontiguousarray(L1_ffn_W1),
        "b1": np.ascontiguousarray(L1_ffn_b1),
        "w2": np.ascontiguousarray(L1_ffn_W2),
        "b2": np.ascontiguousarray(L1_ffn_b2[:, None]),
        "wout": np.ascontiguousarray(L1_ffn_Wout),
        "bout": np.ascontiguousarray(L1_ffn_bout.reshape(NC, 128).T),
    }
    wupc_full = np.ascontiguousarray(
        wup.reshape(32, 128, 8, 128).transpose(0, 3, 2, 1)
        .reshape(32, 128, 1024))
    wdnc_full = np.ascontiguousarray(
        wdn.reshape(8, 128, 32, 128).transpose(0, 3, 2, 1)
        .reshape(8, 128, 4096))
    maps = []
    for core in range(8):
        b, o = core // 4, (core % 4) * NTOK
        m = dict(shared)
        m["xt"] = np.ascontiguousarray(x[b, o:o + NTOK].T)
        m["xrevt"] = np.ascontiguousarray(x[b][::-1][o:o + NTOK].T)
        m["wups"] = np.ascontiguousarray(wupc_full[4 * core:4 * core + 4])
        m["wdns"] = np.ascontiguousarray(wdnc_full[core])
        maps.append(m)
    res = run_bass_kernel_spmd(nc, maps, core_ids=list(range(8)))
    out = np.zeros((2, NSEQ, D), np.float32)
    for core in range(8):
        b, o = core // 4, (core % 4) * NTOK
        out[b, o:o + NTOK] = res.results[core]["yt"].T
    return out


# version 16
# speedup vs baseline: 1.1015x; 1.0017x over previous
import numpy as np
import ml_dtypes
from contextlib import ExitStack

import concourse.bass as bass
import concourse.mybir as mybir
from concourse.bass_utils import run_bass_kernel_spmd

F32 = mybir.dt.float32
BF16 = mybir.dt.bfloat16
I32 = mybir.dt.int32
AF = mybir.ActivationFunctionType
ALU = mybir.AluOpType
AX = mybir.AxisListType

D = 1024
NC = 8
DFF = 4096
NFF = 32
NTOK = 2048
NSEQ = 8192
HEADS = 16
HDIM = 64
LN_EPS = 1e-5
RMS_EPS = 1e-8
PHI_EPS = 1e-6
DATA_NORM = float(np.float32(64.0 ** -0.25))
NITER_BISECT = 42
ROUTER_K = 1024.0
KOFF = 16384


class Prog:
    SECTIONS = {
        "tensor": ("pe", 1),
        "vector": ("ve", 1),
        "scalar": ("sc", 1),
        "gpsimd": ("gp", 1),
    }

    def __init__(self, nc):
        self.nc = nc
        self.es = ExitStack()
        self.items = {s: [] for s in
                      ("tensor", "vector", "scalar", "gpsimd", "sync")}
        self.counts = {}
        self.n_cc = 0
        self.final_deps = []

    def _add(self, section, sem, inc, fn, deps):
        n = self.counts.get(sem, 0) + inc
        self.counts[sem] = n
        self.items[section].append((fn, [d for d in deps if d is not None],
                                    sem, inc))
        return (sem, n)

    def t(self, fn, deps=()):
        return self._add("tensor", "pe", 1, fn, deps)

    def v(self, fn, deps=()):
        return self._add("vector", "ve", 1, fn, deps)

    def s(self, fn, deps=()):
        return self._add("scalar", "sc", 1, fn, deps)

    def g(self, fn, deps=()):
        return self._add("gpsimd", "gp", 1, fn, deps)

    def sd(self, fn, deps=(), ch="ld"):
        return self._add("sync", f"syd_{ch}", 16, fn, deps)

    def ad(self, fn, deps=(), ch="ld"):
        return self._add("scalar", f"scd_{ch}", 16, fn, deps)

    def gd(self, fn, deps=(), ch="ld"):
        return self._add("gpsimd", f"gpd_{ch}", 16, fn, deps)

    def total(self, queue, ch):
        name = f"{queue}_{ch}"
        return (name, self.counts.get(name, 0))

    def col(self, fn, deps=()):
        self.n_cc += 1
        sem = f"cc{self.n_cc}"
        return self._add("gpsimd", sem, 1, fn, deps)

    def finish(self, deps):
        self.final_deps = list(deps)

    def build(self):
        nc = self.nc
        sems = {}
        for name in self.counts:
            sems[name] = self.es.enter_context(nc.semaphore(name))
        block = self.es.enter_context(nc.Block())
        hw = {}

        def coalesce(deps):
            best = {}
            for dsem, dval in deps:
                if best.get(dsem, 0) < dval:
                    best[dsem] = dval
            return best.items()

        def emit(section, eng):
            for fn, deps, sem, inc in self.items[section]:
                for dsem, dval in coalesce(deps):
                    key = (section, dsem)
                    if hw.get(key, 0) < dval:
                        hw[key] = dval
                        eng.wait_ge(sems[dsem], dval)
                ins = fn(eng)
                ins.then_inc(sems[sem], inc)
            if section == "sync":
                for dsem, dval in coalesce(self.final_deps):
                    key = (section, dsem)
                    if hw.get(key, 0) < dval:
                        hw[key] = dval
                        eng.wait_ge(sems[dsem], dval)

        if self.items["tensor"]:
            block.tensor(lambda t: emit("tensor", t))
        if self.items["vector"]:
            block.vector(lambda v: emit("vector", v))
        if self.items["scalar"]:
            block.scalar(lambda s: emit("scalar", s))
        if self.items["gpsimd"]:
            block.gpsimd(lambda g: emit("gpsimd", g))
        block.sync(lambda s: emit("sync", s))
        return self


def sb(nc, es, name, shape, dt=F32):
    return es.enter_context(nc.sbuf_tensor(name, shape, dt))


def ps(nc, es, name, shape, dt=F32):
    return es.enter_context(nc.psum_tensor(name, shape, dt))


def fmview(ap, n):
    return ap.rearrange("(a p) n -> p a n", p=128)


def emit_bisect(P, s_rwf, s_ones2, s_oner, p_cnt, p_bc, st,
                dep_rwf, dep_ones2, k=ROUTER_K, niter=NITER_BISECT):
    t_lo, t_hi, t_mid = st["t_lo"], st["t_hi"], st["t_mid"]
    fs, fsn, rowcnt = st["fs"], st["fsn"], st["rowcnt"]
    mask64 = st["mask64"]

    c1 = P.v(lambda v: v.memset(t_hi[:], 1e4), [dep_rwf])
    c2 = P.v(lambda v: v.memset(t_lo[:], -1e4), [c1])
    last = P.v(lambda v: v.memset(t_mid[:], 0.0), [c2])
    prev_f1 = None
    for i in range(niter):
        slot = i % 2
        mk = P.v(lambda v: v.tensor_scalar(
            mask64[:], s_rwf[:], t_mid[:], 0.0, op0=ALU.is_ge, op1=ALU.add,
            accum_out=rowcnt[:]), [last, prev_f1])
        cm = P.t(lambda t, _s=slot: t.matmul(
            p_cnt[:, _s:_s + 1], lhsT=s_ones2[:], rhs=rowcnt[:],
            start=True, stop=True), [mk, dep_ones2, prev_f1])
        f1 = P.v(lambda v, _s=slot: v.tensor_scalar(
            fs[:], p_cnt[:, _s:_s + 1], k, None, op0=ALU.is_ge), [cm])
        f2 = P.v(lambda v: v.tensor_scalar(fsn[:], fs[:], -1, 1,
                                           op0=ALU.mult, op1=ALU.add), [f1])
        s1 = P.v(lambda v: v.copy_predicated(t_lo[:], fs[:], t_mid[:]), [f2])
        s2 = P.v(lambda v: v.copy_predicated(t_hi[:], fsn[:], t_mid[:]), [s1])
        s3 = P.v(lambda v: v.tensor_tensor(t_mid[:], t_lo[:], t_hi[:],
                                           ALU.add), [s2])
        last = P.v(lambda v: v.tensor_scalar(t_mid[:], t_mid[:], 0.5, None,
                                             op0=ALU.mult), [s3])
        prev_f1 = f1
    return last


def emit_ln(P, n, i, w, s_y, s_out, s_m, s_r, s_tmp, s_trow, s_onec, s_oner,
            p_s1, p_s2, p_mbc, p_rbc, ytoks, war, s_eps=None,
            extra_out_dep=None, s_onecb=None):
    s1tok = s2tok = None
    sqprev = war.get("sq")
    tmpb = s_tmp[:].bitcast(BF16) if s_onecb is not None else None
    for j in range(NC):
        off = j * n + i * 512
        s1tok = P.t(lambda t, _o=off, _w=w, _j=j: t.matmul(
            p_s1[:, 0:_w], lhsT=s_onec[:], rhs=s_y[:, _o:_o + _w],
            start=(_j == 0), stop=(_j == NC - 1)),
            [ytoks[j], war.get("s1")])
        sq = P.s(lambda s, _o=off, _w=w: s.activation(
            tmpb[:, 0:_w], s_y[:, _o:_o + _w], AF.Square),
            [ytoks[j], sqprev])
        s2tok = P.t(lambda t, _w=w, _j=j: t.matmul(
            p_s2[:, 0:_w], lhsT=s_onecb[:], rhs=tmpb[:, 0:_w],
            start=(_j == 0), stop=(_j == NC - 1)),
            [sq, war.get("s2")])
        sqprev = s2tok
    war["sq"] = sqprev
    o = 0
    f1 = P.v(lambda v, _w=w, _o=o: v.tensor_scalar(
        s_m[:, _o:_o + _w], p_s1[:, 0:_w], 1.0 / D, None, op0=ALU.mult),
        [s1tok])
    f2 = P.v(lambda v, _w=w, _o=o: v.tensor_scalar(
        s_r[:, _o:_o + _w], p_s2[:, 0:_w], 1.0 / D, None, op0=ALU.mult),
        [s2tok])
    war["s1"] = f1
    war["s2"] = f2
    f3 = P.v(lambda v, _w=w, _o=o: v.tensor_tensor(
        s_trow[:, 0:_w], s_m[:, _o:_o + _w], s_m[:, _o:_o + _w],
        ALU.mult), [f1, f2, war.get("trow")])
    f4 = P.v(lambda v, _w=w, _o=o: v.tensor_tensor(
        s_r[:, _o:_o + _w], s_r[:, _o:_o + _w], s_trow[:, 0:_w],
        ALU.subtract), [f3])
    war["trow"] = f4
    f5 = P.s(lambda s, _w=w, _o=o: s.activation(
        s_r[:, _o:_o + _w], s_r[:, _o:_o + _w], AF.Ln,
        bias=s_eps[:]), [f4, war.get("eps_dep")])
    war["eps_dep"] = None
    f6 = P.s(lambda s, _w=w, _o=o: s.activation(
        s_r[:, _o:_o + _w], s_r[:, _o:_o + _w], AF.Exp, scale=-0.5), [f5])
    b1 = P.t(lambda t, _w=w, _o=o: t.matmul(
        p_mbc[:, 0:_w], lhsT=s_oner[:], rhs=s_m[:, _o:_o + _w],
        start=True, stop=True), [f1, war.get("bc")])
    b2 = P.t(lambda t, _w=w, _o=o: t.matmul(
        p_rbc[:, 0:_w], lhsT=s_oner[:], rhs=s_r[:, _o:_o + _w],
        start=True, stop=True), [f6])
    outtoks = []
    for j in range(NC):
        off = j * n + i * 512
        n1 = P.v(lambda v, _o=off, _w=w: v.tensor_tensor(
            s_out[:, _o:_o + _w], s_y[:, _o:_o + _w], p_mbc[:, 0:_w],
            ALU.subtract), [b1, b2, ytoks[j], extra_out_dep])
        n2 = P.v(lambda v, _o=off, _w=w: v.tensor_tensor(
            s_out[:, _o:_o + _w], s_out[:, _o:_o + _w], p_rbc[:, 0:_w],
            ALU.mult), [n1])
        outtoks.append(n2)
    war["bc"] = outtoks[-1]
    return outtoks


def build_v2(n=NTOK, dump=False):
    nc = bass.Bass("TRN2", target_bir_lowering=False, debug=False)
    nsub = n // 512
    ntile = n // 128
    H = 1024

    XT = nc.dram_tensor("xt", [D, n], F32, kind="ExternalInput").ap()
    XRT = nc.dram_tensor("xrevt", [D, n], F32, kind="ExternalInput").ap()
    RWC = nc.dram_tensor("rwc", [128, NC], F32, kind="ExternalInput").ap()
    RB = nc.dram_tensor("rb", [1, 1], F32, kind="ExternalInput").ap()
    IDB = nc.dram_tensor("identb", [128, 128], BF16, kind="ExternalInput").ap()
    WUPS = nc.dram_tensor("wups", [4, 128, 1024], F32,
                          kind="ExternalInput").ap()
    WDNS = nc.dram_tensor("wdns", [128, 4096], F32,
                          kind="ExternalInput").ap()
    WQKVC = nc.dram_tensor("wqkvc", [6, 128, NC, 512], F32,
                           kind="ExternalInput").ap()
    BQ = nc.dram_tensor("bq", [128, 8], F32, kind="ExternalInput").ap()
    BK = nc.dram_tensor("bk", [128, 8], F32, kind="ExternalInput").ap()
    BV = nc.dram_tensor("bvrow", [1, D], F32, kind="ExternalInput").ap()
    PJ = nc.dram_tensor("projt", [64, HEADS * 256], F32,
                        kind="ExternalInput").ap()
    WOC = nc.dram_tensor("woc", [2, 128, NC, 512], F32,
                         kind="ExternalInput").ap()
    BO = nc.dram_tensor("bo", [128, NC], F32, kind="ExternalInput").ap()
    WINC = nc.dram_tensor("winc", [128, NC, 64], F32,
                          kind="ExternalInput").ap()
    BIN = nc.dram_tensor("bin", [64, 1], F32, kind="ExternalInput").ap()
    W1 = nc.dram_tensor("w1", [64, 16], F32, kind="ExternalInput").ap()
    B1 = nc.dram_tensor("b1", [64, 16], F32, kind="ExternalInput").ap()
    W2 = nc.dram_tensor("w2", [64, 16], F32, kind="ExternalInput").ap()
    B2_ = nc.dram_tensor("b2", [64, 1], F32, kind="ExternalInput").ap()
    WOUT = nc.dram_tensor("wout", [64, D], F32, kind="ExternalInput").ap()
    BOUT = nc.dram_tensor("bout", [128, NC], F32, kind="ExternalInput").ap()
    YT = nc.dram_tensor("yt", [D, n], F32, kind="ExternalOutput").ap()

    cc1_in = nc.dram_tensor("cc1_in", [1, n], F32, kind="Internal").ap()
    cc1_out = nc.dram_tensor("cc1_out", [4, n], F32, kind="Internal").ap()
    ST_in = nc.dram_tensor("st_in", [1, 2], F32, kind="Internal").ap()
    ST_out = nc.dram_tensor("st_out", [8, 2], F32, kind="Internal").ap()
    WUP_in = nc.dram_tensor("wup_in", [1, 4 * 128 * 1024], BF16,
                            kind="Internal").ap()
    WUP_out = nc.dram_tensor("wup_out", [8, 4 * 128 * 1024], BF16,
                             kind="Internal").ap()
    WDN_in = nc.dram_tensor("wdn_in", [1, 4 * 128 * 1024], BF16,
                            kind="Internal").ap()
    WDN_out = nc.dram_tensor("wdn_out", [8, 4 * 128 * 1024], BF16,
                             kind="Internal").ap()
    kv_in = nc.dram_tensor("kv_in", [128, 2080], F32, kind="Internal").ap()
    kv_out = nc.dram_tensor("kv_out", [128, 2080], F32, kind="Internal").ap()
    if dump:
        XD1 = nc.dram_tensor("xd1", [128, NC * n], F32,
                             kind="ExternalOutput").ap()
        XDH = nc.dram_tensor("xdh", [128, NC * n], BF16,
                             kind="ExternalOutput").ap()
        XD2 = nc.dram_tensor("xd2", [128, NC * n], F32,
                             kind="ExternalOutput").ap()
        XDW = nc.dram_tensor("xdw", [8, 4 * 128 * 1024], BF16,
                             kind="ExternalOutput").ap()
        XDW2 = nc.dram_tensor("xdw2", [8, 4 * 128 * 1024], BF16,
                              kind="ExternalOutput").ap()
        XD3 = nc.dram_tensor("xd3", [128, NC * n], F32,
                             kind="ExternalOutput").ap()
        XDKV = nc.dram_tensor("xdkv", [128, 2080], F32,
                              kind="ExternalOutput").ap()
        XDHH = nc.dram_tensor("xdhh", [128, NFF * 1024], BF16,
                              kind="ExternalOutput").ap()
        XDSB = nc.dram_tensor("xdsb", [128, 1024], F32,
                              kind="ExternalOutput").ap()
        XDAW = nc.dram_tensor("xdaw", [128, 1024], BF16,
                              kind="ExternalOutput").ap()
        XDH3 = nc.dram_tensor("xdh3", [128, NC * n], BF16,
                              kind="ExternalOutput").ap()

    P = Prog(nc)
    es = P.es
    SA = sb(nc, es, "SA", [128, NC * n])
    SB = sb(nc, es, "SB", [128, NC * n])
    SBb = SB[:].bitcast(BF16)
    SC = sb(nc, es, "SC", [128, 8192])
    SCb = SC[:].bitcast(BF16)
    SD = sb(nc, es, "SD", [128, 2080])
    SE = sb(nc, es, "SE", [128, 2048])
    SEb = SE[:].bitcast(BF16)
    AW = sb(nc, es, "AW", [128, 2048])
    AWb = AW[:].bitcast(BF16)
    PJT = sb(nc, es, "PJT", [128, HEADS * 256], BF16)
    s_pj = PJT
    s_tmp = sb(nc, es, "s_tmp", [128, 512])
    s_tmpb = s_tmp[:].bitcast(BF16)
    s_p56 = sb(nc, es, "s_p56", [128, 1024])
    s_rwc = sb(nc, es, "s_rwc", [128, NC])
    s_rb = sb(nc, es, "s_rb", [1, 1])
    s_rwf = sb(nc, es, "s_rwf", [128, 4 * n // 128])
    s_idb = sb(nc, es, "s_idb", [128, 128], BF16)
    s_onec = sb(nc, es, "s_onec", [128, 1])
    s_onecb = sb(nc, es, "s_onecb", [128, 1], BF16)
    s_ones2 = sb(nc, es, "s_ones2", [128, 128])
    s_oner = s_ones2[0:1, :]
    s_onerb = sb(nc, es, "s_onerb", [1, 128], BF16)
    s_eps = sb(nc, es, "s_eps", [1, 1])
    s_eps8 = sb(nc, es, "s_eps8", [1, 1])
    s_epsphi = sb(nc, es, "s_epsphi", [1, 1])
    s_m = sb(nc, es, "s_m", [1, 512])
    s_r = sb(nc, es, "s_r", [1, 512])
    s_trow = sb(nc, es, "s_trow", [1, 512])
    s_vv = SD[0:1, 0:512]
    s_c2 = SD[0:1, 512:1024]
    s_hr2 = SD[0:1, 1024:2048]
    s_mrow = s_m[0:1, 0:512]
    s_acc = sb(nc, es, "s_acc", [128, 8])
    s_red = sb(nc, es, "s_red", [128, 2])
    s_th = sb(nc, es, "s_th", [1, 2])
    s_thc = sb(nc, es, "s_thc", [128, 2])
    s_aps = sb(nc, es, "s_aps", [1, 16])
    st = {kk: sb(nc, es, kk, [128, 1]) for kk in
          ("t_lo", "t_hi", "t_mid", "rowcnt")}
    st["fs"] = sb(nc, es, "fs", [128, 1], I32)
    st["fsn"] = sb(nc, es, "fsn", [128, 1], I32)
    st["mask64"] = sb(nc, es, "mask64", [128, 4 * n // 128], BF16)
    s_bq = sb(nc, es, "s_bq", [128, 8])
    s_bk = sb(nc, es, "s_bk", [128, 8])
    s_bvb = SD[0:1, 0:512].bitcast(BF16)
    s_denb = s_trow[:].bitcast(BF16)[0:1, 0:512]
    s_phik2 = [sb(nc, es, f"s_phik{i}", [128, 256], BF16) for i in range(2)]
    s_nmx4 = sb(nc, es, "s_nmx4", [128, 4])
    s_nmx2 = [sb(nc, es, f"s_nmx{i}", [128, 1]) for i in range(2)]
    s_phiqT = [SEb[:, 0:512], SEb[:, 512:1024]]
    s_kvs = sb(nc, es, "s_kvs", [128, 132], BF16)
    s_bo = sb(nc, es, "s_bo", [128, NC])
    s_bin = sb(nc, es, "s_bin", [64, 1])
    s_w1 = sb(nc, es, "s_w1", [64, 16])
    s_b1 = sb(nc, es, "s_b1", [64, 16])
    s_w2 = sb(nc, es, "s_w2", [64, 16])
    s_b2 = sb(nc, es, "s_b2", [64, 1])
    s_bout = sb(nc, es, "s_bout", [128, NC])
    s_bv = s_p56[0:1, 0:D]
    s_exp2 = [s_p56[:, 512:768], s_p56[:, 768:1024]]
    s_den = s_p56[0:1, 0:512]
    s_dbc = s_p56[:, 0:512]
    s_sbc = SE[:, 0:1024]
    s_s = SC[0:64, 0:512]
    s_hi = SD[0:64, 0:512]
    s_g = SC[0:64, 512:1024]
    s_winb = SCb[:, 8192:8704]
    s_woutb = SCb[0:64, 8704:9728]
    s_gb = SCb[0:64, 9728:10240]
    P01 = ps(nc, es, "P01", [128, 1024])
    P23 = ps(nc, es, "P23", [128, 1024])
    B = [P01[:, 0:512], P01[:, 512:1024], P23[:, 0:512], P23[:, 512:1024]]
    B += [ps(nc, es, f"B{i}", [128, 512]) for i in range(4, 7)]
    B7 = ps(nc, es, "B7", [128, 1024], BF16)
    PP = [P01, P23]

    war = {}

    P.sd(lambda s: s.dma_start(
        out=SA[:].rearrange("p (a m) -> p a m", a=NC), in_=fmview(XT, n)),
        ch="x")
    P.sd(lambda s: s.dma_start(
        out=SB[:].rearrange("p (a m) -> p a m", a=NC), in_=fmview(XRT, n)),
        ch="x")
    lxall = P.total("syd", "x")
    for ap_, tile in ((RWC, s_rwc), (RB, s_rb), (IDB, s_idb), (BQ, s_bq),
                      (BK, s_bk), (BO, s_bo), (BIN, s_bin), (W1, s_w1),
                      (B1, s_b1), (W2, s_w2), (B2_, s_b2), (BOUT, s_bout)):
        P.sd(lambda s, _a=ap_, _t=tile: s.dma_start(out=_t[:], in_=_a))
    lall = P.total("syd", "ld")
    m2 = P.v(lambda v: v.memset(s_onec[:], 1.0))
    m2b = P.v(lambda v: v.memset(s_onecb[:], 1.0))
    m3 = P.v(lambda v: v.memset(s_ones2[:], 1.0))
    m1 = m3
    m3b = P.v(lambda v: v.memset(s_onerb[:], 1.0))
    m4 = P.v(lambda v: v.memset(s_eps[:], LN_EPS))
    m4b = P.v(lambda v: v.memset(s_eps8[:], RMS_EPS))
    m4c = P.v(lambda v: v.memset(s_epsphi[:], PHI_EPS))

    s_rwo = SD[0:1, 0:n]
    rwevs = []
    for i in range(nsub):
        pb = B[i % 2]
        mmtok = None
        for j in range(NC):
            mmtok = P.t(lambda t, _i=i, _j=j, _pb=pb: t.matmul(
                _pb[0:1, 0:512], lhsT=s_rwc[:, _j:_j + 1],
                rhs=SA[:, _j * n + _i * 512:_j * n + _i * 512 + 512],
                start=(_j == 0), stop=(_j == NC - 1)),
                [lxall, lall, war.get(("rwp", i % 2))])
        ev = P.v(lambda v, _i=i, _pb=pb: v.tensor_scalar(
            s_rwo[:, _i * 512:_i * 512 + 512], _pb[0:1, 0:512], s_rb[:],
            None, op0=ALU.add), [mmtok, lall])
        war[("rwp", i % 2)] = ev
        rwevs.append(ev)
    P.gd(lambda g: g.dma_start(out=cc1_in, in_=s_rwo[:]), rwevs, ch="cc")
    cc1 = P.col(lambda g: g.collective_compute(
        "AllGather", ALU.bypass,
        replica_groups=[[0, 1, 2, 3], [4, 5, 6, 7]],
        ins=[cc1_in.opt()], outs=[cc1_out.opt()]),
        [P.total("gpd", "cc")])
    P.gd(lambda g: g.dma_start(
        out=s_rwf[:], in_=cc1_out.rearrange("r (p f) -> (r p) f", p=32)),
        [cc1], ch="g1")
    lrwf = P.total("gpd", "g1")

    red_toks = []
    for pi in range(8):
        reg = pi % 2
        ld = P.ad(lambda s, _pi=pi, _reg=reg: s.dma_start(
            out=AW[:, _reg * 1024:(_reg + 1) * 1024],
            in_=(WUPS[_pi:_pi + 1].rearrange("o p m -> (o p) m")
                 if _pi < 4 else
                 WDNS[:, (_pi - 4) * 1024:(_pi - 3) * 1024])),
            [red_toks[pi - 2] if pi >= 2 else None], ch=f"ws{reg}")
        rt = P.v(lambda v, _pi=pi, _reg=reg: v.tensor_reduce(
            s_acc[:, _pi:_pi + 1], AW[:, _reg * 1024:(_reg + 1) * 1024],
            axis=AX.X, op=ALU.add, apply_absolute_value=True),
            [P.total("scd", f"ws{reg}")])
        red_toks.append(rt)
    r1 = P.v(lambda v: v.tensor_reduce(
        s_red[:, 0:1], s_acc[:, 0:4], axis=AX.X, op=ALU.add), red_toks)
    r2 = P.v(lambda v: v.tensor_reduce(
        s_red[:, 1:2], s_acc[:, 4:8], axis=AX.X, op=ALU.add),
        red_toks + [r1])
    apm = P.t(lambda t: t.matmul(B[2][0:1, 0:2], lhsT=s_onec[:],
                                 rhs=s_red[:, 0:2], start=True, stop=True),
              [r2, m2])
    stv = P.v(lambda v: v.tensor_copy(s_th[:, 0:2], B[2][0:1, 0:2]), [apm])
    P.gd(lambda g: g.dma_start(out=ST_in, in_=s_th[:, 0:2]), [stv], ch="cs")
    ccS = P.col(lambda g: g.collective_compute(
        "AllGather", ALU.bypass,
        replica_groups=[[0, 1, 2, 3, 4, 5, 6, 7]],
        ins=[ST_in.opt()], outs=[ST_out.opt()]),
        [P.total("gpd", "cs")])
    P.gd(lambda g: g.dma_start(
        out=s_aps[:], in_=ST_out.rearrange("r c -> () (r c)")), [ccS],
        ch="g2")
    laps = P.total("gpd", "g2")
    a1 = P.v(lambda v: v.tensor_reduce(
        s_th[:, 0:1], s_aps[0:1, 0:16:2], axis=AX.X, op=ALU.add),
        [laps, stv])
    a2 = P.v(lambda v: v.tensor_reduce(
        s_th[:, 1:2], s_aps[0:1, 1:16:2], axis=AX.X, op=ALU.add),
        [laps, a1])
    a3 = P.v(lambda v: v.tensor_scalar(
        s_th[:, 0:2], s_th[:, 0:2], 0.5 / (D * DFF), None, op0=ALU.mult),
        [a1, a2])
    bb = P.t(lambda t: t.matmul(B[2][:, 2:4], lhsT=s_oner[:],
                                rhs=s_th[:, 0:2], start=True, stop=True),
             [a3, m1, stv])
    thc = P.v(lambda v: v.tensor_copy(s_thc[:], B[2][:, 2:4]), [bb])

    bis_last = emit_bisect(P, s_rwf, s_ones2, s_oner, B[3], B[4], st,
                           lrwf, m1, niter=NITER_BISECT)

    scr = {0: SC[:, 0:1024], 1: SC[:, 1024:2048]}
    prev_sg = {}
    prev_ld = {}
    for pi in range(8):
        reg = pi % 2
        slot = pi % 2
        which = 0 if pi < 4 else 1
        ld = P.ad(lambda s, _pi=pi, _reg=reg: s.dma_start(
            out=AW[:, _reg * 1024:(_reg + 1) * 1024],
            in_=(WUPS[_pi:_pi + 1].rearrange("o p m -> (o p) m")
                 if _pi < 4 else
                 WDNS[:, (_pi - 4) * 1024:(_pi - 3) * 1024])),
            [red_toks[6 + reg], prev_ld.get(reg)], ch=f"wt{reg}")
        ldt = P.total("scd", f"wt{reg}")
        ab = P.s(lambda s, _reg=reg: s.activation(
            scr[_reg], AW[:, _reg * 1024:(_reg + 1) * 1024], AF.Abs),
            [ldt, prev_sg.get(("mk", reg))])
        mk = P.v(lambda v, _reg=reg, _w=which: v.tensor_scalar(
            scr[_reg], scr[_reg],
            s_thc[:, _w:_w + 1], None, op0=ALU.is_gt), [thc, ab])
        wm = P.v(lambda v, _reg=reg: v.tensor_tensor(
            scr[_reg], scr[_reg],
            AW[:, _reg * 1024:(_reg + 1) * 1024], ALU.mult), [mk])
        prev_ld[reg] = wm
        sg = P.s(lambda s, _slot=slot, _reg=reg: s.activation(
            PJT[:, _slot * 1024:(_slot + 1) * 1024],
            scr[_reg], AF.Sign),
            [wm, prev_sg.get(("st", slot))])
        prev_sg[("mk", reg)] = sg
        stok = P.ad(lambda s, _pi=pi, _slot=slot: s.dma_start(
            out=((WUP_in[0:1, _pi * 131072:(_pi + 1) * 131072]
                  if _pi < 4 else
                  WDN_in[0:1, (_pi - 4) * 131072:(_pi - 3) * 131072])
                 .rearrange("o (p m) -> (o p) m", p=128)),
            in_=PJT[:, _slot * 1024:(_slot + 1) * 1024]),
            [sg], ch=f"ts{slot}")
        prev_sg[("st", slot)] = stok
        if pi == 3:
            ccWu = P.col(lambda g: g.collective_compute(
                "AllGather", ALU.bypass,
                replica_groups=[[0, 1, 2, 3, 4, 5, 6, 7]],
                ins=[WUP_in.opt()], outs=[WUP_out.opt()]),
                [P.total("scd", "ts0"), P.total("scd", "ts1")])
    ccWd = P.col(lambda g: g.collective_compute(
        "AllGather", ALU.bypass,
        replica_groups=[[0, 1, 2, 3, 4, 5, 6, 7]],
        ins=[WDN_in.opt()], outs=[WDN_out.opt()]),
        [P.total("scd", "ts0"), P.total("scd", "ts1")])

    tern_done = prev_sg[("st", 0)], prev_sg[("st", 1)]
    pjevs = []
    pjld = None
    for half in range(2):
        l1 = P.sd(lambda s, _h=half: s.dma_start(
            out=SE[0:64, 0:2048], in_=PJ[:, _h * 2048:(_h + 1) * 2048]),
            [pjld], ch="pj")
        l2 = P.sd(lambda s, _h=half: s.dma_start(
            out=SE[64:128, 0:2048], in_=PJ[:, _h * 2048:(_h + 1) * 2048]),
            ch="pj")
        lt = P.total("syd", "pj")
        ev = P.v(lambda v, _h=half: v.tensor_scalar(
            PJT[:, _h * 2048:(_h + 1) * 2048], SE[:, 0:2048],
            DATA_NORM, None, op0=ALU.mult),
            [lt, tern_done[0], tern_done[1]])
        pjevs.append(ev)
        pjld = ev
    pj = pjevs[-1]
    P.sd(lambda s: s.dma_start(out=s_bv, in_=BV), ch="bv")
    lbv = P.total("syd", "bv")

    x1toks = {}
    xbtoks = {}
    prev_mb = None
    prev_b0 = P.total("gpd", "cc")
    ln1 = {"sq": prev_sg[("mk", 0)], "s1": None, "s2": None}
    rowrd = {}
    ytoks_all = {}
    for i in range(nsub):
        mr = P.v(lambda v, _i=i: v.tensor_scalar(
            s_mrow[:], s_rwo[:, _i * 512:_i * 512 + 512],
            st["t_lo"][0:1, 0:1], None, op0=ALU.is_ge),
            [bis_last, rwevs[i], prev_mb])
        mb = P.t(lambda t: t.matmul(
            B[5][:, 0:512], lhsT=s_oner[:], rhs=s_mrow[:],
            start=True, stop=True), [mr, m1, war.get("B5rd")])
        prev_mb = mb
        ytoks = []
        for j in range(NC):
            off = j * n + i * 512
            ym = P.v(lambda v, _o=off: v.tensor_tensor(
                SB[:, _o:_o + 512], SB[:, _o:_o + 512], B[5][:, 0:512],
                ALU.mult), [mb, lxall])
            ya = P.v(lambda v, _o=off: v.tensor_tensor(
                SB[:, _o:_o + 512], SB[:, _o:_o + 512], SA[:, _o:_o + 512],
                ALU.add), [ym])
            ytoks.append(ya)
        war["B5rd"] = ytoks[-1]
        ytoks_all[i] = ytoks
    for i in range(nsub):
        ytoks = ytoks_all[i]
        s1tok = None
        s2tok = None
        sqprev = ln1.get("sq")
        for j in range(NC):
            off = j * n + i * 512
            s1tok = P.t(lambda t, _o=off, _j=j: t.matmul(
                B[6][0:1, 0:512], lhsT=s_onec[:], rhs=SB[:, _o:_o + 512],
                start=(_j == 0), stop=(_j == NC - 1)),
                [ytoks[j], m2, ln1.get("s1")])
            sq = P.s(lambda s, _o=off, _j=j: s.activation(
                s_tmpb[:, (_j % 2) * 512:(_j % 2) * 512 + 512],
                SB[:, _o:_o + 512], AF.Square), [ytoks[j], sqprev])
            s2tok = P.t(lambda t, _j=j: t.matmul(
                B[4][0:1, 0:512], lhsT=s_onecb[:],
                rhs=s_tmpb[:, (_j % 2) * 512:(_j % 2) * 512 + 512],
                start=(_j == 0), stop=(_j == NC - 1)),
                [sq, m2b, ln1.get("s2")])
            sqprev = s2tok
        ln1["sq"] = sqprev
        f1 = P.v(lambda v: v.tensor_scalar(
            s_m[:], B[6][0:1, 0:512], 1.0 / D, None, op0=ALU.mult),
            [s1tok, rowrd.get("m")])
        f2 = P.v(lambda v: v.tensor_scalar(
            s_r[:], B[4][0:1, 0:512], 1.0 / D, None, op0=ALU.mult),
            [s2tok, rowrd.get("r")])
        ln1["s1"] = f1
        ln1["s2"] = f2
        f3 = P.v(lambda v: v.tensor_tensor(
            s_trow[:], s_m[:], s_m[:], ALU.mult), [f1, rowrd.get("trow")])
        f4 = P.v(lambda v: v.tensor_tensor(
            s_r[:], s_r[:], s_trow[:], ALU.subtract), [f3, f2])
        rowrd["trow"] = f4
        f4b = P.s(lambda s: s.activation(s_vv[:], s_r[:], AF.Sqrt),
                  [f4, rowrd.get("vv")])
        f5 = P.s(lambda s: s.activation(
            s_r[:], s_r[:], AF.Ln, bias=s_eps[:]),
            [f4b, (m4 if i == 0 else None)])
        f6 = P.s(lambda s: s.activation(
            s_r[:], s_r[:], AF.Exp, scale=-0.5), [f5])
        g1 = P.v(lambda v: v.tensor_tensor(
            s_vv[:], s_vv[:], s_r[:], ALU.mult), [f6, f4b])
        g2 = P.s(lambda s: s.activation(
            s_vv[:], s_vv[:], AF.Identity, bias=s_eps8[:]),
            [g1, (m4b if i == 0 else None)])
        g3 = P.v(lambda v: v.reciprocal(s_vv[:], s_vv[:]), [g2])
        g4 = P.v(lambda v: v.tensor_tensor(
            s_c2[:], s_vv[:], s_r[:], ALU.mult), [g3, rowrd.get("c2")])
        b_m = P.t(lambda t: t.matmul(
            B[0][:, 0:512], lhsT=s_oner[:], rhs=s_m[:],
            start=True, stop=True), [f1, m1, prev_b0, rowrd.get("B0")])
        b_r = P.t(lambda t: t.matmul(
            B[1][:, 0:512], lhsT=s_oner[:], rhs=s_r[:],
            start=True, stop=True), [f6, prev_b0, rowrd.get("B1")])
        b_c = P.t(lambda t: t.matmul(
            B[2][:, 0:512], lhsT=s_oner[:], rhs=s_c2[:],
            start=True, stop=True), [g4, thc, rowrd.get("B2")])
        rowrd["m"] = b_m
        rowrd["r"] = b_r
        rowrd["c2"] = b_c
        rowrd["vv"] = g4
        t2l = t3l = None
        for j in range(NC):
            off = j * n + i * 512
            t1 = P.v(lambda v, _o=off: v.tensor_tensor(
                SB[:, _o:_o + 512], SB[:, _o:_o + 512], B[0][:, 0:512],
                ALU.subtract), [b_m, ytoks[j]])
            t2 = P.v(lambda v, _o=off: v.tensor_tensor(
                SA[:, _o:_o + 512], SB[:, _o:_o + 512], B[1][:, 0:512],
                ALU.mult), [t1, b_r])
            t3 = P.v(lambda v, _o=off, _j=j, _i=i: v.tensor_tensor(
                SCb[:, _j * 2048 + _i * 512:_j * 2048 + _i * 512 + 512],
                SB[:, _o:_o + 512], B[2][:, 0:512], ALU.mult),
                [t1, b_c,
                 (prev_sg[("mk", j)] if i == 0 and j < 2 else None)])
            x1toks[(i, j)] = t2
            xbtoks[(i, j)] = t3
            t2l, t3l = t2, t3
        rowrd["B0"] = t3l
        rowrd["B1"] = t2l
        rowrd["B2"] = t3l
    p3done = t3l
    bvc = P.v(lambda v: v.tensor_copy(s_bvb[:], s_bv), [lbv, p3done])
    snap1 = None
    if dump:
        P.sd(lambda s: s.dma_start(out=XD1, in_=SA[:]), [p3done], ch="dbg")
        snap1 = P.sd(lambda s: s.dma_start(
            out=XDH, in_=SCb[:, 0:NC * n]), [p3done], ch="dbg")
        P.sd(lambda s: s.dma_start(out=XDW, in_=WUP_out), [ccWu], ch="dbg")
        snap1 = P.sd(lambda s: s.dma_start(out=XDW2, in_=WDN_out),
                     [ccWd, snap1], ch="dbg")
        snap1 = P.total("syd", "dbg")

    awfree = {0: None, 1: None, 2: None, 3: None}
    fw = {"B01": rowrd["B1"], "B23": rowrd["B2"],
          "B4": ln1["s2"], "B5": war["B5rd"], "B6": ln1["s1"],
          "sq": ln1["sq"], "tmprd": (snap1 or ln1["sq"])}
    awfree[0] = awfree[1] = prev_ld[0]
    awfree[2] = awfree[3] = prev_ld[1]
    hwar = p3done
    ln2war = {"eps_dep": None, "sq": ln1["sq"], "trow": rowrd["trow"],
              "s1": None, "s2": None, "bc": None}
    x2toks = {}
    dnfree = {}
    sbcrd = {}
    for hh in range(2):
        tok0 = hh * H
        geltoks = {}
        stat_emit = []
        statA_last = statB_last = None
        upA_war = {0: fw["B01"], 1: fw["B23"]}
        for f in range(32):
            u = f % 4
            c, k = f // 4, f % 4
            ld = P.sd(lambda s, _u=u, _c=c, _k=k: s.dma_start(
                out=AWb[:, _u * 1024:(_u + 1) * 1024],
                in_=WUP_out[_c:_c + 1,
                            _k * 131072:(_k + 1) * 131072].rearrange(
                    "o (p m) -> (o p) m", p=128)),
                [ccWu if (hh == 0 and f < 4) else None, awfree[u],
                 (ln2war["bc"] if hh == 1 and f < 4 else None),
                 (ln2war["sq"] if hh == 1 and f < 4 else None)],
                ch=f"wu{u}")
            ldt = P.total("syd", f"wu{u}")
            pA = B[0 + 2 * (f % 2)]
            pB = B[1 + 2 * (f % 2)]
            mmA = mmB = None
            for j in range(NC):
                mmA = P.t(lambda t, _u=u, _j=j, _pA=pA, _t0=tok0: t.matmul(
                    _pA[:, 0:512],
                    lhsT=AWb[:, _u * 1024 + _j * 128:_u * 1024 + (_j + 1) * 128],
                    rhs=SCb[:, _j * 2048 + _t0:_j * 2048 + _t0 + 512],
                    start=(_j == 0), stop=(_j == NC - 1)),
                    [ldt, xbtoks[(2 * hh, j)], upA_war[f % 2]])
                mmB = P.t(lambda t, _u=u, _j=j, _pB=pB, _t0=tok0: t.matmul(
                    _pB[:, 0:512],
                    lhsT=AWb[:, _u * 1024 + _j * 128:_u * 1024 + (_j + 1) * 128],
                    rhs=SCb[:, _j * 2048 + _t0 + 512:_j * 2048 + _t0 + 1024],
                    start=(_j == 0), stop=(_j == NC - 1)),
                    [xbtoks[(2 * hh + 1, j)], mmA])
            awfree[u] = mmB
            if dump and hh == 1 and f == 0:
                P.sd(lambda s: s.dma_start(
                    out=XDAW, in_=AWb[:, 0:1024]), [mmB], ch="dbg")
                awfree[u] = P.sd(lambda s: s.dma_start(
                    out=XDH3, in_=SCb[:, 0:NC * n]), [mmB], ch="dbg")
            if stat_emit:
                statA_last, statB_last = stat_emit.pop()()
            gA = P.s(lambda s, _f=f, _pA=pA: s.activation(
                SBb[:, _f * 1024:_f * 1024 + 512], _pA[:, 0:512], AF.Gelu),
                [mmA, hwar])
            gB = P.s(lambda s, _f=f, _pB=pB: s.activation(
                SBb[:, _f * 1024 + 512:(_f + 1) * 1024], _pB[:, 0:512],
                AF.Gelu), [mmB])
            geltoks[f] = (gA, gB)
            upA_war[f % 2] = gB
            sqA = P.s(lambda s, _f=f: s.activation(
                s_tmpb[:, 0:512],
                SBb[:, _f * 1024:_f * 1024 + 512], AF.Square),
                [gA, fw["sq"], statA_last])
            sqB = P.s(lambda s, _f=f: s.activation(
                s_tmpb[:, 512:1024],
                SBb[:, _f * 1024 + 512:(_f + 1) * 1024], AF.Square),
                [sqA, statB_last])
            fw["sq"] = sqB

            def emit_stats(_f=f, _sqA=sqA, _sqB=sqB,
                           _sAp=statA_last, _sBp=statB_last,
                           _b4=fw["B4"], _b5=fw["B5"]):
                sA = P.t(lambda t: t.matmul(
                    B[4][0:1, 0:512], lhsT=s_onecb[:],
                    rhs=s_tmpb[:, 0:512],
                    start=(_f == 0), stop=(_f == 31)),
                    [_sqA, m2b, _sAp, (_b4 if _f == 0 else None)])
                sB = P.t(lambda t: t.matmul(
                    B[5][0:1, 0:512], lhsT=s_onecb[:],
                    rhs=s_tmpb[:, 512:1024],
                    start=(_f == 0), stop=(_f == 31)),
                    [_sqB, _sBp, (_b5 if _f == 0 else None)])
                return sA, sB
            stat_emit.append(emit_stats)
        statA_last, statB_last = stat_emit.pop()()
        fw["tmprd"] = statB_last
        fw["B01"] = upA_war[0]
        fw["B23"] = upA_war[1]
        hrA = P.v(lambda v: v.tensor_scalar(
            s_hr2[0:1, 0:512], B[4][0:1, 0:512], 1.0 / DFF, None,
            op0=ALU.mult), [statA_last, sbcrd.get("hr2"),
                            (prev_mb if hh == 0 else None)])
        hrB = P.v(lambda v: v.tensor_scalar(
            s_hr2[0:1, 512:1024], B[5][0:1, 0:512], 1.0 / DFF, None,
            op0=ALU.mult), [statB_last, hrA])
        h1 = P.s(lambda s: s.activation(s_hr2[:], s_hr2[:], AF.Sqrt),
                 [hrA, hrB])
        h2 = P.s(lambda s: s.activation(
            s_hr2[:], s_hr2[:], AF.Identity, bias=s_eps8[:]), [h1])
        h3 = P.v(lambda v: v.reciprocal(s_hr2[:], s_hr2[:]), [h2])
        b_sA = P.t(lambda t: t.matmul(
            B[4][:, 0:512], lhsT=s_oner[:], rhs=s_hr2[0:1, 0:512],
            start=True, stop=True), [h3, m1, hrA])
        b_sB = P.t(lambda t: t.matmul(
            B[5][:, 0:512], lhsT=s_oner[:], rhs=s_hr2[0:1, 512:1024],
            start=True, stop=True), [h3, hrB])
        cpA = P.v(lambda v: v.tensor_copy(SE[:, 0:512], B[4][:, 0:512]),
                  [b_sA, sbcrd.get("sbc"), (pj if hh == 0 else None)])
        cpB = P.v(lambda v: v.tensor_copy(SE[:, 512:1024], B[5][:, 0:512]),
                  [b_sB, cpA])
        fw["B4"] = cpA
        fw["B5"] = cpB
        sbcrd["hr2"] = b_sB
        ztoksA = {}
        ztoksB = {}
        for cj in range(8):
            ldq = {}
            for q in range(2):
                r = q
                ldq[q] = P.sd(lambda s, _cj=cj, _q=q, _r=r: s.dma_start(
                    out=AWb[:, _r * 2048:(_r + 1) * 2048].rearrange(
                        "p (j m) -> p j m", j=2),
                    in_=WDN_out[_cj:_cj + 1,
                                (2 * _q) * 131072:
                                (2 * _q + 2) * 131072].rearrange(
                        "o (j p m) -> (o p) j m", j=2, p=128, m=1024)),
                    [ccWd if (hh == 0 and cj == 0) else None,
                     awfree[2 * r], awfree[2 * r + 1],
                     dnfree.get((hh, cj - 1, r))],
                    ch=f"wd{r}")
            pA = B[0 + 2 * (cj % 2)]
            pB = B[1 + 2 * (cj % 2)]
            mmA = mmB = None
            for f in range(32):
                r = f // 16
                lt = P.total("syd", f"wd{r}") if f % 16 == 0 else None
                mmA = P.t(lambda t, _f=f, _r=r, _pA=pA, _lt=lt: t.matmul(
                    _pA[:, 0:512],
                    lhsT=AWb[:, _r * 2048 + (_f % 16) * 128:
                             _r * 2048 + (_f % 16 + 1) * 128],
                    rhs=SBb[:, _f * 1024:_f * 1024 + 512],
                    start=(_f == 0), stop=(_f == 31)),
                    [lt, geltoks[f][0],
                     (fw.get("B01" if cj % 2 == 0 else "B23")
                      if f == 0 else None)])
                mmB = P.t(lambda t, _f=f, _pB=pB: t.matmul(
                    _pB[:, 0:512],
                    lhsT=AWb[:, (_f // 16) * 2048 + (_f % 16) * 128:
                             (_f // 16) * 2048 + (_f % 16 + 1) * 128],
                    rhs=SBb[:, _f * 1024 + 512:(_f + 1) * 1024],
                    start=(_f == 0), stop=(_f == 31)),
                    [geltoks[f][1], mmA])
                if f == 15:
                    dnfree[(hh, cj, 0)] = mmB
                if f == 31:
                    dnfree[(hh, cj, 1)] = mmB
            tA = P.v(lambda v, _pA=pA: v.tensor_tensor(
                s_tmp[:, 0:512], _pA[:, 0:512], SE[:, 0:512],
                ALU.mult), [mmB, cpA, fw.get("tmp"), fw["tmprd"]])
            zA = P.v(lambda v, _cj=cj, _t0=tok0: v.tensor_tensor(
                SA[:, _cj * n + _t0:_cj * n + _t0 + 512],
                s_tmp[:, 0:512],
                SA[:, _cj * n + _t0:_cj * n + _t0 + 512], ALU.add), [tA])
            tB = P.v(lambda v, _pB=pB: v.tensor_tensor(
                s_tmp[:, 0:512], _pB[:, 0:512], SE[:, 512:1024],
                ALU.mult), [mmB, cpB, zA])
            zB = P.v(lambda v, _cj=cj, _t0=tok0: v.tensor_tensor(
                SA[:, _cj * n + _t0 + 512:_cj * n + _t0 + 1024],
                s_tmp[:, 0:512],
                SA[:, _cj * n + _t0 + 512:_cj * n + _t0 + 1024], ALU.add),
                [tB])
            fw["tmp"] = zB
            if cj % 2 == 0:
                fw["B01"] = zB
            else:
                fw["B23"] = zB
            ztoksA[cj] = zA
            ztoksB[cj] = zB
        sbcrd["sbc"] = fw["tmp"]
        ln2war["sq"] = ztoksB[7]
        ln2war["bc"] = ztoksB[7]
        for ts in range(2):
            gi = hh * 2 + ts
            ytl = [ztoksA[cj] if ts == 0 else ztoksB[cj] for cj in range(8)]
            ln2war["eps_dep"] = None
            outt = emit_ln(P, n, gi, 512, SA, SA, s_m, s_r, s_tmp, s_trow,
                           s_onec, s_oner, B[6][0:1, :], B[4][0:1, :],
                           B[5], B[2], ytl, ln2war, s_eps=s_eps,
                           s_onecb=s_onecb)
            x2toks[gi] = outt
        fw["sq"] = ln2war["sq"]
        fw["B4"] = ln2war["s2"]
        fw["B5"] = ln2war["bc"]
        fw["B6"] = ln2war["s1"]
        fw["B23"] = ln2war["bc"]
        fw["tmprd"] = ln2war["sq"]
        fw["tmp"] = ln2war["sq"]
        sbcrd["hr2"] = ln2war["s2"]
        sbcrd["sbc"] = ln2war["bc"]
        hwar = None
        if hh == 0:
            hwar = ln2war["bc"]
            upA_war = {0: ln2war["bc"], 1: ln2war["bc"]}
            for u in range(4):
                awfree[u] = dnfree[(0, 7, u // 2)]
    p4done = x2toks[3][-1]
    snap2 = None
    if dump:
        snap2 = P.sd(lambda s: s.dma_start(out=XD2, in_=SA[:]), [p4done],
                     ch="dbg")
        P.sd(lambda s: s.dma_start(out=XDHH, in_=SBb[:, 0:NFF * 1024]),
             [p4done], ch="dbg")
        P.sd(lambda s: s.dma_start(out=XDSB, in_=SE[:, 0:1024]),
             [p4done], ch="dbg")
        p4done = P.total("syd", "dbg")

    awf = [dnfree[(1, 7, 0)], dnfree[(1, 7, 1)]]
    for wi, (widx0, dst, sbias) in enumerate(((0, 0, s_bq), (2, KOFF, s_bk))):
        for half in range(2):
            wln = P.gd(lambda g, _w=widx0 + half: g.dma_start(
                out=AWb[:, 0:NC * 512].rearrange("p (a m) -> p a m", a=NC),
                in_=WQKVC[_w:_w + 1].rearrange("o p a m -> (o p) a m")),
                [war.get("awfree2"), p4done] + awf, ch="wqk")
            awf = []
            wlnt = P.total("gpd", "wqk")
            for i in range(nsub):
                casts = []
                for j in range(NC):
                    off = j * n + i * 512
                    casts.append(P.v(lambda v, _o=off, _j=j: v.tensor_copy(
                        SEb[:, _j * 512:(_j + 1) * 512], SA[:, _o:_o + 512]),
                        [x2toks[i][j], war.get("x2b")]))
                for hq in range(4):
                    hp = half * 4 + hq
                    mmt = None
                    for j in range(NC):
                        mmt = P.t(lambda t, _j=j, _hq=hq: t.matmul(
                            B[_hq % 2][:, 0:512],
                            lhsT=AWb[:, _j * 512 + _hq * 128:
                                     _j * 512 + _hq * 128 + 128],
                            rhs=SEb[:, _j * 512:(_j + 1) * 512],
                            start=(_j == 0), stop=(_j == NC - 1)),
                            [wlnt, casts[j], war.get(("pqk", hq % 2)),
                             (fw["B01"] if wi == 0 and half == 0 and i == 0
                              and hq < 2 else None)])
                    ev = P.s(lambda s, _i=i, _hp=hp, _hq=hq, _d=dst,
                             _sb=sbias: s.activation(
                        SBb[:, _d + _hp * n + _i * 512:
                            _d + _hp * n + _i * 512 + 512],
                        B[_hq % 2][:, 0:512], AF.Identity,
                        bias=_sb[:, _hp:_hp + 1]), [mmt, lall, p4done])
                    war[("pqk", hq % 2)] = ev
                    war[("qk", wi, hp)] = ev
                war["x2b"] = war[("pqk", 1)]
            war["awfree2"] = war[("pqk", 1)]
    for half in range(2):
        wlv = P.gd(lambda g, _h=half: g.dma_start(
            out=AWb[:, 0:NC * 512].rearrange("p (a m) -> p a m", a=NC),
            in_=WQKVC[4 + _h:5 + _h].rearrange("o p a m -> (o p) a m")), [war["awfree2"]], ch="wv")
        wlvt = P.total("gpd", "wv")
        for i in range(nsub):
            casts = []
            for j in range(NC):
                off = j * n + i * 512
                casts.append(P.v(lambda v, _o=off, _j=j: v.tensor_copy(
                    SEb[:, _j * 512:(_j + 1) * 512], SA[:, _o:_o + 512]),
                    [x2toks[i][j], war.get("x2b")]))
            for r in range(4):
                tt = i * 4 + r
                mmt = None
                for j in range(NC):
                    mmt = P.t(lambda t, _j=j, _r=r: t.matmul(
                        B[2][:, 0:512],
                        lhsT=SEb[:, _j * 512 + _r * 128:
                                 _j * 512 + _r * 128 + 128],
                        rhs=AWb[:, _j * 512:(_j + 1) * 512],
                        start=(_j == 0), stop=False),
                        [wlvt, casts[j], war.get("pv"),
                         (fw["B23"] if half == 0 and i == 0 and r == 0
                          else None)])
                mmt = P.t(lambda t, _h=half: t.matmul(
                    B[2][:, 0:512], lhsT=s_onerb[:],
                    rhs=s_bvb[0:1, _h * 512:(_h + 1) * 512],
                    start=False, stop=True), [mmt, m3b, bvc])
                vx = P.v(lambda v, _tt=tt, _h=half: v.tensor_copy(
                    SCb[:, _tt * 1024 + _h * 512:
                        _tt * 1024 + (_h + 1) * 512],
                    B[2][:, 0:512]), [mmt, war.get("vxw"), p4done])
                war["pv"] = vx
                war["vxw"] = vx
                war[("vx", tt, half)] = vx
            war["x2b"] = war["pv"]
        war["awfree2"] = war["pv"]
    LN16 = float(np.log(16.0))
    phs = [B[0][:, 0:256], B[1][:, 0:256], B[4][:, 0:256], B[5][:, 0:256]]
    exs = [s_p56[:, g * 256:(g + 1) * 256] for g in range(4)]
    pks4 = [SEb[:, t * 256:(t + 1) * 256] for t in range(4)]
    kvq = []
    kvlast = {}
    for h in range(HEADS):
        hp, hb = h // 2, (h % 2) * 64
        vhalf, hv = h // 8, h % 8
        for tt in range(ntile):
            g = tt % 4
            rr = tt * 128
            xp = P.t(lambda t, _hp=hp, _hb=hb, _rr=rr, _h=h, _g=g:
                     t.matmul(
                phs[_g],
                lhsT=SBb[_hb:_hb + 64,
                         KOFF + _hp * n + _rr:KOFF + _hp * n + _rr + 128],
                rhs=s_pj[_hb:_hb + 64, _h * 256:(_h + 1) * 256],
                start=True, stop=True),
                [war[("qk", 1, hp)], pj, war.get(("xpfree", g)),
                 (war[("qk", 1, 3)] if h == 0 and tt < 4 else None),
                 (ln2war["bc"] if h == 0 and tt < 4 else None)])
            rm = P.v(lambda v, _g=g: v.reduce_max(
                s_nmx4[:, _g:_g + 1], phs[_g], axis=AX.X,
                negate=True), [xp, war.get(("nmxrd", g))])
            ba = P.v(lambda v, _g=g: v.tensor_scalar(
                s_nmx4[:, _g:_g + 1], s_nmx4[:, _g:_g + 1], -LN16, None,
                op0=ALU.add), [rm])
            ex = P.s(lambda s, _g=g: s.activation(
                exs[_g], phs[_g], AF.Exp, bias=s_nmx4[:, _g:_g + 1]),
                [ba, war.get(("exrd", g))])
            war[("xpfree", g)] = ex
            war[("nmxrd", g)] = ex
            ph = P.v(lambda v, _g=g: v.tensor_scalar(
                pks4[_g], exs[_g], PHI_EPS / 16.0, None,
                op0=ALU.add), [ex, war.get(("pkrd", g))])
            war[("exrd", g)] = ph

            def mk_kv(_h=h, _tt=tt, _g=g, _ph=ph, _vh=vhalf, _hv=hv):
                vsl = SCb[:, _tt * 1024 + _vh * 512 + _hv * 64:
                          _tt * 1024 + _vh * 512 + _hv * 64 + 64]
                kv0 = P.t(lambda t: t.matmul(
                    B[6][:, 0:64], lhsT=pks4[_g][:, 0:128], rhs=vsl,
                    start=(_tt == 0), stop=False),
                    [_ph, war[("vx", _tt, _vh)],
                     (war.get("kv0ev") if _tt == 0 else None),
                     (ln2war["s1"] if _h == 0 and _tt == 0 else None)])
                kv0 = P.t(lambda t: t.matmul(
                    B[6][:, 64:65], lhsT=pks4[_g][:, 0:128], rhs=s_onecb[:],
                    start=False, stop=(_tt == ntile - 1)), [kv0, m2b])
                kv1 = P.t(lambda t: t.matmul(
                    B[3][:, 0:64], lhsT=pks4[_g][:, 128:256], rhs=vsl,
                    start=(_tt == 0), stop=False),
                    [_ph, war[("vx", _tt, _vh)],
                     (war.get("kv1ev") if _tt == 0 else None)])
                kv1 = P.t(lambda t: t.matmul(
                    B[3][:, 64:65], lhsT=pks4[_g][:, 128:256],
                    rhs=s_onecb[:],
                    start=False, stop=(_tt == ntile - 1)), [kv1, m2b])
                war[("pkrd", _g)] = kv1
                if _tt == ntile - 1:
                    kvlast[_h] = (kv0, kv1)
            kvq.append(mk_kv)
            if tt == ntile - 1:
                def mk_e(_h=h):
                    e1 = P.v(lambda v: v.tensor_copy(
                        SD[:, _h * 65:(_h + 1) * 65], B[6][:, 0:65]),
                        [kvlast[_h][0], kvlast[_h][1],
                         (p4done if _h == 0 else None),
                         (prev_mb if _h == 0 else None)])
                    e2 = P.v(lambda v: v.tensor_copy(
                        SD[:, 1040 + _h * 65:1040 + (_h + 1) * 65],
                        B[3][:, 0:65]), [e1])
                    war["kv0ev"] = e1
                    war["kv1ev"] = e2
                    war[("kvh", _h)] = e2
                kvq.append(mk_e)
            while len(kvq) > 2:
                kvq.pop(0)()
    while kvq:
        kvq.pop(0)()

    dkv = P.gd(lambda gg: gg.dma_start(out=kv_in, in_=SD[:, 0:2080]),
               [war[("kvh", hh)] for hh in range(HEADS)], ch="kvio")
    cc3 = P.col(lambda gg: gg.collective_compute(
        "AllReduce", ALU.add, replica_groups=[[0, 1, 2, 3], [4, 5, 6, 7]],
        ins=[kv_in.opt()], outs=[kv_out.opt()]), [P.total("gpd", "kvio")])
    P.gd(lambda gg: gg.dma_start(out=SD[:, 0:2080], in_=kv_out),
         [cc3], ch="kvb")
    lkvb = P.total("gpd", "kvb")

    if dump:
        P.sd(lambda s: s.dma_start(out=XDKV, in_=kv_out), [cc3], ch="dbg")
    pw = {"eps_dep": None, "sq": ln2war["sq"], "trow": ln2war["trow"]}
    phs6 = [B[4][:, 0:256], B[5][:, 0:256], B[2][:, 0:256], B[6][:, 0:256]]
    for _r in range(4):
        war[("xpk", _r)] = war["kv1ev"]
    if dump:
        for cj in range(NC):
            pw[("zr", cj)] = snap2
    war["pTread"] = war["x2b"]
    x3toks = {}
    for i in range(nsub):
        ndtoks = []
        for h in range(HEADS):
            hp, hb = h // 2, (h % 2) * 64
            kc = P.v(lambda v, _h=h: v.tensor_copy(
                s_kvs[:, 0:65], SD[:, _h * 65:(_h + 1) * 65]),
                [lkvb, war.get("kvsread")])
            kc2 = P.v(lambda v, _h=h: v.tensor_copy(
                s_kvs[:, 65:130],
                SD[:, 1040 + _h * 65:1040 + (_h + 1) * 65]), [kc])
            trtoks = []
            for r in range(4):
                n0 = i * 512 + r * 128
                xq = P.t(lambda t, _hp=hp, _hb=hb, _n0=n0, _r=r, _h=h:
                         t.matmul(
                    phs6[_r],
                    lhsT=SBb[_hb:_hb + 64,
                             _hp * n + _n0:_hp * n + _n0 + 128],
                    rhs=s_pj[_hb:_hb + 64, _h * 256:(_h + 1) * 256],
                    start=True, stop=True), [pj, war[("qk", 0, hp)],
                                             war.get(("xpk", r))])
                par = r % 2
                rm = P.v(lambda v, _r=r, _p=par: v.reduce_max(
                    s_nmx2[_p][:], phs6[_r], axis=AX.X,
                    negate=True), [xq, war.get(("nmxread", par))])
                ex = P.s(lambda s, _r=r, _p=par: s.activation(
                    s_exp2[_p][:], phs6[_r], AF.Exp,
                    bias=s_nmx2[_p][:]), [rm, war.get(("exread", par))])
                war[("xpk", r)] = ex
                war[("nmxread", par)] = ex
                phq = P.v(lambda v, _p=par: v.tensor_scalar(
                    s_phik2[_p][:], s_exp2[_p][:], 1.0 / 16.0,
                    PHI_EPS / 16.0, op0=ALU.mult, op1=ALU.add),
                    [ex, war.get(("phikread", par))])
                war[("exread", par)] = phq
                t1 = P.t(lambda t, _p=par: t.transpose(
                    B7[:, _p * 256:_p * 256 + 128], s_phik2[_p][:, 0:128],
                    s_idb[:]), [phq, lall, war.get(("trread", par))])
                t2 = P.t(lambda t, _p=par: t.transpose(
                    B7[:, _p * 256 + 128:_p * 256 + 256],
                    s_phik2[_p][:, 128:256], s_idb[:]), [t1])
                c1t = P.v(lambda v, _r=r, _p=par: v.tensor_copy(
                    s_phiqT[0][:, _r * 128:(_r + 1) * 128],
                    B7[:, _p * 256:_p * 256 + 128]), [t2, war.get("pTread")])
                c2t = P.v(lambda v, _r=r, _p=par: v.tensor_copy(
                    s_phiqT[1][:, _r * 128:(_r + 1) * 128],
                    B7[:, _p * 256 + 128:_p * 256 + 256]), [c1t])
                war[("phikread", par)] = c2t
                war[("trread", par)] = c2t
                trtoks.append(c2t)
            nm = P.t(lambda t: t.matmul(
                B[0][0:65, 0:512], lhsT=s_kvs[:, 0:65],
                rhs=s_phiqT[0][:], start=True, stop=False),
                trtoks + [kc2, war.get("numread")])
            nm = P.t(lambda t: t.matmul(
                B[0][0:65, 0:512],
                lhsT=s_kvs[:, 65:130],
                rhs=s_phiqT[1][:], start=False, stop=True), [nm])
            war["pTread"] = nm
            war["kvsread"] = nm
            d1 = P.s(lambda s: s.activation(
                s_den[:], B[0][64:65, 0:512], AF.Ln, bias=s_epsphi[:]),
                [nm, m4c, war.get("denread")])
            d2 = P.s(lambda s: s.activation(
                s_den[:], s_den[:], AF.Exp, scale=-1.0), [d1])
            dcb = P.v(lambda v: v.tensor_copy(s_denb[:], s_den[:]),
                      [d2, war.get("denbrd")])
            db = P.t(lambda t: t.matmul(
                B[1][:, 0:512], lhsT=s_onerb[:], rhs=s_denb[:],
                start=True, stop=True), [dcb, m3b, war.get("dbcread")])
            dc = P.v(lambda v: v.tensor_copy(s_dbc[:], B[1][:, 0:512]),
                     [db])
            war["denread"] = dc
            war["denbrd"] = db
            nd = P.v(lambda v, _hp=hp, _hb=hb, _i=i: v.tensor_tensor(
                SCb[_hb:_hb + 64,
                    _hp * n + _i * 512:_hp * n + _i * 512 + 512],
                B[0][0:64, 0:512], s_dbc[_hb:_hb + 64, :], ALU.mult),
                [dc, nm, war.get("ndw")])
            war["numread"] = nd
            war["dbcread"] = nd
            ndtoks.append(nd)
        ztoks = []
        for whalf in range(2):
            wlo = P.gd(lambda g, _h=whalf: g.dma_start(
                out=AWb[:, 0:NC * 512].rearrange("p (a m) -> p a m", a=NC),
                in_=WOC[_h:_h + 1].rearrange("o p a m -> (o p) a m")),
                [war.get("awfree2"), pw.get("wofree")], ch="wo")
            wlot = P.total("gpd", "wo")
            for cj4 in range(4):
                cj = whalf * 4 + cj4
                mmt = None
                for j in range(NC):
                    mmt = P.t(lambda t, _c4=cj4, _j=j, _i=i,
                              _lt=wlot: t.matmul(
                        B[2 + _c4 % 2][:, 0:512],
                        lhsT=AWb[:, _j * 512 + _c4 * 128:
                                 _j * 512 + (_c4 + 1) * 128],
                        rhs=SCb[:, _j * n + _i * 512:
                                _j * n + _i * 512 + 512],
                        start=(_j == 0), stop=(_j == NC - 1)),
                        ndtoks + [wlot, pw.get(("o", cj4 % 2))])
                pw["wofree"] = mmt
                ev = P.s(lambda s, _c=cj, _c4=cj4: s.activation(
                    s_tmp[:, 0:512], B[2 + _c4 % 2][:, 0:512], AF.Identity,
                    bias=s_bo[:, _c:_c + 1]), [mmt, lall, pw.get("tread")])
                zt = P.v(lambda v, _c=cj, _i=i: v.tensor_tensor(
                    SA[:, _c * n + _i * 512:_c * n + _i * 512 + 512],
                    s_tmp[:, 0:512],
                    SA[:, _c * n + _i * 512:_c * n + _i * 512 + 512],
                    ALU.add), [ev, pw.get(("zr", cj))])
                pw[("o", cj4 % 2)] = zt
                pw["tread"] = zt
                ztoks.append(zt)
        war["awfree2"] = pw["wofree"]
        pw["sq"] = ztoks[-1]
        pw["eps_dep"] = None
        outt = emit_ln(P, n, i, 512, SA, SA, s_m, s_r, s_tmp, s_trow,
                       s_onec, s_oner, B[4][0:1, :], B[5][0:1, :], B[6],
                       B[1], ztoks, pw, s_eps=s_eps, s_onecb=s_onecb)
        pw["tread"] = pw["sq"]
        for cj in range(NC):
            pw[("zr", cj)] = outt[-1]
        for _r in range(4):
            war[("xpk", _r)] = outt[-1]
        if i + 1 < nsub:
            war["ndw"] = outt[-1]
        x3toks[i] = outt

    x3done = x3toks[nsub - 1][-1]
    if dump:
        snap3 = P.sd(lambda s: s.dma_start(out=XD3, in_=SA[:]), [x3done],
                     ch="dbg")
        dwseed = {("zr", cj): snap3 for cj in range(NC)}
    else:
        dwseed = {}
    P.gd(lambda g: g.dma_start(
        out=s_winb[:].rearrange("p (a m) -> p a m", a=NC), in_=WINC),
        [lkvb], ch="wsm")
    P.gd(lambda g: g.dma_start(out=s_woutb[:], in_=WOUT), [lkvb],
         ch="wsm")
    lwsm = P.total("gpd", "wsm")
    dw = {"eps_dep": None, "sq": pw.get("sq"), "trow": pw.get("trow")}
    dw.update(dwseed)
    stoks = []
    for i in range(nsub):
        casts = []
        for j in range(NC):
            off = j * n + i * 512
            casts.append(P.v(lambda v, _o=off, _j=j: v.tensor_copy(
                SEb[:, _j * 512:(_j + 1) * 512], SA[:, _o:_o + 512]),
                [x3toks[i][j], dw.get("x3b"), war["pTread"]]))
        mmt = None
        for j in range(NC):
            mmt = P.t(lambda t, _j=j: t.matmul(
                B[6][0:64, 0:512], lhsT=s_winb[:, _j * 64:(_j + 1) * 64],
                rhs=SEb[:, _j * 512:(_j + 1) * 512],
                start=(_j == 0), stop=(_j == NC - 1)),
                [lwsm, casts[j], dw.get("psread"), war["numread"]])
        sv = P.s(lambda s: s.activation(
            s_s[:], B[6][0:64, 0:512], AF.Identity, bias=s_bin[:]),
            [mmt, lall, dw.get("sread"), war["numread"]])
        dw["psread"] = sv
        gt = None
        for ii in range(16):
            hi = P.s(lambda s, _ii=ii: s.activation(
                s_hi[:], s_s[:], AF.Relu,
                scale=s_w1[:, _ii:_ii + 1], bias=s_b1[:, _ii:_ii + 1]),
                [sv, lall, dw.get("hiread"), dw.get("hi_war"),
                 war["kvsread"]])
            if ii == 0:
                gt = P.v(lambda v: v.tensor_scalar(
                    s_g[:], s_hi[:], s_w2[:, 0:1], None, op0=ALU.mult),
                    [hi, dw.get("gread")])
            else:
                gt = P.v(lambda v, _ii=ii: v.scalar_tensor_tensor(
                    s_g[:], s_hi[:], s_w2[:, _ii:_ii + 1], s_g[:],
                    op0=ALU.mult, op1=ALU.add), [hi, gt])
            dw["hiread"] = gt
        dw["sread"] = gt
        g2 = P.v(lambda v: v.tensor_scalar(
            s_g[:], s_g[:], s_b2[:], None, op0=ALU.add), [gt])
        gb = P.v(lambda v: v.tensor_copy(s_gb[:], s_g[:]),
                 [g2, dw.get("gbread")])
        dw["gread"] = gb
        ztoks = []
        for cj in range(NC):
            mmt = P.t(lambda t, _c=cj: t.matmul(
                B[2 + _c % 2][:, 0:512],
                lhsT=s_woutb[:, _c * 128:(_c + 1) * 128],
                rhs=s_gb[:], start=True, stop=True),
                [gb, lwsm, dw.get(("o", cj % 2))])
            ev = P.s(lambda s, _c=cj: s.activation(
                s_tmp[:, 0:512], B[2 + _c % 2][:, 0:512], AF.Identity,
                bias=s_bout[:, _c:_c + 1]), [mmt, dw.get("tread")])
            zt = P.v(lambda v, _c=cj, _i=i: v.tensor_tensor(
                SA[:, _c * n + _i * 512:_c * n + _i * 512 + 512],
                s_tmp[:, 0:512],
                SA[:, _c * n + _i * 512:_c * n + _i * 512 + 512], ALU.add),
                [ev, dw.get(("zr", cj)), gb])
            dw[("o", cj % 2)] = zt
            dw["tread"] = zt
            ztoks.append(zt)
        dw["gbread"] = ztoks[-1]
        dw["x3b"] = ztoks[-1]
        dw["sq"] = ztoks[-1]
        outt = emit_ln(P, n, i, 512, SA, SA, s_m, s_r, s_tmp, s_trow,
                       s_onec, s_oner, B[4][0:1, :], B[5][0:1, :], B[0],
                       B[1], ztoks, dw, s_eps=s_eps, s_onecb=s_onecb)
        dw["tread"] = dw["sq"]
        stok = P.sd(lambda s, _i=i: s.dma_start(
            out=fmview(YT, n)[:, :, _i * 512:_i * 512 + 512],
            in_=SA[:].rearrange("p (a m) -> p a m", a=NC)[
                :, :, _i * 512:_i * 512 + 512]), outt, ch="st")
        for cj in range(NC):
            dw[("zr", cj)] = stok
        dw["hi_war"] = stok
        stoks.append(stok)
    if dump:
        stoks = stoks + [P.total("syd", "dbg")]
    P.finish(stoks)
    P.build()
    return nc


_V2_NC = None


def kernel(L0_attn_x, L0_attn_router_w, L0_attn_router_b, L0_attn_wq,
           L0_attn_wk, L0_attn_wv, L0_ffn_w_up, L0_ffn_w_down, L1_attn_w_qkv,
           L1_attn_b_qkv, L1_attn_w_out, L1_attn_b_out, L1_attn_proj_matrix,
           L1_ffn_Win, L1_ffn_bin_, L1_ffn_W1, L1_ffn_b1, L1_ffn_W2,
           L1_ffn_b2, L1_ffn_Wout, L1_ffn_bout):
    _ = (L0_attn_wq.shape, L0_attn_wk.shape, L0_attn_wv.shape)
    global _V2_NC
    if _V2_NC is None:
        _V2_NC = build_v2(NTOK)
    nc = _V2_NC
    x = np.ascontiguousarray(L0_attn_x, dtype=np.float32)
    identb = np.eye(128, dtype=np.float32).astype(ml_dtypes.bfloat16)
    bqkv = L1_attn_b_qkv
    wup = np.asarray(L0_ffn_w_up)
    wdn = np.asarray(L0_ffn_w_down)
    wqkv = np.asarray(L1_attn_w_qkv)
    wo = np.asarray(L1_attn_w_out)
    shared = {
        "rwc": np.ascontiguousarray(L0_attn_router_w.reshape(NC, 128).T),
        "rb": np.array([[L0_attn_router_b[0]]], dtype=np.float32),
        "identb": identb,
        "wqkvc": np.ascontiguousarray(
            wqkv.reshape(8, 128, 6, 512).transpose(2, 1, 0, 3)),
        "woc": np.ascontiguousarray(
            wo.reshape(8, 128, 2, 512).transpose(2, 1, 0, 3)),
        "bq": np.ascontiguousarray(bqkv[0:D].reshape(NC, 128).T),
        "bk": np.ascontiguousarray(bqkv[D:2 * D].reshape(NC, 128).T),
        "bvrow": np.ascontiguousarray(bqkv[2 * D:3 * D][None, :]),
        "projt": np.ascontiguousarray(np.concatenate(
            [L1_attn_proj_matrix[h].T for h in range(HEADS)], axis=1)),
        "bo": np.ascontiguousarray(L1_attn_b_out.reshape(NC, 128).T),
        "winc": np.ascontiguousarray(
            np.asarray(L1_ffn_Win).reshape(8, 128, 64).transpose(1, 0, 2)),
        "bin": np.ascontiguousarray(L1_ffn_bin_[:, None]),
        "w1": np.ascontiguousarray(L1_ffn_W1),
        "b1": np.ascontiguousarray(L1_ffn_b1),
        "w2": np.ascontiguousarray(L1_ffn_W2),
        "b2": np.ascontiguousarray(L1_ffn_b2[:, None]),
        "wout": np.ascontiguousarray(L1_ffn_Wout),
        "bout": np.ascontiguousarray(L1_ffn_bout.reshape(NC, 128).T),
    }
    wupc_full = np.ascontiguousarray(
        wup.reshape(32, 128, 8, 128).transpose(0, 3, 2, 1)
        .reshape(32, 128, 1024))
    wdnc_full = np.ascontiguousarray(
        wdn.reshape(8, 128, 32, 128).transpose(0, 3, 2, 1)
        .reshape(8, 128, 4096))
    maps = []
    for core in range(8):
        b, o = core // 4, (core % 4) * NTOK
        m = dict(shared)
        m["xt"] = np.ascontiguousarray(x[b, o:o + NTOK].T)
        m["xrevt"] = np.ascontiguousarray(x[b][::-1][o:o + NTOK].T)
        m["wups"] = np.ascontiguousarray(wupc_full[4 * core:4 * core + 4])
        m["wdns"] = np.ascontiguousarray(wdnc_full[core])
        maps.append(m)
    res = run_bass_kernel_spmd(nc, maps, core_ids=list(range(8)))
    out = np.zeros((2, NSEQ, D), np.float32)
    for core in range(8):
        b, o = core // 4, (core % 4) * NTOK
        out[b, o:o + NTOK] = res.results[core]["yt"].T
    return out
